# version 06
# baseline (speedup 1.0000x reference)
from contextlib import ExitStack

import numpy as np
import ml_dtypes
import concourse.bass as bass
import concourse.mybir as mybir
from concourse.bass_utils import run_bass_kernel_spmd
from concourse.replica_groups import maybe_share_collective_output_space

F32 = mybir.dt.float32
BF16 = mybir.dt.bfloat16
I32 = mybir.dt.int32
U8 = mybir.dt.uint8
OP = mybir.AluOpType
AF = mybir.ActivationFunctionType
AX = mybir.AxisListType

B, T, D = 2, 4096, 2048
TOK = 1024
NTI = 8
INTER = 5504
NKD = 43
H1 = 8192
DK = 128
CH = 64
G4 = [[0, 1, 2, 3], [4, 5, 6, 7]]
G8 = [list(range(8))]
LN_EPS = 1e-5
RMS0_EPS = 1e-5
L2_EPS = 1e-6
BIT_EPS = 1e-8
ATANH_HALF = 0.5493061443340549
RND = 3 * 2.0 ** 22

N_SUB = 6
DEV_DUMP_DECODE = False
DEV_DUMP_F0 = False
DEV_DUMP_F1 = False


class Ctr:
    def __init__(self, nc, name):
        self.h = nc.alloc_semaphore(name)
        self.v = 0

    def inc(self, inst, k=1):
        inst.then_inc(self.h, k)
        self.v += k
        return inst

    def wait(self, eng, v=None):
        eng.wait_ge(self.h, self.v if v is None else v)


def _bf(a):
    return np.ascontiguousarray(np.asarray(a).astype(ml_dtypes.bfloat16))


def _c(a):
    return np.ascontiguousarray(a)


def _host_layout(inp):
    x = np.asarray(inp["L0_attn_x"])
    g_idxT = _c(np.asarray(inp["L0_ffn_gate_idx"]).T)
    u_idxT = _c(np.asarray(inp["L0_ffn_up_idx"]).T)
    d_idxT = _c(np.asarray(inp["L0_ffn_down_idx"]).T)
    lutb = _c(np.broadcast_to(np.asarray(inp["L0_ffn_lut"])[None, :], (128, 16)))

    w0q = np.asarray(inp["L0_attn_Wq"]); w0k = np.asarray(inp["L0_attn_Wk"])
    w0v = np.asarray(inp["L0_attn_Wv"]); w0b = np.asarray(inp["L0_attn_Wb"])
    w0o = np.asarray(inp["L0_attn_Wo"])
    cq = _c(np.broadcast_to(np.asarray(inp["L0_attn_conv_q"])[None, :], (128, 4)))
    ck = _c(np.broadcast_to(np.asarray(inp["L0_attn_conv_k"])[None, :], (128, 4)))
    cv = _c(np.broadcast_to(np.asarray(inp["L0_attn_conv_v"])[None, :], (128, 4)))
    onw = _c(np.broadcast_to(np.asarray(inp["L0_attn_onorm_w"])[None, :], (128, 128)))
    onwcol = _c(np.asarray(inp["L0_attn_onorm_w"], np.float32)[:, None])

    w1q = np.asarray(inp["L1_attn_Wq"]); w1k = np.asarray(inp["L1_attn_Wk"])
    w1v = np.asarray(inp["L1_attn_Wv"]); w1o = np.asarray(inp["L1_attn_Wo"])
    perm = np.concatenate([np.arange(0, 128, 2), np.arange(1, 128, 2)])

    f1T = _c(np.asarray(inp["L1_ffn_w1"]).T)
    f2T = _c(np.asarray(inp["L1_ffn_w2"]).T)
    f3T = _c(np.asarray(inp["L1_ffn_w3"]).T)
    g1b = _c(np.broadcast_to(np.asarray(inp["L1_ffn_g1"])[None, :], (128, D)))
    g2b = _c(np.broadcast_to(np.asarray(inp["L1_ffn_g2"])[None, :], (128, D)))
    g3b = _c(np.broadcast_to(np.asarray(inp["L1_ffn_g3"])[None, :], (128, H1)))

    ident = np.eye(128, dtype=ml_dtypes.bfloat16)
    eye64 = np.eye(64, dtype=ml_dtypes.bfloat16)
    eye64x2 = (2.0 * np.eye(64, dtype=np.float32)).astype(ml_dtypes.bfloat16)
    stril_u8 = _c(np.tril(np.ones((64, 64), np.uint8), -1))
    striu_u8 = _c(np.triu(np.ones((64, 64), np.uint8), 1))
    triu_i = _bf(np.triu(np.ones((64, 64), np.float32)))
    triu_i128 = _bf(np.triu(np.ones((128, 128), np.float32)))
    ones_col = np.ones((128, 1), dtype=ml_dtypes.bfloat16)
    inv_freq = 1.0 / (10000.0 ** (np.arange(0, DK, 2, dtype=np.float64) / DK))
    freqs = np.outer(inv_freq, np.arange(T, dtype=np.float64))
    cosb = _bf(np.concatenate([np.cos(freqs), np.cos(freqs)], axis=0).astype(np.float32))
    sinb = _bf(np.concatenate([np.sin(freqs), np.sin(freqs)], axis=0).astype(np.float32))
    mr = np.zeros((128, 128), np.float32)
    for d in range(64):
        mr[d, d + 64] = -1.0
        mr[d + 64, d] = 1.0
    rope_perm = _bf(mr.T)
    kk = np.arange(128)[:, None]
    qq = np.arange(512)[None, :]
    cmask = np.concatenate(
        [(qq >= off * 128 + kk).astype(np.float32) for off in range(4)],
        axis=1)
    cmask = _bf(cmask)

    in_maps = []
    for c in range(8):
        b, q = c // 4, c % 4
        hc0 = 512 * q
        rows = slice(256 * c, 256 * (c + 1))
        d_rows = slice(688 * c, 688 * (c + 1))
        prow = (np.arange(512 * q, 512 * (q + 1)).reshape(4, 128)[:, perm]).reshape(-1)
        m = {
            "x": _c(x[b, 1024 * q:1024 * (q + 1)]),
            "gidx": _c(g_idxT[rows]), "uidx": _c(u_idxT[rows]),
            "didx": _c(d_idxT[d_rows]), "lutb": lutb,
            "w0q": _c(w0q[:, hc0:hc0 + 512]), "w0k": _c(w0k[:, hc0:hc0 + 512]),
            "w0v": _c(w0v[:, hc0:hc0 + 512]), "w0b": _c(w0b[:, 4 * q:4 * q + 4]),
            "w0o": _c(w0o[hc0:hc0 + 512, :]),
            "cq": cq, "ck": ck, "cv": cv, "onw": onw, "onwc": onwcol,
            "w1q": _c(w1q[prow, :].T), "w1k": _c(w1k[prow, :].T),
            "w1v": _c(w1v[hc0:hc0 + 512, :].T),
            "w1o": _c(w1o.T[hc0:hc0 + 512, :]),
            "f1s": _c(f1T[rows]), "f2s": _c(f2T[rows]),
            "f3s": _c(f3T[1024 * c:1024 * (c + 1)]),
            "g1b": g1b, "g2b": g2b, "g3b": g3b,
            "ident": ident, "eye64": eye64, "eye64x2": eye64x2,
            "stril_u8": stril_u8, "striu_u8": striu_u8,
            "triu_i": triu_i, "triu_i128": triu_i128, "ones_col": ones_col,
            "ones_row": np.ones((1, 128), np.float32),
            "ones_rowb": np.ones((1, 128), ml_dtypes.bfloat16),
            "ones32": np.ones((128, 1), np.float32),
            "cosb": cosb, "sinb": sinb, "cmask": cmask, "rope_perm": rope_perm,
        }
        in_maps.append(m)
    return in_maps


def _decl_inputs(nc):
    t = {}
    def d(name, shape, dt):
        t[name] = nc.dram_tensor(name, list(shape), dt, kind="ExternalInput")
    d("x", (TOK, D), F32)
    d("gidx", (256, INTER), I32); d("uidx", (256, INTER), I32)
    d("didx", (688, D), I32); d("lutb", (128, 16), F32)
    d("w0q", (D, 512), F32); d("w0k", (D, 512), F32); d("w0v", (D, 512), F32)
    d("w0b", (D, 4), F32); d("w0o", (512, D), F32)
    d("cq", (128, 4), F32); d("ck", (128, 4), F32); d("cv", (128, 4), F32)
    d("onw", (128, 128), F32); d("onwc", (128, 1), F32)
    d("w1q", (D, 512), F32); d("w1k", (D, 512), F32); d("w1v", (D, 512), F32)
    d("w1o", (512, D), F32)
    d("f1s", (256, H1), F32); d("f2s", (256, H1), F32); d("f3s", (1024, D), F32)
    d("g1b", (128, D), F32); d("g2b", (128, D), F32); d("g3b", (128, H1), F32)
    d("ident", (128, 128), BF16); d("eye64", (64, 64), BF16)
    d("eye64x2", (64, 64), BF16)
    d("stril_u8", (64, 64), U8); d("striu_u8", (64, 64), U8)
    d("triu_i", (64, 64), BF16); d("triu_i128", (128, 128), BF16)
    d("ones_col", (128, 1), BF16)
    d("ones_row", (1, 128), F32); d("ones32", (128, 1), F32)
    d("cosb", (128, T), BF16); d("sinb", (128, T), BF16)
    d("rope_perm", (128, 128), BF16)
    d("cmask", (128, 4 * 512), BF16); d("ones_rowb", (1, 128), BF16)
    return t


CTR_ALIAS = {
    "dec_ld0": "ln_x0", "dec_ld1": "ln_x1", "dec_done": "ln_rc",
    "dec_st0": "ag_so0", "dec_st1": "ag_so1", "dec_col": "ccol",
    "bp_l0": "ln_x0", "bp_l1": "ln_x1", "bp_rd": "ln_rc", "bp_qz": "ln_sr",
    "bp_s0": "ag_so0", "bp_s1": "ag_so1", "bp_pr": "f0_ln", "bp_cl": "ccol",
    "bp_bc": "f0_tr",
    "f1_ln": "f0_ln", "f1_q": "f0_sl", "f1_qs": "f0_h",
    "f1_h": "f0_sl", "f1_hq": "f0_htr", "f1_hqs": "f0_hte",
    "a1_cl": "ccol", "a0_cl": "ccol", "a0_c0": "a1_c0",
    "a0_cvv": "a1_sc", "a0_cvs": "a1_e1",
    "a0_l2a": "a1_ns", "a0_l2b": "a1_nm", "a0_l2c": "a1_nv",
    "a0_l2d": "a1_s2", "a0_l2e": "a1_mk",
    "sc_t": "a1_av", "sc_v": "a1_dn", "sc_s": "a1_br",
    "wo_w": "ag_so0", "wo_so0": "f0_st0", "wo_so1": "f0_st1",
    "wo_so2": "f0_st2", "wo_so3": "f0_st3",
    "wo_mm": "f0_gu", "wo_ev": "f0_dk", "wo_cl": "ccol",
    "rs_x0": "ln_x0", "rs_x1": "ln_x1",
    "rs_y0": "f0_xr0", "rs_y1": "f0_xr1", "rs_ad": "f0_yev",
    "rs_w0": "ag_so0", "rs_w1": "ag_so1",
    "a1_wl": "wgl", "a1_rp": "f0_te", "a1_ra": "f0_tr",
    "c_dbgf0": "cdbg", "c_dbgf1": "cdbg", "c_dbgq": "cdbg",
    "c_copyout": "cdbg",
}


class K:
    def __init__(self, nc, inp, nsub):
        self.nc = nc
        self.inp = inp
        self.nsub = nsub
        self.sems = {}
        self._uid = 0
        self.pool = [Ctr(nc, f"ctr{i}") for i in range(72)]
        self._pool_i = 0

    def nm(self, base):
        self._uid += 1
        return f"{base}_{self._uid}"

    def ctr(self, name):
        name = CTR_ALIAS.get(name, name)
        if name not in self.sems:
            assert self._pool_i < len(self.pool), "sem pool exhausted"
            self.sems[name] = self.pool[self._pool_i]
            self._pool_i += 1
        return self.sems[name]


def emit_sem_clear(k):
    nc = k.nc
    with nc.Block() as blk:
        @blk.gpsimd
        def _(g):
            for c in k.pool:
                g.sem_clear(c.h)
    nc.all_engine_barrier()


def emit_const_load(k):
    nc, inp = k.nc, k.inp
    k.ident = nc.alloc_sbuf_tensor("sb_ident", [128, 128], BF16)
    k.lut_bf = nc.alloc_sbuf_tensor("sb_lutbf", [128, 16], F32)
    k.ones_col = nc.alloc_sbuf_tensor("sb_ones", [128, 1], BF16)
    k.ones_row = nc.alloc_sbuf_tensor("sb_onesr", [1, 128], F32)
    k.ones32 = nc.alloc_sbuf_tensor("sb_ones32", [128, 1], F32)
    k.g1b32 = nc.alloc_sbuf_tensor("sb_g1b", [128, D], F32)
    k.g2b32 = nc.alloc_sbuf_tensor("sb_g2b", [128, D], F32)
    k.g3bb = nc.alloc_sbuf_tensor("sb_g3b", [128, H1], BF16)
    c = k.ctr("c0")
    with nc.Block() as blk:
        @blk.sync
        def _(sync):
            c.inc(sync.dma_start(out=k.ident[:, :], in_=inp["ident"][:, :]), 16)
            c.inc(sync.dma_start(out=k.ones_col[:, :], in_=inp["ones_col"][:, :]), 16)
            c.inc(sync.dma_start(out=k.ones_row[:, :], in_=inp["ones_row"][:, :]), 16)
            c.inc(sync.dma_start(out=k.ones32[:, :], in_=inp["ones32"][:, :]), 16)
            c.inc(sync.dma_start(out=k.g1b32[:, :], in_=inp["g1b"][:, :]), 16)
            c.inc(sync.dma_start(out=k.g2b32[:, :], in_=inp["g2b"][:, :]), 16)
            c.wait(sync)
        @blk.gpsimd
        def _(g):
            cg = k.ctr("c0g")
            cg.inc(g.dma_start(out=k.lut_bf[:, :], in_=inp["lutb"][:, :]), 16)
            cg.inc(g.dma_start(out=k.g3bb[:, :], in_=inp["g3b"][:, :]), 16)
            cg.wait(g)
            c.wait(g)
    nc.all_engine_barrier()


def emit_ghost_decode(k):
    nc, inp = k.nc, k.inp
    sp8 = maybe_share_collective_output_space("AllGather", G8, prefer_local=True)
    k.wg_hbm = nc.dram_tensor("wg_hbm", [D, INTER], BF16, kind="Internal", addr_space=sp8)
    k.wu_hbm = nc.dram_tensor("wu_hbm", [D, INTER], BF16, kind="Internal", addr_space=sp8)
    k.wd_hbm = nc.dram_tensor("wd_hbm", [INTER, D], BF16, kind="Internal", addr_space=sp8)
    wg_sh = nc.dram_tensor("wg_sh", [256, INTER], BF16, kind="Internal")
    wu_sh = nc.dram_tensor("wu_sh", [256, INTER], BF16, kind="Internal")
    wd_sh = nc.dram_tensor("wd_sh", [688, D], BF16, kind="Internal")

    cl = [k.ctr("dec_ld0"), k.ctr("dec_ld1")]
    cd = k.ctr("dec_done")
    cs = [k.ctr("dec_st0"), k.ctr("dec_st1")]
    tiles = []
    for src, dst, rt, w in (
        (inp["gidx"], wg_sh, [128, 128], INTER),
        (inp["uidx"], wu_sh, [128, 128], INTER),
        (inp["didx"], wd_sh, [128] * 5 + [48], D),
    ):
        r0 = 0
        for nrow in rt:
            tiles.append((src, dst, r0, nrow, w))
            r0 += nrow

    with (
        nc.sbuf_tensor("dec_idx0", [128, INTER], I32) as ib0,
        nc.sbuf_tensor("dec_idx1", [128, INTER], I32) as ib1,
        nc.sbuf_tensor("dec_f", [128, INTER], BF16) as fb,
        nc.sbuf_tensor("dec_tmp", [128, INTER], BF16) as tmp,
        nc.sbuf_tensor("dec_acc0", [128, INTER], BF16) as ac0,
        nc.sbuf_tensor("dec_acc1", [128, INTER], BF16) as ac1,
    ):
        ibufs = [ib0, ib1]; accs = [ac0, ac1]
        b_l = [c.v for c in cl]; b_d = cd.v; b_s = [c.v for c in cs]
        with nc.Block() as blk:
            @blk.sync
            def _(sync):
                for i, (src, dst, r0, nrow, w) in enumerate(tiles):
                    if i >= 2:
                        cd.wait(sync, b_d + i - 1)
                    cl[i % 2].inc(sync.dma_start(out=ibufs[i % 2][0:nrow, 0:w],
                                                 in_=src[r0:r0 + nrow, :]), 16)

            @blk.vector
            def _(v):
                for i, (src, dst, r0, nrow, w) in enumerate(tiles):
                    cl[i % 2].wait(v, b_l[i % 2] + 16 * (i // 2 + 1))
                    if i >= 2:
                        cs[i % 2].wait(v, b_s[i % 2] + 16 * ((i - 2) // 2 + 1))
                    ib = ibufs[i % 2][0:nrow, 0:w]
                    ac = accs[i % 2][0:nrow, 0:w]
                    fbs = fb[0:nrow, 0:w]
                    v.tensor_copy(fbs, ib)
                    v.drain()
                    v.tensor_scalar(ac, fbs, 0.0, k.lut_bf[0:nrow, 0:1],
                                    OP.is_equal, OP.mult)
                    for j in range(1, 15):
                        v.drain()
                        v.tensor_scalar(tmp[0:nrow, 0:w], fbs, float(j),
                                        k.lut_bf[0:nrow, j:j + 1], OP.is_equal, OP.mult)
                        v.drain()
                        v.tensor_tensor(ac, ac, tmp[0:nrow, 0:w], OP.add)
                    v.drain()
                    v.tensor_scalar(tmp[0:nrow, 0:w], fbs, 15.0,
                                    k.lut_bf[0:nrow, 15:16], OP.is_equal, OP.mult)
                    v.drain()
                    cd.inc(v.tensor_tensor(ac, ac, tmp[0:nrow, 0:w], OP.add))

            @blk.gpsimd
            def _(g):
                for i, (src, dst, r0, nrow, w) in enumerate(tiles):
                    cd.wait(g, b_d + i + 1)
                    cs[i % 2].inc(g.dma_start(out=dst[r0:r0 + nrow, :],
                                              in_=accs[i % 2][0:nrow, 0:w]), 16)
                cs[0].wait(g)
                cs[1].wait(g)
                ccol = k.ctr("dec_col")
                ccol.inc(g.collective_compute("AllGather", OP.bypass, replica_groups=G8,
                                              ins=[wg_sh[:, :]], outs=[k.wg_hbm[:, :]]), 1)
                ccol.inc(g.collective_compute("AllGather", OP.bypass, replica_groups=G8,
                                              ins=[wu_sh[:, :]], outs=[k.wu_hbm[:, :]]), 1)
                ccol.inc(g.collective_compute("AllGather", OP.bypass, replica_groups=G8,
                                              ins=[wd_sh[:, :]], outs=[k.wd_hbm[:, :]]), 1)
                ccol.wait(g)
    nc.all_engine_barrier()


def emit_ln_block(k, x_src, tiles, xnb, xnb_stride, cdone):
    nc = k.nc
    cx = [k.ctr("ln_x0"), k.ctr("ln_x1")]
    crc = k.ctr("ln_rc"); csr = k.ctr("ln_sr")
    b_x = [c.v for c in cx]
    b_rc, b_sr, b_done = crc.v, csr.v, cdone.v
    xb = k.ln_xb; sq = k.ln_sq; st = k.ln_st
    n = len(tiles)
    with nc.Block() as blk:
        @blk.sync
        def _(sync):
            for i, gt in enumerate(tiles):
                if i >= 2:
                    cdone.wait(sync, b_done + i - 1)
                cx[i % 2].inc(sync.dma_start(out=xb[:, (i % 2) * D:(i % 2 + 1) * D],
                                             in_=x_src[gt * 128:(gt + 1) * 128, :]), 16)

        @blk.vector
        def _(v):
            for i, gt in enumerate(tiles):
                xt = xb[:, (i % 2) * D:(i % 2 + 1) * D]
                cx[i % 2].wait(v, b_x[i % 2] + 16 * (i // 2 + 1))
                if i > 0:
                    v.drain()
                v.reduce_sum(st[:, 0:1], xt, AX.X)
                v.tensor_tensor(sq[:, :], xt, xt, OP.mult)
                v.drain()
                v.reduce_sum(st[:, 1:2], sq[:, :], AX.X)
                v.drain()
                v.tensor_scalar(st[:, 0:1], st[:, 0:1], 1.0 / D, None, OP.mult)
                v.tensor_scalar(st[:, 1:2], st[:, 1:2], 1.0 / D, None, OP.mult)
                v.drain()
                v.tensor_tensor(st[:, 2:3], st[:, 0:1], st[:, 0:1], OP.mult)
                v.drain()
                v.tensor_tensor(st[:, 2:3], st[:, 1:2], st[:, 2:3], OP.subtract)
                v.drain()
                v.tensor_scalar(st[:, 2:3], st[:, 2:3], LN_EPS, None, OP.add)
                v.drain()
                crc.inc(v.reciprocal(st[:, 3:4], st[:, 2:3]))
                v.drain()
                csr.wait(v, b_sr + i + 1)
                cdone.inc(v.tensor_scalar(xnb[:, i * xnb_stride:i * xnb_stride + D],
                                          xt, st[:, 0:1], st[:, 4:5],
                                          OP.subtract, OP.mult))

        @blk.scalar
        def _(s):
            for i, gt in enumerate(tiles):
                crc.wait(s, b_rc + i + 1)
                csr.inc(s.activation(st[:, 4:5], st[:, 3:4], AF.Sqrt))


def emit_ffn0(k, x_src, x_dst):
    nc = k.nc
    ST = 4
    NSUP = NTI // ST
    OCW = 512
    NOC = (INTER + OCW - 1) // OCW

    cln = k.ctr("f0_ln")
    ctr_ = k.ctr("f0_tr"); cte = k.ctr("f0_te")
    cwg = [k.ctr("f0_wg0"), k.ctr("f0_wg1")]
    cgu = k.ctr("f0_gu")
    csl = k.ctr("f0_sl"); ch_ = k.ctr(k.nm("f0_h"))
    chtr = k.ctr("f0_htr"); chte = k.ctr("f0_hte")
    cwd = [k.ctr(f"f0_wd{i}") for i in range(8)]
    cdk = k.ctr("f0_dk")
    cxr = [k.ctr(f"f0_xr{i}") for i in range(4)]
    cyev = k.ctr("f0_yev")
    cst = [k.ctr(f"f0_st{i}") for i in range(4)]

    with (
        nc.sbuf_tensor(k.nm("f0_xnb"), [128, ST * D], BF16) as xnb,
        nc.sbuf_tensor(k.nm("f0_xnT"), [128, ST * D], BF16) as xnT,
        nc.sbuf_tensor(k.nm("f0_h"), [128, ST * INTER], BF16) as hbuf,
        nc.sbuf_tensor(k.nm("f0_silu"), [128, 2 * OCW], F32) as silu_t,
        nc.sbuf_tensor(k.nm("f0_xr"), [128, 4 * 512], F32) as xrbuf,
        nc.sbuf_tensor(k.nm("f0_y"), [128, 4 * 512], F32) as ybuf,
    ):
        for sup in range(NSUP):
            tiles = list(range(sup * ST, sup * ST + ST))
            with (
                nc.sbuf_tensor(k.nm("f0_lnx"), [128, 2 * D], F32) as ln_xb,
                nc.sbuf_tensor(k.nm("f0_lnsq"), [128, D], F32) as ln_sq,
                nc.sbuf_tensor(k.nm("f0_lnst"), [128, 16], F32) as ln_st,
                nc.psum_tensor(k.nm("f0_pta"), [128, 128], BF16) as pta,
                nc.psum_tensor(k.nm("f0_ptb"), [128, 128], BF16) as ptb,
            ):
                pts = [pta, ptb]
                k.ln_xb, k.ln_sq, k.ln_st = ln_xb, ln_sq, ln_st
                b_ln = cln.v
                emit_ln_block(k, x_src, tiles, xnb, D, cln)
                b_tr, b_te = ctr_.v, cte.v
                with nc.Block() as blk:
                    @blk.tensor
                    def _(te):
                        for i in range(ST):
                            cln.wait(te, b_ln + i + 1)
                            for kc in range(16):
                                n = i * 16 + kc
                                if n >= 2:
                                    cte.wait(te, b_te + n - 1)
                                ctr_.inc(te.transpose(
                                    pts[n % 2][:, :],
                                    xnb[:, i * D + kc * 128:i * D + (kc + 1) * 128],
                                    k.ident[:, :]))

                    @blk.scalar
                    def _(s):
                        for i in range(ST):
                            for kc in range(16):
                                n = i * 16 + kc
                                ctr_.wait(s, b_tr + n + 1)
                                cte.inc(s.copy(
                                    xnT[:, i * D + kc * 128:i * D + (kc + 1) * 128],
                                    pts[n % 2][:, :]))
                nc.all_engine_barrier()

            if DEV_DUMP_F0 and sup == 0:
                cdbg = k.ctr("c_dbgf0")
                with nc.Block() as blk:
                    @blk.sync
                    def _(sync):
                        cdbg.inc(sync.dma_start(out=k.dbg_xnT[:, :], in_=xnT[:, :]), 16)
                        cdbg.inc(sync.dma_start(out=k.dbg_xnb[:, :], in_=xnb[:, :]), 16)
                        cdbg.wait(sync)
                nc.all_engine_barrier()

            with (
                nc.sbuf_tensor(k.nm("f0_wgs"), [128, 2 * 16 * OCW], BF16) as wgs,
                nc.sbuf_tensor(k.nm("f0_wus"), [128, 2 * 16 * OCW], BF16) as wus,
                nc.psum_tensor(k.nm("f0_pg0"), [128, 512], F32) as pg0,
                nc.psum_tensor(k.nm("f0_pg1"), [128, 512], F32) as pg1,
                nc.psum_tensor(k.nm("f0_pu0"), [128, 512], F32) as pu0,
                nc.psum_tensor(k.nm("f0_pu1"), [128, 512], F32) as pu1,
            ):
                pgs = [pg0, pg1]; pus = [pu0, pu1]
                b_gu, b_sl, b_h = cgu.v, csl.v, ch_.v
                b_wg = [c.v for c in cwg]
                with nc.Block() as blk:
                    @blk.sync
                    def _(sync):
                        for oc in range(NOC):
                            w = min(OCW, INTER - oc * OCW)
                            if oc >= 2:
                                cgu.wait(sync, b_gu + (oc - 1) * ST)
                            sl = (oc % 2) * 16 * OCW
                            for hbm, sb in ((k.wg_hbm, wgs), (k.wu_hbm, wus)):
                                for qtr in range(4):
                                    cwg[oc % 2].inc(sync.dma_start(
                                        out=sb[:, sl + qtr * 4 * w: sl + (qtr + 1) * 4 * w]
                                            .rearrange("p (c w) -> p c w", c=4),
                                        in_=hbm[qtr * 512:(qtr + 1) * 512,
                                                oc * OCW:oc * OCW + w]
                                            .rearrange("(c p) w -> p c w", p=128)), 16)

                    @blk.tensor
                    def _(te):
                        for oc in range(NOC):
                            w = min(OCW, INTER - oc * OCW)
                            cwg[oc % 2].wait(te, b_wg[oc % 2] + 16 * 8 * (oc // 2 + 1))
                            sl = (oc % 2) * 16 * OCW
                            for t in range(ST):
                                g = oc * ST + t
                                pgx, pux = pgs[g % 2], pus[g % 2]
                                if g >= 2:
                                    csl.wait(te, b_sl + g - 1)
                                    ch_.wait(te, b_h + g - 1)
                                for kc in range(16):
                                    te.matmul(pgx[:, 0:w],
                                              xnT[:, t * D + kc * 128:t * D + (kc + 1) * 128],
                                              wgs[:, sl + kc * w: sl + (kc + 1) * w],
                                              start=(kc == 0), stop=(kc == 15))
                                for kc in range(16):
                                    inst = te.matmul(pux[:, 0:w],
                                                     xnT[:, t * D + kc * 128:t * D + (kc + 1) * 128],
                                                     wus[:, sl + kc * w: sl + (kc + 1) * w],
                                                     start=(kc == 0), stop=(kc == 15))
                                cgu.inc(inst)

                    @blk.scalar
                    def _(s):
                        for oc in range(NOC):
                            w = min(OCW, INTER - oc * OCW)
                            for t in range(ST):
                                g = oc * ST + t
                                cgu.wait(s, b_gu + g + 1)
                                if g >= 2:
                                    ch_.wait(s, b_h + g - 1)
                                csl.inc(s.activation(
                                    silu_t[:, (g % 2) * OCW:(g % 2) * OCW + w],
                                    pgs[g % 2][:, 0:w], AF.Silu))

                    @blk.vector
                    def _(v):
                        for oc in range(NOC):
                            w = min(OCW, INTER - oc * OCW)
                            for t in range(ST):
                                g = oc * ST + t
                                csl.wait(v, b_sl + g + 1)
                                ch_.inc(v.scalar_tensor_tensor(
                                    hbuf[:, t * INTER + oc * OCW:t * INTER + oc * OCW + w],
                                    pus[g % 2][:, 0:w], 1.0,
                                    silu_t[:, (g % 2) * OCW:(g % 2) * OCW + w],
                                    OP.mult, OP.mult))
                nc.all_engine_barrier()

            if DEV_DUMP_F0 and sup == 0:
                cdbg = k.ctr("c_dbgf0")
                with nc.Block() as blk:
                    @blk.sync
                    def _(sync):
                        cdbg.inc(sync.dma_start(out=k.dbg_h[:, :], in_=hbuf[:, :]), 16)
                        cdbg.wait(sync)
                nc.all_engine_barrier()

            with (
                nc.psum_tensor(k.nm("f0_pt2a"), [128, 128], BF16) as pt2a,
                nc.psum_tensor(k.nm("f0_pt2b"), [128, 128], BF16) as pt2b,
            ):
                pt2s = [pt2a, pt2b]
                b_htr, b_hte = chtr.v, chte.v
                with nc.Block() as blk:
                    @blk.tensor
                    def _(te):
                        for t in range(ST):
                            for kc in range(NKD):
                                n = t * NKD + kc
                                if n >= 2:
                                    chte.wait(te, b_hte + n - 1)
                                chtr.inc(te.transpose(
                                    pt2s[n % 2][:, :],
                                    hbuf[:, t * INTER + kc * 128:t * INTER + (kc + 1) * 128],
                                    k.ident[:, :]))

                    @blk.scalar
                    def _(s):
                        for t in range(ST):
                            for kc in range(NKD):
                                n = t * NKD + kc
                                chtr.wait(s, b_htr + n + 1)
                                chte.inc(s.copy(
                                    hbuf[:, t * INTER + kc * 128:t * INTER + (kc + 1) * 128],
                                    pt2s[n % 2][:, :]))
                nc.all_engine_barrier()

            with (
                nc.sbuf_tensor(k.nm("f0_wds"), [128, 8 * 512], BF16) as wds,
                nc.psum_tensor(k.nm("f0_pd"), [128, 4 * 512], F32) as pd,
            ):
                b_wd = [c.v for c in cwd]; b_dk = cdk.v
                b_xr = [c.v for c in cxr]; b_yev = cyev.v
                b_st = [c.v for c in cst]
                with nc.Block() as blk:
                    @blk.sync
                    def _(sync):
                        for ms in range(4):
                            for kc in range(NKD):
                                n = ms * NKD + kc
                                if n >= 8:
                                    cdk.wait(sync, b_dk + n - 7)
                                cwd[n % 8].inc(sync.dma_start(
                                    out=wds[:, (n % 8) * 512:(n % 8 + 1) * 512],
                                    in_=k.wd_hbm[kc * 128:(kc + 1) * 128,
                                                 ms * 512:(ms + 1) * 512]), 16)

                    @blk.scalar
                    def _(s):
                        for ms in range(4):
                            for t in range(ST):
                                n = ms * ST + t
                                gt = sup * ST + t
                                if n >= 4:
                                    cyev.wait(s, b_yev + n - 3)
                                cxr[n % 4].inc(s.dma_start(
                                    out=xrbuf[:, (n % 4) * 512:(n % 4 + 1) * 512],
                                    in_=x_src[gt * 128:(gt + 1) * 128,
                                              ms * 512:(ms + 1) * 512]), 16)

                    @blk.gpsimd
                    def _(g):
                        for ms in range(4):
                            for t in range(ST):
                                n = ms * ST + t
                                gt = sup * ST + t
                                cyev.wait(g, b_yev + n + 1)
                                cst[n % 4].inc(g.dma_start(
                                    out=x_dst[gt * 128:(gt + 1) * 128,
                                              ms * 512:(ms + 1) * 512],
                                    in_=ybuf[:, (n % 4) * 512:(n % 4 + 1) * 512]), 16)
                        for c4 in cst:
                            c4.wait(g)

                    @blk.tensor
                    def _(te):
                        for ms in range(4):
                            for kc in range(NKD):
                                n = ms * NKD + kc
                                cwd[n % 8].wait(te, b_wd[n % 8] + 16 * (n // 8 + 1))
                                if kc == 0 and ms >= 1:
                                    cyev.wait(te, b_yev + (ms - 1) * ST + ST)
                                for t in range(ST):
                                    inst = te.matmul(
                                        pd[:, t * 512:(t + 1) * 512],
                                        hbuf[:, t * INTER + kc * 128:t * INTER + (kc + 1) * 128],
                                        wds[:, (n % 8) * 512:(n % 8 + 1) * 512],
                                        start=(kc == 0), stop=(kc == NKD - 1))
                                cdk.inc(inst)

                    @blk.vector
                    def _(v):
                        for ms in range(4):
                            for t in range(ST):
                                n = ms * ST + t
                                cdk.wait(v, b_dk + (ms + 1) * NKD)
                                cxr[n % 4].wait(v, b_xr[n % 4] + 16 * (n // 4 + 1))
                                if n >= 4:
                                    cst[n % 4].wait(v, b_st[n % 4] + 16 * ((n - 4) // 4 + 1))
                                cyev.inc(v.scalar_tensor_tensor(
                                    ybuf[:, (n % 4) * 512:(n % 4 + 1) * 512],
                                    pd[:, t * 512:(t + 1) * 512], 0.5,
                                    xrbuf[:, (n % 4) * 512:(n % 4 + 1) * 512],
                                    OP.mult, OP.add))
                nc.all_engine_barrier()
    nc.all_engine_barrier()


def emit_bitnet_prep(k):
    nc, inp = k.nc, k.inp
    sp8 = maybe_share_collective_output_space("AllGather", G8, prefer_local=True)
    spr = maybe_share_collective_output_space("AllReduce", G8, prefer_local=True)
    k.w1q_hbm = nc.dram_tensor("w1q_hbm", [D, H1], BF16, kind="Internal", addr_space=sp8)
    k.w2q_hbm = nc.dram_tensor("w2q_hbm", [D, H1], BF16, kind="Internal", addr_space=sp8)
    k.w3q_hbm = nc.dram_tensor("w3q_hbm", [H1, D], BF16, kind="Internal", addr_space=sp8)
    w1_sh = nc.dram_tensor("w1_sh", [256, H1], BF16, kind="Internal")
    w2_sh = nc.dram_tensor("w2_sh", [256, H1], BF16, kind="Internal")
    w3_sh = nc.dram_tensor("w3_sh", [1024, D], BF16, kind="Internal")
    ar_in = nc.dram_tensor("ar_in", [1, 4], F32, kind="Internal")
    ar_out = nc.dram_tensor("ar_out", [1, 4], F32, kind="Internal", addr_space=spr)

    tiles = []
    for src, dst, nt, w, mi in ((inp["f1s"], w1_sh, 2, H1, 0),
                                (inp["f2s"], w2_sh, 2, H1, 1),
                                (inp["f3s"], w3_sh, 8, D, 2)):
        for r in range(nt):
            tiles.append((src, dst, r, w, mi))

    cwl = [k.ctr("bp_l0"), k.ctr("bp_l1")]
    crd = k.ctr("bp_rd")
    cqz = k.ctr("bp_qz")
    cqs = [k.ctr("bp_s0"), k.ctr("bp_s1")]
    cpr = k.ctr("bp_pr")
    ccl = k.ctr("bp_cl")
    cbc = k.ctr("bp_bc")
    k.qs_dbg = nc.alloc_sbuf_tensor("bp_qs_keep", [128, 16], F32)

    with (
        nc.sbuf_tensor("bp_w0", [128, H1], F32) as wb0,
        nc.sbuf_tensor("bp_w1", [128, H1], F32) as wb1,
        nc.sbuf_tensor("bp_m", [128, H1], BF16) as mbuf,
        nc.sbuf_tensor("bp_q0", [128, H1], BF16) as qb0,
        nc.sbuf_tensor("bp_q1", [128, H1], BF16) as qb1,
        nc.sbuf_tensor("bp_st", [128, 16], F32) as st,
        nc.sbuf_tensor("bp_prow", [1, 16], F32) as prow,
        nc.sbuf_tensor("bp_qs", [128, 16], F32) as qs,
    ):
        wbufs = [wb0, wb1]; qbufs = [qb0, qb1]
        b_l = [c.v for c in cwl]; b_rd = crd.v
        with nc.Block() as blk:
            @blk.sync
            def _(sync):
                for i, (src, dst, r, w, mi) in enumerate(tiles):
                    if i >= 2:
                        crd.wait(sync, b_rd + i - 1)
                    cwl[i % 2].inc(sync.dma_start(
                        out=wbufs[i % 2][:, 0:w],
                        in_=src[r * 128:(r + 1) * 128, :]), 16)

            @blk.vector
            def _(v):
                v.memset(st[:, 0:3], 0.0)
                v.drain()
                for i, (src, dst, r, w, mi) in enumerate(tiles):
                    cwl[i % 2].wait(v, b_l[i % 2] + 16 * (i // 2 + 1))
                    v.reduce_sum(st[:, 4:5], wbufs[i % 2][:, 0:w], AX.X,
                                 apply_absolute_value=True)
                    v.drain()
                    crd.inc(v.tensor_tensor(st[:, mi:mi + 1], st[:, mi:mi + 1],
                                            st[:, 4:5], OP.add))
                    v.drain()
        nc.all_engine_barrier()
        with (nc.psum_tensor("bp_pp", [1, 8], F32) as pp,):
            with nc.Block() as blk:
                @blk.tensor
                def _(te):
                    te.matmul(pp[0:1, 0:3], k.ones32[:, 0:1], st[:, 0:3],
                              start=True, stop=True)
            nc.all_engine_barrier()
            with nc.Block() as blk:
                @blk.vector
                def _(v):
                    v.memset(prow[0:1, 0:4], 0.0)
                    v.drain()
                    v.tensor_copy(prow[0:1, 0:3], pp[0:1, 0:3])
                    v.drain()
            nc.all_engine_barrier()
        with nc.Block() as blk:
            @blk.sync
            def _(sync):
                cpr.inc(sync.dma_start(out=ar_in[0:1, 0:4], in_=prow[0:1, 0:4]), 16)
                cpr.wait(sync)
            @blk.gpsimd
            def _(g):
                cpr.wait(g)
                ccl.inc(g.collective_compute("AllReduce", OP.add, replica_groups=G8,
                                             ins=[ar_in[:, :]], outs=[ar_out[:, :]]), 1)
                ccl.wait(g)
        nc.all_engine_barrier()
        with (nc.psum_tensor("bp_bc", [128, 8], F32) as bc,):
            with nc.Block() as blk:
                @blk.sync
                def _(sync):
                    cbc.inc(sync.dma_start(out=prow[0:1, 4:7], in_=ar_out[0:1, 0:3]), 16)
                    cbc.wait(sync)
            nc.all_engine_barrier()
            with nc.Block() as blk:
                @blk.tensor
                def _(te):
                    te.matmul(bc[:, 0:3], k.ones_row[0:1, :], prow[0:1, 4:7],
                              start=True, stop=True)
            nc.all_engine_barrier()
            with nc.Block() as blk:
                @blk.vector
                def _(v):
                    v.memset(qs[:, :], 0.0)
                    v.drain()
                    v.tensor_scalar(qs[:, 0:3], bc[:, 0:3], 1.0 / (H1 * D), None, OP.mult)
                    v.drain()
                    v.tensor_scalar(qs[:, 3:6], qs[:, 0:3], BIT_EPS, ATANH_HALF,
                                    OP.add, OP.mult)
                    v.drain()
                    v.tensor_scalar(qs[:, 6:9], qs[:, 3:6], -1.0, None, OP.mult)
                    v.drain()
                    v.tensor_scalar(qs[:, 12:15], qs[:, 0:3], -1.0, 1.0, OP.mult, OP.add)
                    v.drain()
                    v.reciprocal(qs[:, 12:15], qs[:, 12:15])
                    v.drain()
                    v.tensor_scalar(st[:, 8:11], qs[:, 0:3], 1.0, None, OP.add)
                    v.drain()
                    crd.inc(v.tensor_tensor(qs[:, 12:15], qs[:, 12:15], st[:, 8:11],
                                            OP.mult))
                @blk.scalar
                def _(s):
                    crd.wait(s)
                    s.activation(qs[:, 9:12], qs[:, 12:15], AF.Ln, scale=1.0)
                    s.drain()
                    cqz.inc(s.mul(qs[:, 9:12], qs[:, 9:12], 0.5))
            nc.all_engine_barrier()
        with nc.Block() as blk:
            @blk.vector
            def _(v):
                v.tensor_copy(k.qs_dbg[:, :], qs[:, :])
                v.drain()
                v.tensor_copy(k.qs_dbg[0:1, 15:16], prow[0:1, 4:5])
        nc.all_engine_barrier()
        b_l2 = [c.v for c in cwl]; b_qz = cqz.v; b_s2 = [c.v for c in cqs]
        with nc.Block() as blk:
            @blk.sync
            def _(sync):
                for i, (src, dst, r, w, mi) in enumerate(tiles):
                    if i >= 2:
                        cqz.wait(sync, b_qz + i - 1)
                    cwl[i % 2].inc(sync.dma_start(
                        out=wbufs[i % 2][:, 0:w],
                        in_=src[r * 128:(r + 1) * 128, :]), 16)

            @blk.vector
            def _(v):
                for i, (src, dst, r, w, mi) in enumerate(tiles):
                    cwl[i % 2].wait(v, b_l2[i % 2] + 16 * (i // 2 + 1))
                    if i >= 2:
                        cqs[i % 2].wait(v, b_s2[i % 2] + 16 * ((i - 2) // 2 + 1))
                    wt = wbufs[i % 2][:, 0:w]
                    qt = qbufs[i % 2][:, 0:w]
                    v.drain()
                    v.tensor_scalar(qt, wt, qs[:, 3 + mi:4 + mi], qs[:, 9 + mi:10 + mi],
                                    OP.is_gt, OP.mult)
                    v.tensor_scalar(mbuf[:, 0:w], wt, qs[:, 6 + mi:7 + mi],
                                    qs[:, 9 + mi:10 + mi], OP.is_lt, OP.mult)
                    v.drain()
                    cqz.inc(v.tensor_tensor(qt, qt, mbuf[:, 0:w], OP.subtract))

            @blk.gpsimd
            def _(g):
                for i, (src, dst, r, w, mi) in enumerate(tiles):
                    cqz.wait(g, b_qz + i + 1)
                    cqs[i % 2].inc(g.dma_start(out=dst[r * 128:(r + 1) * 128, :],
                                               in_=qbufs[i % 2][:, 0:w]), 16)
                for c2 in cqs:
                    c2.wait(g)
                ccl.inc(g.collective_compute("AllGather", OP.bypass, replica_groups=G8,
                                             ins=[w1_sh[:, :]], outs=[k.w1q_hbm[:, :]]), 1)
                ccl.inc(g.collective_compute("AllGather", OP.bypass, replica_groups=G8,
                                             ins=[w2_sh[:, :]], outs=[k.w2q_hbm[:, :]]), 1)
                ccl.inc(g.collective_compute("AllGather", OP.bypass, replica_groups=G8,
                                             ins=[w3_sh[:, :]], outs=[k.w3q_hbm[:, :]]), 1)
                ccl.wait(g)
    nc.all_engine_barrier()


def emit_ffn1(k, x_src, x_dst):
    nc = k.nc
    ST = 2
    NSUP = NTI // ST
    OCW = 512
    NOC1 = H1 // OCW
    NKD1 = H1 // 128

    cln = k.ctr("f1_ln")
    cq = k.ctr("f1_q")
    cqs = k.ctr("f1_qs")
    ctr_ = k.ctr("f0_tr"); cte = k.ctr("f0_te")
    cwg = [k.ctr("f0_wg0"), k.ctr("f0_wg1")]
    cgu = k.ctr("f0_gu")
    ch_ = k.ctr("f1_h")
    chq = k.ctr("f1_hq")
    chqs = k.ctr("f1_hqs")
    chtr = k.ctr("f0_htr"); chte = k.ctr("f0_hte")
    cwd = [k.ctr(f"f0_wd{i}") for i in range(8)]
    cdk = k.ctr("f0_dk")
    cxr = [k.ctr(f"f0_xr{i}") for i in range(4)]
    cyev = k.ctr("f0_yev")
    cst = [k.ctr(f"f0_st{i}") for i in range(4)]

    with (
        nc.sbuf_tensor(k.nm("f1_xnb"), [128, ST * D], BF16) as xnb,
        nc.sbuf_tensor(k.nm("f1_xqT"), [128, 2 * ST * D], BF16) as xqT,
        nc.sbuf_tensor(k.nm("f1_xq"), [128, 2 * D], BF16) as xqb,
        nc.sbuf_tensor(k.nm("f1_h"), [128, ST * H1], BF16) as hbuf,
        nc.sbuf_tensor(k.nm("f1_u"), [128, D], F32) as ubuf,
        nc.sbuf_tensor(k.nm("f1_hsq"), [128, H1], BF16) as hsq,
        nc.sbuf_tensor(k.nm("f1_st"), [128, 16], F32) as qst,
        nc.sbuf_tensor(k.nm("f1_xr"), [128, 4 * 512], F32) as xrbuf,
        nc.sbuf_tensor(k.nm("f1_y"), [128, 4 * 512], F32) as ybuf,
    ):
        for sup in range(NSUP):
            tiles = list(range(sup * ST, sup * ST + ST))
            with (
                nc.sbuf_tensor(k.nm("f1_lnx"), [128, 2 * D], F32) as ln_xb,
                nc.sbuf_tensor(k.nm("f1_lnsq"), [128, D], F32) as ln_sq,
                nc.sbuf_tensor(k.nm("f1_lnst"), [128, 16], F32) as ln_st,
                nc.psum_tensor(k.nm("f1_pta"), [128, 128], BF16) as pta,
                nc.psum_tensor(k.nm("f1_ptb"), [128, 128], BF16) as ptb,
            ):
                pts = [pta, ptb]
                k.ln_xb, k.ln_sq, k.ln_st = ln_xb, ln_sq, ln_st
                b_ln = cln.v
                emit_ln_block(k, x_src, tiles, xnb, D, cln)
                b_q, b_qs, b_tr, b_te = cq.v, cqs.v, ctr_.v, cte.v
                with nc.Block() as blk:
                    @blk.vector
                    def _(v):
                        for i in range(ST):
                            cln.wait(v, b_ln + i + 1)
                            if i >= 1:
                                cte.wait(v, b_te + 32 * i)
                            xnt = xnb[:, i * D:(i + 1) * D]
                            v.drain()
                            v.tensor_tensor(ubuf[:, :], xnt, xnt, OP.mult)
                            v.drain()
                            v.reduce_sum(qst[:, 0:1], ubuf[:, :], AX.X)
                            v.drain()
                            v.tensor_scalar(qst[:, 0:1], qst[:, 0:1], 1.0 / D, BIT_EPS,
                                            OP.mult, OP.add)
                            v.drain()
                            cq.inc(v.reciprocal(qst[:, 1:2], qst[:, 0:1]))
                            cqs.wait(v, b_qs + i + 1)
                            for bidx, gb in ((0, k.g1b32), (1, k.g2b32)):
                                v.drain()
                                v.scalar_tensor_tensor(ubuf[:, :], xnt, qst[:, 2:3],
                                                       gb[:, :], OP.mult, OP.mult)
                                v.drain()
                                v.reduce_max(qst[:, 3:4], ubuf[:, :], AX.X,
                                             apply_absolute_value=True)
                                v.drain()
                                v.tensor_scalar(qst[:, 3:4], qst[:, 3:4], 1e-4, None,
                                                OP.max)
                                v.drain()
                                v.reciprocal(qst[:, 4:5], qst[:, 3:4])
                                v.drain()
                                v.tensor_scalar(qst[:, 4:5], qst[:, 4:5], 127.0, None,
                                                OP.mult)
                                v.tensor_scalar(qst[:, 5:6], qst[:, 3:4], 1.0 / 127.0,
                                                None, OP.mult)
                                v.drain()
                                v.tensor_scalar(ubuf[:, :], ubuf[:, :], qst[:, 4:5],
                                                RND, OP.mult, OP.add)
                                v.drain()
                                cq.inc(v.tensor_scalar(
                                    xqb[:, bidx * D:(bidx + 1) * D], ubuf[:, :],
                                    RND, qst[:, 5:6], OP.subtract, OP.mult))

                    @blk.scalar
                    def _(s):
                        for i in range(ST):
                            cq.wait(s, b_q + 3 * i + 1)
                            s.activation(qst[:, 2:3], qst[:, 1:2], AF.Sqrt)
                            s.drain()
                            cqs.inc(s.copy(qst[:, 6:7], qst[:, 2:3]))
                            for n2 in range(32):
                                n = i * 32 + n2
                                ctr_.wait(s, b_tr + n + 1)
                                b2, kc = n2 // 16, n2 % 16
                                cte.inc(s.copy(
                                    xqT[:, b2 * ST * D + i * D + kc * 128:
                                        b2 * ST * D + i * D + (kc + 1) * 128],
                                    pts[n % 2][:, :]))

                    @blk.tensor
                    def _(te):
                        for i in range(ST):
                            cq.wait(te, b_q + 3 * (i + 1))
                            for n2 in range(32):
                                n = i * 32 + n2
                                b2, kc = n2 // 16, n2 % 16
                                if n >= 2:
                                    cte.wait(te, b_te + n - 1)
                                ctr_.inc(te.transpose(
                                    pts[n % 2][:, :],
                                    xqb[:, b2 * D + kc * 128:b2 * D + (kc + 1) * 128],
                                    k.ident[:, :]))
                nc.all_engine_barrier()

            if DEV_DUMP_F1 and sup == 0:
                cdbg = k.ctr("c_dbgf1")
                with nc.Block() as blk:
                    @blk.sync
                    def _(sync):
                        cdbg.inc(sync.dma_start(out=k.dbg_xqT[:, :], in_=xqT[:, :]), 16)
                        cdbg.wait(sync)
                nc.all_engine_barrier()

            for wstream in range(2):
                whbm = k.w1q_hbm if wstream == 0 else k.w2q_hbm
                with (
                    nc.sbuf_tensor(k.nm("f1_ws"), [128, 2 * 16 * OCW], BF16) as wss,
                    nc.psum_tensor(k.nm("f1_pg0"), [128, 512], F32) as pg0,
                    nc.psum_tensor(k.nm("f1_pg1"), [128, 512], F32) as pg1,
                ):
                    pgs = [pg0, pg1]
                    b_gu = cgu.v; b_h = ch_.v
                    b_wg = [c.v for c in cwg]
                    with nc.Block() as blk:
                        @blk.sync
                        def _(sync):
                            for oc in range(NOC1):
                                if oc >= 2:
                                    cgu.wait(sync, b_gu + (oc - 1) * ST)
                                sl = (oc % 2) * 16 * OCW
                                for qtr in range(4):
                                    cwg[oc % 2].inc(sync.dma_start(
                                        out=wss[:, sl + qtr * 4 * OCW: sl + (qtr + 1) * 4 * OCW]
                                            .rearrange("p (c w) -> p c w", c=4),
                                        in_=whbm[qtr * 512:(qtr + 1) * 512,
                                                 oc * OCW:(oc + 1) * OCW]
                                            .rearrange("(c p) w -> p c w", p=128)), 16)

                        @blk.tensor
                        def _(te):
                            for oc in range(NOC1):
                                cwg[oc % 2].wait(te, b_wg[oc % 2] + 64 * (oc // 2 + 1))
                                sl = (oc % 2) * 16 * OCW
                                for t in range(ST):
                                    g = oc * ST + t
                                    if g >= 2:
                                        ch_.wait(te, b_h + g - 1)
                                    for kc in range(16):
                                        inst = te.matmul(
                                            pgs[g % 2][:, :],
                                            xqT[:, wstream * ST * D + t * D + kc * 128:
                                                wstream * ST * D + t * D + (kc + 1) * 128],
                                            wss[:, sl + kc * OCW: sl + (kc + 1) * OCW],
                                            start=(kc == 0), stop=(kc == 15))
                                    cgu.inc(inst)

                        @blk.scalar
                        def _(s):
                            if wstream != 0:
                                return
                            for oc in range(NOC1):
                                for t in range(ST):
                                    g = oc * ST + t
                                    cgu.wait(s, b_gu + g + 1)
                                    ch_.inc(s.activation(
                                        hbuf[:, t * H1 + oc * OCW:t * H1 + (oc + 1) * OCW],
                                        pgs[g % 2][:, :], AF.Silu))

                        @blk.vector
                        def _(v):
                            if wstream != 1:
                                return
                            for oc in range(NOC1):
                                for t in range(ST):
                                    g = oc * ST + t
                                    cgu.wait(v, b_gu + g + 1)
                                    hs = hbuf[:, t * H1 + oc * OCW:t * H1 + (oc + 1) * OCW]
                                    v.drain()
                                    ch_.inc(v.tensor_tensor(hs, hs, pgs[g % 2][:, :],
                                                            OP.mult))
                    nc.all_engine_barrier()

            if DEV_DUMP_F1 and sup == 0:
                cdbg = k.ctr("c_dbgf1")
                with nc.Block() as blk:
                    @blk.sync
                    def _(sync):
                        cdbg.inc(sync.dma_start(out=k.dbg_h[:, :], in_=hbuf[:, :]), 16)
                        cdbg.wait(sync)
                nc.all_engine_barrier()

            b_hq, b_hqs = chq.v, chqs.v
            with nc.Block() as blk:
                @blk.vector
                def _(v):
                    for t in range(ST):
                        ht = hbuf[:, t * H1:(t + 1) * H1]
                        v.drain()
                        v.tensor_tensor(hsq[:, :], ht, ht, OP.mult)
                        v.drain()
                        v.reduce_sum(qst[:, 0:1], hsq[:, :], AX.X)
                        v.drain()
                        v.tensor_scalar(qst[:, 0:1], qst[:, 0:1], 1.0 / H1, BIT_EPS,
                                        OP.mult, OP.add)
                        v.drain()
                        chq.inc(v.reciprocal(qst[:, 1:2], qst[:, 0:1]))
                        chqs.wait(v, b_hqs + t + 1)
                        v.drain()
                        v.scalar_tensor_tensor(ht, ht, qst[:, 2:3], k.g3bb[:, :],
                                               OP.mult, OP.mult)
                        v.drain()
                        v.reduce_max(qst[:, 3:4], ht, AX.X, apply_absolute_value=True)
                        v.drain()
                        v.tensor_scalar(qst[:, 3:4], qst[:, 3:4], 1e-4, None, OP.max)
                        v.drain()
                        v.reciprocal(qst[:, 4:5], qst[:, 3:4])
                        v.drain()
                        v.tensor_scalar(qst[:, 4:5], qst[:, 4:5], 127.0, None, OP.mult)
                        v.tensor_scalar(qst[:, 5:6], qst[:, 3:4], 1.0 / 127.0, None,
                                        OP.mult)
                        for ch2 in range(4):
                            hs2 = hbuf[:, t * H1 + ch2 * D:t * H1 + (ch2 + 1) * D]
                            v.drain()
                            v.tensor_scalar(ubuf[:, :], hs2, qst[:, 4:5], RND,
                                            OP.mult, OP.add)
                            v.drain()
                            inst = v.tensor_scalar(hs2, ubuf[:, :], RND, qst[:, 5:6],
                                                   OP.subtract, OP.mult)
                        chq.inc(inst)

                @blk.scalar
                def _(s):
                    for t in range(ST):
                        chq.wait(s, b_hq + 2 * t + 1)
                        s.activation(qst[:, 2:3], qst[:, 1:2], AF.Sqrt)
                        s.drain()
                        chqs.inc(s.copy(qst[:, 7:8], qst[:, 2:3]))
            nc.all_engine_barrier()

            if DEV_DUMP_F1 and sup == 0:
                cdbg = k.ctr("c_dbgf1")
                with nc.Block() as blk:
                    @blk.sync
                    def _(sync):
                        cdbg.inc(sync.dma_start(out=k.dbg_hq[:, :], in_=hbuf[:, :]), 16)
                        cdbg.wait(sync)
                nc.all_engine_barrier()

            with (
                nc.psum_tensor(k.nm("f1_pt2a"), [128, 128], BF16) as pt2a,
                nc.psum_tensor(k.nm("f1_pt2b"), [128, 128], BF16) as pt2b,
            ):
                pt2s = [pt2a, pt2b]
                b_htr, b_hte = chtr.v, chte.v
                with nc.Block() as blk:
                    @blk.tensor
                    def _(te):
                        for t in range(ST):
                            for kc in range(NKD1):
                                n = t * NKD1 + kc
                                if n >= 2:
                                    chte.wait(te, b_hte + n - 1)
                                chtr.inc(te.transpose(
                                    pt2s[n % 2][:, :],
                                    hbuf[:, t * H1 + kc * 128:t * H1 + (kc + 1) * 128],
                                    k.ident[:, :]))

                    @blk.scalar
                    def _(s):
                        for t in range(ST):
                            for kc in range(NKD1):
                                n = t * NKD1 + kc
                                chtr.wait(s, b_htr + n + 1)
                                chte.inc(s.copy(
                                    hbuf[:, t * H1 + kc * 128:t * H1 + (kc + 1) * 128],
                                    pt2s[n % 2][:, :]))
                nc.all_engine_barrier()

            with (
                nc.sbuf_tensor(k.nm("f1_wds"), [128, 8 * 512], BF16) as wds,
                nc.psum_tensor(k.nm("f1_pd"), [128, ST * 512], F32) as pd,
            ):
                b_wd = [c.v for c in cwd]; b_dk = cdk.v
                b_xr = [c.v for c in cxr]; b_yev = cyev.v
                b_st = [c.v for c in cst]
                with nc.Block() as blk:
                    @blk.sync
                    def _(sync):
                        for ms in range(4):
                            for kc in range(NKD1):
                                n = ms * NKD1 + kc
                                if n >= 8:
                                    cdk.wait(sync, b_dk + n - 7)
                                cwd[n % 8].inc(sync.dma_start(
                                    out=wds[:, (n % 8) * 512:(n % 8 + 1) * 512],
                                    in_=k.w3q_hbm[kc * 128:(kc + 1) * 128,
                                                  ms * 512:(ms + 1) * 512]), 16)

                    @blk.scalar
                    def _(s):
                        for ms in range(4):
                            for t in range(ST):
                                n = ms * ST + t
                                gt = sup * ST + t
                                if n >= 4:
                                    cyev.wait(s, b_yev + n - 3)
                                cxr[n % 4].inc(s.dma_start(
                                    out=xrbuf[:, (n % 4) * 512:(n % 4 + 1) * 512],
                                    in_=x_src[gt * 128:(gt + 1) * 128,
                                              ms * 512:(ms + 1) * 512]), 16)

                    @blk.gpsimd
                    def _(g):
                        for ms in range(4):
                            for t in range(ST):
                                n = ms * ST + t
                                gt = sup * ST + t
                                cyev.wait(g, b_yev + n + 1)
                                cst[n % 4].inc(g.dma_start(
                                    out=x_dst[gt * 128:(gt + 1) * 128,
                                              ms * 512:(ms + 1) * 512],
                                    in_=ybuf[:, (n % 4) * 512:(n % 4 + 1) * 512]), 16)
                        for c4 in cst:
                            c4.wait(g)

                    @blk.tensor
                    def _(te):
                        for ms in range(4):
                            for kc in range(NKD1):
                                n = ms * NKD1 + kc
                                cwd[n % 8].wait(te, b_wd[n % 8] + 16 * (n // 8 + 1))
                                if kc == 0 and ms >= 1:
                                    cyev.wait(te, b_yev + (ms - 1) * ST + ST)
                                for t in range(ST):
                                    inst = te.matmul(
                                        pd[:, t * 512:(t + 1) * 512],
                                        hbuf[:, t * H1 + kc * 128:t * H1 + (kc + 1) * 128],
                                        wds[:, (n % 8) * 512:(n % 8 + 1) * 512],
                                        start=(kc == 0), stop=(kc == NKD1 - 1))
                                cdk.inc(inst)

                    @blk.vector
                    def _(v):
                        for ms in range(4):
                            for t in range(ST):
                                n = ms * ST + t
                                cdk.wait(v, b_dk + (ms + 1) * NKD1)
                                cxr[n % 4].wait(v, b_xr[n % 4] + 16 * (n // 4 + 1))
                                if n >= 4:
                                    cst[n % 4].wait(v, b_st[n % 4] + 16 * ((n - 4) // 4 + 1))
                                v.drain()
                                cyev.inc(v.scalar_tensor_tensor(
                                    ybuf[:, (n % 4) * 512:(n % 4 + 1) * 512],
                                    pd[:, t * 512:(t + 1) * 512], 0.5,
                                    xrbuf[:, (n % 4) * 512:(n % 4 + 1) * 512],
                                    OP.mult, OP.add))
                nc.all_engine_barrier()
    nc.all_engine_barrier()


def emit_ln_to_hbm(k, x_src, dst_hbm):
    nc = k.nc
    cln = k.ctr("ag_ln")
    cso = [k.ctr("ag_so0"), k.ctr("ag_so1")]
    cx = [k.ctr("ln_x0"), k.ctr("ln_x1")]
    crc = k.ctr("ln_rc"); csr = k.ctr("ln_sr")
    with (
        nc.sbuf_tensor(k.nm("ag_lnx"), [128, 2 * D], F32) as xb,
        nc.sbuf_tensor(k.nm("ag_lnsq"), [128, D], F32) as sq,
        nc.sbuf_tensor(k.nm("ag_lnst"), [128, 16], F32) as st,
        nc.sbuf_tensor(k.nm("ag_xnb"), [128, 2 * D], BF16) as xnb,
    ):
        b_x = [c.v for c in cx]
        b_rc, b_sr, b_ln = crc.v, csr.v, cln.v
        b_so = [c.v for c in cso]
        tiles = list(range(NTI))
        with nc.Block() as blk:
            @blk.sync
            def _(sync):
                for i, gt in enumerate(tiles):
                    if i >= 2:
                        cln.wait(sync, b_ln + i - 1)
                    cx[i % 2].inc(sync.dma_start(
                        out=xb[:, (i % 2) * D:(i % 2 + 1) * D],
                        in_=x_src[gt * 128:(gt + 1) * 128, :]), 16)

            @blk.vector
            def _(v):
                for i, gt in enumerate(tiles):
                    xt = xb[:, (i % 2) * D:(i % 2 + 1) * D]
                    cx[i % 2].wait(v, b_x[i % 2] + 16 * (i // 2 + 1))
                    if i > 0:
                        v.drain()
                    if i >= 2:
                        cso[i % 2].wait(v, b_so[i % 2] + 16 * ((i - 2) // 2 + 1))
                    v.reduce_sum(st[:, 0:1], xt, AX.X)
                    v.tensor_tensor(sq[:, :], xt, xt, OP.mult)
                    v.drain()
                    v.reduce_sum(st[:, 1:2], sq[:, :], AX.X)
                    v.drain()
                    v.tensor_scalar(st[:, 0:1], st[:, 0:1], 1.0 / D, None, OP.mult)
                    v.tensor_scalar(st[:, 1:2], st[:, 1:2], 1.0 / D, None, OP.mult)
                    v.drain()
                    v.tensor_tensor(st[:, 2:3], st[:, 0:1], st[:, 0:1], OP.mult)
                    v.drain()
                    v.tensor_tensor(st[:, 2:3], st[:, 1:2], st[:, 2:3], OP.subtract)
                    v.drain()
                    v.tensor_scalar(st[:, 2:3], st[:, 2:3], LN_EPS, None, OP.add)
                    v.drain()
                    crc.inc(v.reciprocal(st[:, 3:4], st[:, 2:3]))
                    v.drain()
                    csr.wait(v, b_sr + i + 1)
                    cln.inc(v.tensor_scalar(xnb[:, (i % 2) * D:(i % 2 + 1) * D],
                                            xt, st[:, 0:1], st[:, 4:5],
                                            OP.subtract, OP.mult))

            @blk.scalar
            def _(s):
                for i, gt in enumerate(tiles):
                    crc.wait(s, b_rc + i + 1)
                    csr.inc(s.activation(st[:, 4:5], st[:, 3:4], AF.Sqrt))

            @blk.gpsimd
            def _(g):
                for i, gt in enumerate(tiles):
                    cln.wait(g, b_ln + i + 1)
                    cso[i % 2].inc(g.dma_start(
                        out=dst_hbm[gt * 128:(gt + 1) * 128, :],
                        in_=xnb[:, (i % 2) * D:(i % 2 + 1) * D]), 16)
                for c2 in cso:
                    c2.wait(g)
    nc.all_engine_barrier()


def emit_wo_rs_residual(k, o_all, wo_inp, x_src, x_dst, tag):
    nc = k.nc
    yp_hbm = nc.dram_tensor(k.nm(f"{tag}_yp"), [T, D], BF16, kind="Internal")
    sprs = maybe_share_collective_output_space("ReduceScatter", G4, prefer_local=True)
    yrs_hbm = nc.dram_tensor(k.nm(f"{tag}_yrs"), [TOK, D], BF16, kind="Internal",
                             addr_space=sprs)
    cwo = k.ctr("wo_w")
    cmm = k.ctr("wo_mm")
    cev = k.ctr("wo_ev")
    cso = [k.ctr(f"wo_so{i}") for i in range(4)]
    ccl = k.ctr("wo_cl")
    with (
        nc.sbuf_tensor(k.nm("wo_w"), [128, 4 * D], BF16) as wos,
        nc.sbuf_tensor(k.nm("wo_y"), [128, 4 * 512], BF16) as ybuf,
        nc.psum_tensor(k.nm("wo_p0"), [128, 512], F32) as pw0,
        nc.psum_tensor(k.nm("wo_p1"), [128, 512], F32) as pw1,
    ):
        pws = [pw0, pw1]
        b_wo, b_mm, b_ev = cwo.v, cmm.v, cev.v
        b_so = [c.v for c in cso]
        with nc.Block() as blk:
            @blk.gpsimd
            def _(g):
                cwo.inc(g.dma_start(
                    out=wos[:, :].rearrange("p (c m) -> p c m", c=4),
                    in_=wo_inp[:, :].rearrange("(c p) m -> p c m", p=128)), 16)
                cwo.wait(g)
                for tt in range(32):
                    for ms in range(4):
                        n = tt * 4 + ms
                        cev.wait(g, b_ev + n + 1)
                        cso[n % 4].inc(g.dma_start(
                            out=yp_hbm[tt * 128:(tt + 1) * 128,
                                       ms * 512:(ms + 1) * 512],
                            in_=ybuf[:, (n % 4) * 512:(n % 4 + 1) * 512]), 16)
                for c2 in cso:
                    c2.wait(g)
                ccl.inc(g.collective_compute(
                    "ReduceScatter", OP.add, replica_groups=G4,
                    ins=[yp_hbm[:, :]], outs=[yrs_hbm[:, :]]), 1)
                ccl.wait(g)

            @blk.tensor
            def _(te):
                cwo.wait(te, b_wo + 16)
                for tt in range(32):
                    for ms in range(4):
                        n = tt * 4 + ms
                        if n >= 2:
                            cev.wait(te, b_ev + n - 1)
                        for h in range(4):
                            inst = te.matmul(
                                pws[n % 2][:, :],
                                o_all[:, h * T + tt * 128:h * T + (tt + 1) * 128],
                                wos[:, h * D + ms * 512:h * D + (ms + 1) * 512],
                                start=(h == 0), stop=(h == 3))
                        cmm.inc(inst)

            @blk.vector
            def _(v):
                for tt in range(32):
                    for ms in range(4):
                        n = tt * 4 + ms
                        cmm.wait(v, b_mm + n + 1)
                        if n >= 4:
                            cso[n % 4].wait(v, b_so[n % 4] + 16 * ((n - 4) // 4 + 1))
                        v.drain()
                        cev.inc(v.tensor_copy(
                            ybuf[:, (n % 4) * 512:(n % 4 + 1) * 512],
                            pws[n % 2][:, :]))
        nc.all_engine_barrier()

    cxl = [k.ctr("rs_x0"), k.ctr("rs_x1")]
    cyl = [k.ctr("rs_y0"), k.ctr("rs_y1")]
    cad = k.ctr("rs_ad")
    cwr = [k.ctr("rs_w0"), k.ctr("rs_w1")]
    with (
        nc.sbuf_tensor(k.nm("rs_x"), [128, 2 * D], F32) as xrb,
        nc.sbuf_tensor(k.nm("rs_yb"), [128, 2 * D], BF16) as yrb,
        nc.sbuf_tensor(k.nm("rs_o"), [128, 2 * D], F32) as orb,
    ):
        b_xl = [c.v for c in cxl]; b_yl = [c.v for c in cyl]
        b_ad = cad.v; b_wr = [c.v for c in cwr]
        with nc.Block() as blk:
            @blk.sync
            def _(sync):
                for i in range(NTI):
                    if i >= 2:
                        cad.wait(sync, b_ad + i - 1)
                    cxl[i % 2].inc(sync.dma_start(
                        out=xrb[:, (i % 2) * D:(i % 2 + 1) * D],
                        in_=x_src[i * 128:(i + 1) * 128, :]), 16)

            @blk.scalar
            def _(s):
                for i in range(NTI):
                    if i >= 2:
                        cad.wait(s, b_ad + i - 1)
                    cyl[i % 2].inc(s.dma_start(
                        out=yrb[:, (i % 2) * D:(i % 2 + 1) * D],
                        in_=yrs_hbm[i * 128:(i + 1) * 128, :]), 16)

            @blk.vector
            def _(v):
                for i in range(NTI):
                    cxl[i % 2].wait(v, b_xl[i % 2] + 16 * (i // 2 + 1))
                    cyl[i % 2].wait(v, b_yl[i % 2] + 16 * (i // 2 + 1))
                    if i >= 2:
                        cwr[i % 2].wait(v, b_wr[i % 2] + 16 * ((i - 2) // 2 + 1))
                    v.drain()
                    cad.inc(v.tensor_tensor(orb[:, (i % 2) * D:(i % 2 + 1) * D],
                                            xrb[:, (i % 2) * D:(i % 2 + 1) * D],
                                            yrb[:, (i % 2) * D:(i % 2 + 1) * D],
                                            OP.add))

            @blk.gpsimd
            def _(g):
                for i in range(NTI):
                    cad.wait(g, b_ad + i + 1)
                    cwr[i % 2].inc(g.dma_start(
                        out=x_dst[i * 128:(i + 1) * 128, :],
                        in_=orb[:, (i % 2) * D:(i % 2 + 1) * D]), 16)
                for c2 in cwr:
                    c2.wait(g)
        nc.all_engine_barrier()


def emit_attn1(k, x_src, x_dst):
    nc, inp = k.nc, k.inp
    sp4 = maybe_share_collective_output_space("AllGather", G4, prefer_local=True)
    xn_sh = nc.dram_tensor(k.nm("a1_xnsh"), [TOK, D], BF16, kind="Internal")
    xn_full = nc.dram_tensor(k.nm("a1_xnf"), [T, D], BF16, kind="Internal",
                             addr_space=sp4)
    emit_ln_to_hbm(k, x_src, xn_sh)
    ccl = k.ctr("a1_cl")
    with nc.Block() as blk:
        @blk.gpsimd
        def _(g):
            ccl.inc(g.collective_compute("AllGather", OP.bypass, replica_groups=G4,
                                         ins=[xn_sh[:, :]], outs=[xn_full[:, :]]), 1)
            ccl.wait(g)
    nc.all_engine_barrier()

    NTC = 8
    with (
        nc.sbuf_tensor(k.nm("a1_o"), [128, 4 * T], BF16) as o_all,
        nc.sbuf_tensor(k.nm("a1_cos"), [128, T], BF16) as cosb,
        nc.sbuf_tensor(k.nm("a1_sin"), [128, T], BF16) as sinb,
        nc.sbuf_tensor(k.nm("a1_rpm"), [128, 128], BF16) as rpermb,
        nc.sbuf_tensor(k.nm("a1_msk"), [128, 4 * 512], BF16) as cmask,
        nc.sbuf_tensor(k.nm("a1_st"), [128, 16], F32) as ast,
        nc.sbuf_tensor(k.nm("a1_sq"), [128, 2 * 512], F32) as sqb,
        nc.sbuf_tensor(k.nm("a1_nr"), [1, 2 * 512], F32) as nrow,
    ):
        cc0 = k.ctr("a1_c0")
        b_c0 = cc0.v
        with nc.Block() as blk:
            @blk.sync
            def _(sync):
                cc0.inc(sync.dma_start(out=cosb[:, :], in_=inp["cosb"][:, :]), 16)
                cc0.inc(sync.dma_start(out=sinb[:, :], in_=inp["sinb"][:, :]), 16)
                cc0.inc(sync.dma_start(out=cmask[:, :], in_=inp["cmask"][:, :]), 16)
                cc0.inc(sync.dma_start(out=rpermb[:, :], in_=inp["rope_perm"][:, :]), 16)
                cc0.wait(sync)
        nc.all_engine_barrier()

        for hp in range(2):
            with (
                nc.sbuf_tensor(k.nm("a1_q"), [128, 2 * T], BF16) as qb,
                nc.sbuf_tensor(k.nm("a1_k"), [128, 2 * T], BF16) as kb,
                nc.sbuf_tensor(k.nm("a1_v"), [128, 2 * T], BF16) as vb,
                nc.sbuf_tensor(k.nm("a1_rp"), [128, T], BF16) as rpb,
            ):
                cxt = [k.ctr("a1_xt0"), k.ctr("a1_xt1")]
                cwl = k.ctr("a1_wl")
                cpj = k.ctr("a1_pj")
                cpe = k.ctr("a1_pe")
                with (
                    nc.sbuf_tensor(k.nm("a1_w"), [128, 3 * 4096], BF16) as wqkv,
                    nc.sbuf_tensor(k.nm("a1_xnT"), [128, 16 * 512], BF16) as xnT,
                    nc.psum_tensor(k.nm("a1_pp0"), [128, 512], F32) as pp0,
                    nc.psum_tensor(k.nm("a1_pp1"), [128, 512], F32) as pp1,
                ):
                    pps = [pp0, pp1]
                    b_wl, b_pj, b_pe = cwl.v, cpj.v, cpe.v
                    b_xt = [c.v for c in cxt]
                    with nc.Block() as blk:
                        @blk.gpsimd
                        def _(g):
                            for wi, wname in enumerate(("w1q", "w1k", "w1v")):
                                cwl.inc(g.dma_start(
                                    out=wqkv[:, wi * 4096:(wi + 1) * 4096]
                                        .rearrange("p (c w) -> p c w", c=16),
                                    in_=inp[wname][:, hp * 256:(hp + 1) * 256]
                                        .rearrange("(c p) w -> p c w", p=128)), 16)
                            cwl.wait(g)

                        @blk.sync
                        def _(sync):
                            for tc in range(NTC):
                                if tc >= 1:
                                    cpj.wait(sync, b_pj + 6 * tc)
                                for kc in range(16):
                                    cxt[tc % 2].inc(sync.dma_start_transpose(
                                        xnT[:, kc * 512:(kc + 1) * 512],
                                        xn_full[tc * 512:(tc + 1) * 512,
                                                kc * 128:(kc + 1) * 128]), 16)

                        @blk.tensor
                        def _(te):
                            cwl.wait(te, b_wl + 48)
                            for tc in range(NTC):
                                cxt[tc % 2].wait(te, b_xt[tc % 2] + 256 * (tc // 2 + 1))
                                for ti in range(6):
                                    wi, h2 = ti % 3, ti // 3
                                    g = tc * 6 + ti
                                    if g >= 2:
                                        cpe.wait(te, b_pe + g - 1)
                                    for kc in range(16):
                                        inst = te.matmul(
                                            pps[g % 2][:, :],
                                            wqkv[:, wi * 4096 + kc * 256 + h2 * 128:
                                                 wi * 4096 + kc * 256 + h2 * 128 + 128],
                                            xnT[:, kc * 512:(kc + 1) * 512],
                                            start=(kc == 0), stop=(kc == 15))
                                    cpj.inc(inst)

                        @blk.scalar
                        def _(s):
                            for tc in range(NTC):
                                for ti in range(6):
                                    wi, h2 = ti % 3, ti // 3
                                    g = tc * 6 + ti
                                    cpj.wait(s, b_pj + g + 1)
                                    dst = (qb, kb, vb)[wi]
                                    cpe.inc(s.copy(
                                        dst[:, h2 * T + tc * 512:h2 * T + (tc + 1) * 512],
                                        pps[g % 2][:, :]))
                    nc.all_engine_barrier()

                crp = k.ctr("a1_rp")
                crm = k.ctr("a1_nm")
                cra = k.ctr("a1_ra")
                b_rp, b_rm, b_ra = crp.v, crm.v, cra.v
                with (
                    nc.sbuf_tensor(k.nm("a1_rp2"), [128, T], BF16) as rp2,
                    nc.psum_tensor(k.nm("a1_prp"), [128, 512], F32) as prp,
                    nc.psum_tensor(k.nm("a1_prq"), [128, 512], F32) as prq,
                ):
                    prps = [prp, prq]
                    ir = [0]
                    for h2 in range(2):
                        for ti, tens in enumerate((qb, kb)):
                            t_idx = h2 * 2 + ti
                            tt_ = tens[:, h2 * T:(h2 + 1) * T]
                            nc.vector.drain()
                            nc.vector.tensor_tensor(rpb[:, :], tt_, cosb[:, :],
                                                    OP.mult)
                            nc.vector.drain()
                            crp.inc(nc.vector.tensor_tensor(rp2[:, :], tt_,
                                                            sinb[:, :], OP.mult))
                            for ch in range(8):
                                n = ir[0]
                                crp.wait(nc.tensor, b_rp + t_idx + 1)
                                if n >= 2:
                                    cra.wait(nc.tensor, b_ra + n - 1)
                                crm.inc(nc.tensor.matmul(
                                    prps[n % 2][:, :], rpermb[:, :],
                                    rp2[:, ch * 512:(ch + 1) * 512],
                                    start=True, stop=True))
                                crm.wait(nc.vector, b_rm + n + 1)
                                nc.vector.drain()
                                cra.inc(nc.vector.tensor_tensor(
                                    tt_[:, ch * 512:(ch + 1) * 512],
                                    rpb[:, ch * 512:(ch + 1) * 512],
                                    prps[n % 2][:, :], OP.add))
                                ir[0] += 1
                nc.all_engine_barrier()

                chtr = k.ctr("f0_htr"); chte = k.ctr("f0_hte")
                with (
                    nc.psum_tensor(k.nm("a1_pva"), [128, 128], BF16) as pva,
                    nc.psum_tensor(k.nm("a1_pvb"), [128, 128], BF16) as pvb,
                ):
                    pvs = [pva, pvb]
                    b_htr, b_hte = chtr.v, chte.v
                    with nc.Block() as blk:
                        @blk.tensor
                        def _(te):
                            for n in range(64):
                                if n >= 2:
                                    chte.wait(te, b_hte + n - 1)
                                chtr.inc(te.transpose(pvs[n % 2][:, :],
                                                      vb[:, n * 128:(n + 1) * 128],
                                                      k.ident[:, :]))
                        @blk.scalar
                        def _(s):
                            for n in range(64):
                                chtr.wait(s, b_htr + n + 1)
                                chte.inc(s.copy(vb[:, n * 128:(n + 1) * 128],
                                                pvs[n % 2][:, :]))
                    nc.all_engine_barrier()

                for h2 in range(2):
                    h_abs = hp * 2 + h2
                    q_h = qb[:, h2 * T:(h2 + 1) * T]
                    k_h = kb[:, h2 * T:(h2 + 1) * T]
                    v_h = vb[:, h2 * T:(h2 + 1) * T]
                    cns = k.ctr("a1_ns"); cnm = k.ctr("a1_nm"); cnv = k.ctr("a1_nv")
                    b_ns, b_nm, b_nv = cns.v, cnm.v, cnv.v
                    with (nc.psum_tensor(k.nm("a1_pn"), [1, 512], F32) as pn,
                          nc.psum_tensor(k.nm("a1_pb"), [128, 8], F32) as pbb):
                        with nc.Block() as blk:
                            @blk.scalar
                            def _(s):
                                for g in range(16):
                                    ti, tc = g // 8, g % 8
                                    tens = q_h if ti == 0 else k_h
                                    if g >= 2:
                                        cnm.wait(s, b_nm + g - 1)
                                    cns.inc(s.activation(
                                        sqb[:, (g % 2) * 512:(g % 2 + 1) * 512],
                                        tens[:, tc * 512:(tc + 1) * 512], AF.Square))

                            @blk.tensor
                            def _(te):
                                for g in range(16):
                                    cns.wait(te, b_ns + g + 1)
                                    if g >= 1:
                                        cnv.wait(te, b_nv + g)
                                    cnm.inc(te.matmul(pn[0:1, :], k.ones32[:, 0:1],
                                                      sqb[:, (g % 2) * 512:(g % 2 + 1) * 512],
                                                      start=True, stop=True))

                            @blk.vector
                            def _(v):
                                v.memset(ast[:, 0:3], 0.0)
                                v.drain()
                                for g in range(16):
                                    ti = g // 8
                                    cnm.wait(v, b_nm + g + 1)
                                    v.drain()
                                    v.reduce_max(ast[0:1, 2:3], pn[0:1, :], AX.X)
                                    v.drain()
                                    cnv.inc(v.tensor_tensor(ast[0:1, ti:ti + 1],
                                                            ast[0:1, ti:ti + 1],
                                                            ast[0:1, 2:3], OP.max))
                                    v.drain()
                        nc.all_engine_barrier()
                        csc2 = k.ctr("a1_s2")
                        b_s2 = csc2.v
                        with nc.Block() as blk:
                            @blk.scalar
                            def _(s):
                                s.activation(ast[0:1, 3:4], ast[0:1, 0:1], AF.Sqrt)
                                s.drain()
                                csc2.inc(s.activation(ast[0:1, 4:5], ast[0:1, 1:2],
                                                      AF.Sqrt))
                            @blk.vector
                            def _(v):
                                csc2.wait(v, b_s2 + 1)
                                v.drain()
                                v.tensor_tensor(ast[0:1, 6:7], ast[0:1, 3:4],
                                                ast[0:1, 4:5], OP.mult)
                                v.drain()
                                csc2.inc(v.tensor_scalar(ast[0:1, 6:7], ast[0:1, 6:7],
                                                         -1.0 / (DK ** 0.5), None,
                                                         OP.mult))
                            @blk.tensor
                            def _(te):
                                csc2.wait(te, b_s2 + 2)
                                csc2.inc(te.matmul(pbb[:, 0:1], k.ones_row[0:1, :],
                                                   ast[0:1, 6:7], start=True, stop=True))
                            @blk.scalar
                            def _(s):
                                csc2.wait(s, b_s2 + 3)
                                csc2.inc(s.copy(ast[:, 8:9], pbb[:, 0:1]))
                        nc.all_engine_barrier()

                    csc = k.ctr("a1_sc"); ce1 = k.ctr("a1_e1"); cmk = k.ctr("a1_mk")
                    cav = k.ctr("a1_av"); cdn = k.ctr("a1_dn"); cbr = k.ctr("a1_br")
                    coe = k.ctr("a1_oe")
                    b_sc, b_e1, b_mk = csc.v, ce1.v, cmk.v
                    b_av, b_dn, b_br, b_oe = cav.v, cdn.v, cbr.v, coe.v
                    with ExitStack() as _sa:
                        pSa = _sa.enter_context(
                            nc.psum_tensor(k.nm("a1_psa"), [128, 512], F32))
                        pSb = _sa.enter_context(
                            nc.psum_tensor(k.nm("a1_psb"), [128, 512], F32))
                        pAV = _sa.enter_context(
                            nc.psum_tensor(k.nm("a1_pav"), [128, 512], F32))
                        pD = _sa.enter_context(
                            nc.psum_tensor(k.nm("a1_pdn"), [1, 512], F32))
                        pR = _sa.enter_context(
                            nc.psum_tensor(k.nm("a1_pr"), [128, 512], F32))
                        Pb = _sa.enter_context(
                            nc.sbuf_tensor(k.nm("a1_P"), [128, 2 * 512], BF16))
                        rb = _sa.enter_context(
                            nc.sbuf_tensor(k.nm("a1_rb"), [128, 512], F32))
                        pSs = [pSa, pSb]

                        def m_of(qs, j):
                            return sum(4 * (x + 1) for x in range(qs)) + j

                        def mk_of(qs, j):
                            return 4 * qs + (j - 4 * qs)

                        with nc.Block() as blk:
                            @blk.tensor
                            def _(te):
                                for qs in range(NTC):
                                    nj = 4 * (qs + 1)
                                    for j in range(nj):
                                        m = m_of(qs, j)
                                        if m >= 2:
                                            ce1.wait(te, b_e1 + m - 1)
                                        csc.inc(te.matmul(
                                            pSs[m % 2][:, :],
                                            k_h[:, j * 128:(j + 1) * 128],
                                            q_h[:, qs * 512:(qs + 1) * 512],
                                            start=True, stop=True))
                                        if j >= 4 * qs:
                                            cmk.wait(te, b_mk + 4 * qs + (j - 4 * qs) + 1)
                                        else:
                                            ce1.wait(te, b_e1 + m + 1)
                                        if j == 0:
                                            if qs >= 1:
                                                coe.wait(te, b_oe + qs)
                                                cdn.wait(te, b_dn + qs)
                                        te.matmul(pAV[:, :], v_h[:, j * 128:(j + 1) * 128],
                                                  Pb[:, (m % 2) * 512:(m % 2 + 1) * 512],
                                                  start=(j == 0), stop=(j == nj - 1))
                                        inst = te.matmul(pD[0:1, :], k.ones_col[:, 0:1],
                                                         Pb[:, (m % 2) * 512:(m % 2 + 1) * 512],
                                                         start=(j == 0), stop=(j == nj - 1))
                                        cav.inc(inst)
                                    cdn.wait(te, b_dn + qs + 1)
                                    if qs >= 1:
                                        cbr.wait(te, b_br + qs)
                                    csc.inc(te.matmul(pR[:, :], k.ones_row[0:1, :],
                                                      nrow[0:1, 512:1024],
                                                      start=True, stop=True))

                            @blk.scalar
                            def _(s):
                                nsc = 0
                                for qs in range(NTC):
                                    nj = 4 * (qs + 1)
                                    for j in range(nj):
                                        m = m_of(qs, j)
                                        nsc += 1
                                        csc.wait(s, b_sc + nsc)
                                        if m >= 2:
                                            cav.wait(s, b_av + m - 1)
                                        ce1.inc(s.activation(
                                            Pb[:, (m % 2) * 512:(m % 2 + 1) * 512],
                                            pSs[m % 2][:, :], AF.Exp,
                                            bias=ast[:, 8:9],
                                            scale=1.0 / (DK ** 0.5)))
                                    nsc += 1

                            @blk.vector
                            def _(v):
                                for qs in range(NTC):
                                    nj = 4 * (qs + 1)
                                    for j in range(4 * qs, nj):
                                        m = m_of(qs, j)
                                        off_i = j - 4 * qs
                                        ce1.wait(v, b_e1 + m + 1)
                                        v.drain()
                                        cmk.inc(v.tensor_tensor(
                                            Pb[:, (m % 2) * 512:(m % 2 + 1) * 512],
                                            Pb[:, (m % 2) * 512:(m % 2 + 1) * 512],
                                            cmask[:, off_i * 512:(off_i + 1) * 512],
                                            OP.mult))
                                    cav.wait(v, b_av + m_of(qs, nj - 1) + 1)
                                    v.drain()
                                    v.tensor_copy(nrow[0:1, 0:512], pD[0:1, :])
                                    v.drain()
                                    cdn.inc(v.reciprocal(nrow[0:1, 512:1024],
                                                         nrow[0:1, 0:512]))
                                    csc.wait(v, b_sc + m_of(qs, nj - 1) + 1 + (qs + 1))
                                    v.drain()
                                    v.tensor_copy(rb[:, :], pR[:, :])
                                    v.drain()
                                    cbr.inc(v.tensor_copy(rb[0:1, 0:1], rb[0:1, 0:1]))
                                    v.drain()
                                    coe.inc(v.tensor_tensor(
                                        o_all[:, h_abs * T + qs * 512:
                                              h_abs * T + (qs + 1) * 512],
                                        pAV[:, :], rb[:, :], OP.mult))
                        nc.all_engine_barrier()

        emit_wo_rs_residual(k, o_all, k.inp["w1o"], x_src, x_dst, "a1")
    nc.all_engine_barrier()


def emit_attn0(k, x_src, x_dst):
    nc, inp = k.nc, k.inp
    sp4 = maybe_share_collective_output_space("AllGather", G4, prefer_local=True)
    xn_sh = nc.dram_tensor(k.nm("a0_xnsh"), [TOK, D], BF16, kind="Internal")
    xn_full = nc.dram_tensor(k.nm("a0_xnf"), [T, D], BF16, kind="Internal",
                             addr_space=sp4)
    emit_ln_to_hbm(k, x_src, xn_sh)
    ccl = k.ctr("a0_cl")
    with nc.Block() as blk:
        @blk.gpsimd
        def _(g):
            ccl.inc(g.collective_compute("AllGather", OP.bypass, replica_groups=G4,
                                         ins=[xn_sh[:, :]], outs=[xn_full[:, :]]), 1)
            ccl.wait(g)
    nc.all_engine_barrier()

    NTC = 8
    DKS = 1.0 / (DK ** 0.5)

    with ExitStack() as _st:
        o_all = _st.enter_context(nc.sbuf_tensor(k.nm("a0_o"), [128, 4 * T], BF16))
        betab = _st.enter_context(nc.sbuf_tensor(k.nm("a0_beta"), [1, T], BF16))
        betab1 = _st.enter_context(nc.sbuf_tensor(k.nm("a0_beta1"), [1, T], BF16))
        taps = _st.enter_context(nc.sbuf_tensor(k.nm("a0_taps"), [128, 16], F32))
        onwc = _st.enter_context(nc.sbuf_tensor(k.nm("a0_onw"), [128, 1], F32))
        mk64 = _st.enter_context(nc.sbuf_tensor(k.nm("a0_msk"), [64, 3 * 64], BF16))
        mku = _st.enter_context(nc.sbuf_tensor(k.nm("a0_msku"), [64, 2 * 64], U8))
        onesrb = _st.enter_context(nc.sbuf_tensor(k.nm("a0_orow"), [1, 128], BF16))
        nrow = _st.enter_context(nc.sbuf_tensor(k.nm("a0_nr"), [1, 2 * 512], F32))
        cc0 = k.ctr("a0_c0")
        with nc.Block() as blk:
            @blk.sync
            def _(sync):
                cc0.inc(sync.dma_start(out=taps[:, 0:4], in_=inp["cq"][:, :]), 16)
                cc0.inc(sync.dma_start(out=taps[:, 4:8], in_=inp["ck"][:, :]), 16)
                cc0.inc(sync.dma_start(out=taps[:, 8:12], in_=inp["cv"][:, :]), 16)
                cc0.inc(sync.dma_start(out=onwc[:, :], in_=inp["onwc"][:, :]), 16)
                cc0.inc(sync.dma_start(out=mk64[:, 0:64], in_=inp["eye64"][:, :]), 16)
                cc0.inc(sync.dma_start(out=mk64[:, 64:128], in_=inp["eye64x2"][:, :]), 16)
                cc0.inc(sync.dma_start(out=mk64[:, 128:192], in_=inp["triu_i"][:, :]), 16)
                cc0.inc(sync.dma_start(out=mku[:, 0:64], in_=inp["stril_u8"][:, :]), 16)
                cc0.inc(sync.dma_start(out=mku[:, 64:128], in_=inp["striu_u8"][:, :]), 16)
                cc0.inc(sync.dma_start(out=onesrb[:, :], in_=inp["ones_rowb"][:, :]), 16)
                cc0.wait(sync)
        nc.all_engine_barrier()

        for hp in range(2):
            with ExitStack() as _st2:
                qb = _st2.enter_context(nc.sbuf_tensor(k.nm("a0_q"), [128, 2 * T], BF16))
                kb = _st2.enter_context(nc.sbuf_tensor(k.nm("a0_k"), [128, 2 * T], BF16))
                vb = _st2.enter_context(nc.sbuf_tensor(k.nm("a0_v"), [128, 2 * T], BF16))
                cvb = _st2.enter_context(nc.sbuf_tensor(k.nm("a0_cvb"), [128, T], BF16))
                sqb = _st2.enter_context(nc.sbuf_tensor(k.nm("a0_sq"), [128, 2 * 512], F32))
                cxt = [k.ctr("a1_xt0"), k.ctr("a1_xt1")]
                cwl = k.ctr("a1_wl")
                cpj = k.ctr("a1_pj")
                cpe = k.ctr("a1_pe")
                with ExitStack() as _st3:
                    wqkv = _st3.enter_context(nc.sbuf_tensor(k.nm("a0_w"), [128, 3 * 4096 + 32], BF16))
                    xnT = _st3.enter_context(nc.sbuf_tensor(k.nm("a0_xnT"), [128, 16 * 512], BF16))
                    pp0 = _st3.enter_context(nc.psum_tensor(k.nm("a0_pp0"), [128, 512], F32))
                    pp1 = _st3.enter_context(nc.psum_tensor(k.nm("a0_pp1"), [128, 512], F32))
                    ppb0 = _st3.enter_context(nc.psum_tensor(k.nm("a0_ppb"), [1, 512], F32))
                    ppb1 = _st3.enter_context(nc.psum_tensor(k.nm("a0_ppb2"), [1, 512], F32))
                    pps = [pp0, pp1]
                    b_wl, b_pj, b_pe = cwl.v, cpj.v, cpe.v
                    b_xt = [c.v for c in cxt]
                    with nc.Block() as blk:
                        @blk.gpsimd
                        def _(g):
                            for wi, wname in enumerate(("w0q", "w0k", "w0v")):
                                cwl.inc(g.dma_start(
                                    out=wqkv[:, wi * 4096:(wi + 1) * 4096]
                                        .rearrange("p (c w) -> p c w", c=16),
                                    in_=inp[wname][:, hp * 256:(hp + 1) * 256]
                                        .rearrange("(c p) w -> p c w", p=128)), 16)
                            cwl.inc(g.dma_start(
                                out=wqkv[:, 3 * 4096:3 * 4096 + 32]
                                    .rearrange("p (c w) -> p c w", c=16),
                                in_=inp["w0b"][:, hp * 2:hp * 2 + 2]
                                    .rearrange("(c p) w -> p c w", p=128)), 16)
                            cwl.wait(g)

                        @blk.sync
                        def _(sync):
                            for tc in range(NTC):
                                if tc >= 1:
                                    cpj.wait(sync, b_pj + 7 * tc)
                                for kc in range(16):
                                    cxt[tc % 2].inc(sync.dma_start_transpose(
                                        xnT[:, kc * 512:(kc + 1) * 512],
                                        xn_full[tc * 512:(tc + 1) * 512,
                                                kc * 128:(kc + 1) * 128]), 16)

                        @blk.tensor
                        def _(te):
                            cwl.wait(te, b_wl + 64)
                            for tc in range(NTC):
                                cxt[tc % 2].wait(te, b_xt[tc % 2] + 256 * (tc // 2 + 1))
                                for ti in range(6):
                                    wi, h2 = ti % 3, ti // 3
                                    g = tc * 7 + ti
                                    if g >= 2:
                                        cpe.wait(te, b_pe + g - 1)
                                    for kc in range(16):
                                        inst = te.matmul(
                                            pps[g % 2][:, :],
                                            wqkv[:, wi * 4096 + kc * 256 + h2 * 128:
                                                 wi * 4096 + kc * 256 + h2 * 128 + 128],
                                            xnT[:, kc * 512:(kc + 1) * 512],
                                            start=(kc == 0), stop=(kc == 15))
                                    cpj.inc(inst)
                                if tc >= 1:
                                    cpe.wait(te, b_pe + 7 * tc - 1)
                                for n3 in range(32):
                                    h3, kc = divmod(n3, 16)
                                    inst = te.matmul(
                                        (ppb0, ppb1)[h3][0:1, :],
                                        wqkv[:, 3 * 4096 + kc * 2 + h3:
                                             3 * 4096 + kc * 2 + h3 + 1],
                                        xnT[:, kc * 512:(kc + 1) * 512],
                                        start=(kc == 0), stop=(kc == 15))
                                cpj.inc(inst)

                        @blk.scalar
                        def _(s):
                            for tc in range(NTC):
                                for ti in range(6):
                                    wi, h2 = ti % 3, ti // 3
                                    g = tc * 7 + ti
                                    cpj.wait(s, b_pj + g + 1)
                                    dst = (qb, kb, vb)[wi]
                                    cpe.inc(s.copy(
                                        dst[:, h2 * T + tc * 512:h2 * T + (tc + 1) * 512],
                                        pps[g % 2][:, :]))
                                cpj.wait(s, b_pj + tc * 7 + 7)
                                s.activation(
                                    betab[0:1, tc * 512:(tc + 1) * 512],
                                    ppb0[0:1, :], AF.Sigmoid)
                                cpe.inc(s.activation(
                                    betab1[0:1, tc * 512:(tc + 1) * 512],
                                    ppb1[0:1, :], AF.Sigmoid))
                    nc.all_engine_barrier()

                ccv = k.ctr("a0_cvv"); ccs = k.ctr("a0_cvs")
                b_cv, b_cs = ccv.v, ccs.v
                with nc.Block() as blk:
                    @blk.vector
                    def _(v):
                        for ti in range(6):
                            wi, h2 = ti % 3, ti // 3
                            src_t = (qb, kb, vb)[wi][:, h2 * T:(h2 + 1) * T]
                            tp = taps[:, wi * 4:wi * 4 + 4]
                            if ti >= 1:
                                ccs.wait(v, b_cs + ti)
                            v.drain()
                            v.tensor_scalar(cvb[:, :], src_t, tp[:, 3:4], None, OP.mult)
                            v.drain()
                            v.scalar_tensor_tensor(cvb[:, 1:T], src_t[:, 0:T - 1],
                                                   tp[:, 2:3], cvb[:, 1:T],
                                                   OP.mult, OP.add)
                            v.drain()
                            v.scalar_tensor_tensor(cvb[:, 2:T], src_t[:, 0:T - 2],
                                                   tp[:, 1:2], cvb[:, 2:T],
                                                   OP.mult, OP.add)
                            v.drain()
                            ccv.inc(v.scalar_tensor_tensor(cvb[:, 3:T], src_t[:, 0:T - 3],
                                                           tp[:, 0:1], cvb[:, 3:T],
                                                           OP.mult, OP.add))

                    @blk.scalar
                    def _(s):
                        for ti in range(6):
                            wi, h2 = ti % 3, ti // 3
                            dst_t = (qb, kb, vb)[wi][:, h2 * T:(h2 + 1) * T]
                            ccv.wait(s, b_cv + ti + 1)
                            ccs.inc(s.activation(dst_t, cvb[:, :], AF.Silu))
                nc.all_engine_barrier()

                cl2 = [k.ctr("a0_l2a"), k.ctr("a0_l2b"), k.ctr("a0_l2c"),
                       k.ctr("a0_l2d"), k.ctr("a0_l2e")]
                b2 = [c.v for c in cl2]
                with (
                    nc.psum_tensor(k.nm("a0_pn"), [1, 512], F32) as pn,
                    nc.psum_tensor(k.nm("a0_pb2"), [128, 512], F32) as pb2,
                ):
                    for g in range(32):
                        ti, tc = g // 8, g % 8
                        wi, h2 = ti % 2, ti // 2
                        tens = (qb, kb)[wi][:, h2 * T + tc * 512:h2 * T + (tc + 1) * 512]
                        if g >= 2:
                            cl2[1].wait(nc.scalar, b2[1] + g - 1)
                        cl2[0].inc(nc.scalar.activation(
                            sqb[:, (g % 2) * 512:(g % 2 + 1) * 512], tens, AF.Square))
                        cl2[0].wait(nc.tensor, b2[0] + g + 1)
                        if g >= 1:
                            cl2[2].wait(nc.tensor, b2[2] + g)
                        cl2[1].inc(nc.tensor.matmul(pn[0:1, :], k.ones32[:, 0:1],
                                                    sqb[:, (g % 2) * 512:(g % 2 + 1) * 512],
                                                    start=True, stop=True))
                        cl2[1].wait(nc.vector, b2[1] + g + 1)
                        if g >= 1:
                            cl2[3].wait(nc.vector, b2[3] + 2 * g - 1)
                        nc.vector.drain()
                        nc.vector.tensor_scalar(nrow[0:1, 0:512], pn[0:1, :], 1.0,
                                                L2_EPS, OP.mult, OP.add)
                        nc.vector.drain()
                        cl2[2].inc(nc.vector.reciprocal(nrow[0:1, 0:512],
                                                        nrow[0:1, 0:512]))
                        cl2[2].wait(nc.scalar, b2[2] + g + 1)
                        if g >= 1:
                            cl2[3].wait(nc.scalar, b2[3] + 2 * g)
                        cl2[3].inc(nc.scalar.activation(nrow[0:1, 512:1024],
                                                        nrow[0:1, 0:512], AF.Sqrt))
                        cl2[3].wait(nc.tensor, b2[3] + 2 * g + 1)
                        if g >= 1:
                            cl2[4].wait(nc.tensor, b2[4] + g)
                        cl2[3].inc(nc.tensor.matmul(pb2[:, :], k.ones_row[0:1, :],
                                                    nrow[0:1, 512:1024],
                                                    start=True, stop=True))
                        cl2[3].wait(nc.vector, b2[3] + 2 * (g + 1))
                        nc.vector.drain()
                        sc = DKS if wi == 0 else 1.0
                        cl2[4].inc(nc.vector.scalar_tensor_tensor(
                            tens, tens, sc, pb2[:, :], OP.mult, OP.mult))
                    nc.all_engine_barrier()

                emit_delta_scan(k, hp, qb, kb, vb, betab, betab1, o_all, onwc,
                                mk64, mku, onesrb)
        emit_wo_rs_residual(k, o_all, k.inp["w0o"], x_src, x_dst, "a0")
    nc.all_engine_barrier()


def emit_delta_scan(k, hp, qb, kb, vb, betab, betab1, o_all, onwc, mk64, mku, onesrb):
    nc = k.nc
    NCH = T // CH
    ev = {}

    ct = k.ctr("sc_t"); cv = k.ctr("sc_v"); cs = k.ctr("sc_s")

    def prod(ctr, inst, name):
        ctr.inc(inst)
        ev[name] = (ctr, ctr.v)

    def need(eng, name):
        c2, vv = ev[name]
        eng.wait_ge(c2.h, vv)

    eye = mk64[:, 0:64]; eye2 = mk64[:, 64:128]; triui = mk64[:, 128:192]
    stril = mku[:, 0:64]; striu = mku[:, 64:128]
    TE, V, S = nc.tensor, nc.vector, nc.scalar

    with (
        nc.sbuf_tensor(k.nm("sc_S"), [128, 2 * 128], F32) as Sf,
        nc.sbuf_tensor(k.nm("sc_Sb"), [128, 2 * 128], BF16) as Sb,
        nc.sbuf_tensor(k.nm("sc_sm"), [128, 2 * 1152], BF16) as sm,
        nc.sbuf_tensor(k.nm("sc_or"), [128, 2 * 64], BF16) as oraw,
        nc.sbuf_tensor(k.nm("sc_sq"), [128, 2 * 64], F32) as osq,
        nc.sbuf_tensor(k.nm("sc_onr"), [1, 2 * 64], F32) as onr,
        nc.psum_tensor(k.nm("sc_p0"), [128, 512], F32) as p0,
        nc.psum_tensor(k.nm("sc_p1"), [128, 512], F32) as p1,
        nc.psum_tensor(k.nm("sc_p2"), [128, 512], F32) as p2,
        nc.psum_tensor(k.nm("sc_p3"), [128, 512], F32) as p3,
        nc.psum_tensor(k.nm("sc_pta"), [128, 128], BF16) as pta,
        nc.psum_tensor(k.nm("sc_ptb"), [128, 128], BF16) as ptb,
        nc.psum_tensor(k.nm("sc_pr0"), [1, 512], F32) as prn0,
        nc.psum_tensor(k.nm("sc_pr1"), [1, 512], F32) as prn1,
    ):
        psf = [p0, p1, p2, p3]
        prns = [prn0, prn1]
        ptt = [pta, ptb]
        tcnt = [0]

        def R(h2, lo, hi, p64=True):
            b = h2 * 1152
            ap = sm[:, b + lo:b + hi]
            return ap[0:64, :] if p64 else ap

        V.memset(Sf[:, :], 0.0)
        V.drain()
        prod(cv, V.memset(Sb[:, :], 0.0), "init")
        for h2 in range(2):
            ev[f"Sbf{h2}"] = ev["init"]
            ev[f"evA{h2}"] = ev["init"]
            ev[f"evB{h2}"] = ev["init"]
            ev[f"oev{h2}"] = ev["init"]
            ev[f"onp_r{h2}"] = ev["init"]
        ev["evT0"] = ev["init"]
        ev["evT1"] = ev["init"]
        V.drain()

        for c in range(NCH):
            for h2 in range(2):
                h_abs = 2 * hp + h2
                q_c = qb[:, h2 * T + c * CH:h2 * T + (c + 1) * CH]
                k_c = kb[:, h2 * T + c * CH:h2 * T + (c + 1) * CH]
                v_c = vb[:, h2 * T + c * CH:h2 * T + (c + 1) * CH]
                A_, B_ = psf[2 * h2], psf[2 * h2 + 1]
                bsrc = betab if h2 == 0 else betab1
                brow = bsrc[0:1, c * CH:(c + 1) * CH]

                need(TE, f"evA{h2}")
                prod(ct, TE.matmul(A_[:, 256:320], onesrb[0:1, :], brow,
                                   start=True, stop=True), f"bB{h2}")
                need(V, f"bB{h2}")
                V.drain()
                prod(cv, V.tensor_tensor(R(h2, 0, 64, False), k_c, A_[:, 256:320],
                                         OP.mult), f"kbf{h2}")
                V.drain()
                prod(cv, V.tensor_tensor(R(h2, 64, 128, False), v_c, A_[:, 256:320],
                                         OP.mult), f"vbf{h2}")
                need(TE, f"kbf{h2}")
                need(TE, f"evB{h2}")
                TE.matmul(B_[0:64, 0:64], R(h2, 0, 64, False), k_c,
                          start=True, stop=True)
                prod(ct, TE.matmul(B_[0:64, 64:128], k_c, R(h2, 0, 64, False),
                                   start=True, stop=True), f"AAt{h2}")
                need(TE, f"vbf{h2}")
                for nm3, srcb in (("kbt", R(h2, 0, 64, False)),
                                  ("vbt", R(h2, 64, 128, False)),
                                  ("ktm", k_c)):
                    slot = tcnt[0] % 2
                    need(TE, f"evT{slot}")
                    prod(ct, TE.transpose(ptt[slot][0:64, :], srcb, k.ident[:, :]),
                         f"tr_{nm3}{h2}")
                    off = {"kbt": 640, "vbt": 768, "ktm": 896}[nm3]
                    need(V, f"tr_{nm3}{h2}")
                    V.drain()
                    prod(cv, V.tensor_copy(R(h2, off, off + 128), ptt[slot][0:64, :]),
                         f"{nm3}s{h2}")
                    ev[f"evT{slot}"] = ev[f"{nm3}s{h2}"]
                    tcnt[0] += 1
                need(V, f"AAt{h2}")
                V.drain()
                V.select(R(h2, 128, 192), stril, B_[0:64, 0:64], eye, add_drain=True)
                V.drain()
                prod(cv, V.tensor_copy(R(h2, 128, 129), R(h2, 128, 129)), f"M{h2}")
                V.select(R(h2, 192, 256), striu, B_[0:64, 64:128], eye, add_drain=True)
                V.drain()
                prod(cv, V.tensor_copy(R(h2, 192, 193), R(h2, 192, 193)), f"N{h2}")
                V.tensor_scalar(R(h2, 384, 448), B_[0:64, 64:128], -1.0, None, OP.mult)
                V.drain()
                V.select(R(h2, 256, 320), striu, R(h2, 384, 448), eye, add_drain=True)
                V.drain()
                prod(cv, V.tensor_copy(R(h2, 256, 257), R(h2, 256, 257)), f"Y{h2}")
                V.tensor_scalar(R(h2, 384, 448), B_[0:64, 0:64], -1.0, None, OP.mult)
                V.drain()
                V.select(R(h2, 320, 384), stril, R(h2, 384, 448), eye, add_drain=True)
                V.drain()
                prod(cv, V.tensor_copy(R(h2, 320, 321), R(h2, 320, 321)), f"Yt{h2}")

                for it in range(5):
                    last = (it == 4)
                    need(TE, f"Y{h2}")
                    need(TE, f"M{h2}")
                    prod(ct, TE.matmul(A_[0:64, 0:64], R(h2, 128, 192),
                                       R(h2, 256, 320), start=True, stop=True),
                         f"Z{h2}")
                    if not last:
                        need(TE, f"Yt{h2}")
                        need(TE, f"N{h2}")
                        prod(ct, TE.matmul(B_[0:64, 0:64], R(h2, 192, 256),
                                           R(h2, 320, 384), start=True, stop=True),
                             f"Zt{h2}")
                    need(V, f"Z{h2}")
                    V.drain()
                    prod(cv, V.scalar_tensor_tensor(R(h2, 384, 448),
                                                    A_[0:64, 0:64], -1.0, eye2,
                                                    OP.mult, OP.add), f"G{h2}")
                    if not last:
                        need(V, f"Zt{h2}")
                        V.drain()
                        prod(cv, V.scalar_tensor_tensor(R(h2, 448, 512),
                                                        B_[0:64, 0:64], -1.0, eye2,
                                                        OP.mult, OP.add), f"Gt{h2}")
                    need(TE, f"G{h2}")
                    prod(ct, TE.matmul(A_[0:64, 64:128], R(h2, 320, 384),
                                       R(h2, 384, 448), start=True, stop=True),
                         f"Yp{h2}")
                    if not last:
                        need(TE, f"Gt{h2}")
                        prod(ct, TE.matmul(B_[0:64, 64:128], R(h2, 384, 448),
                                           R(h2, 320, 384), start=True, stop=True),
                             f"Ytp{h2}")
                    need(V, f"Yp{h2}")
                    V.drain()
                    prod(cv, V.tensor_copy(R(h2, 256, 320), A_[0:64, 64:128]),
                         f"Y{h2}")
                    if not last:
                        need(V, f"Ytp{h2}")
                        V.drain()
                        prod(cv, V.tensor_copy(R(h2, 320, 384), B_[0:64, 64:128]),
                             f"Yt{h2}")
                need(V, f"Y{h2}")
                V.drain()
                prod(cv, V.tensor_scalar(R(h2, 512, 576), R(h2, 256, 320), -1.0,
                                         None, OP.mult), f"Ttn{h2}")
                need(TE, f"Y{h2}")
                need(TE, f"vbts{h2}")
                TE.matmul(A_[0:64, 128:256], R(h2, 256, 320), R(h2, 768, 896),
                          start=True, stop=False)
                need(TE, f"Ttn{h2}")
                need(TE, f"kbts{h2}")
                prod(ct, TE.matmul(B_[:, 128:192], R(h2, 640, 768), R(h2, 512, 576),
                                   start=True, stop=True), f"wn{h2}")
                need(V, f"wn{h2}")
                V.drain()
                prod(cv, V.tensor_copy(sm[:, h2 * 1152 + 1024:h2 * 1152 + 1088],
                                       B_[:, 128:192]), f"wns{h2}")
                need(TE, f"wns{h2}")
                need(TE, f"Sbf{h2}")
                prod(ct, TE.matmul(A_[0:64, 128:256],
                                   sm[:, h2 * 1152 + 1024:h2 * 1152 + 1088],
                                   Sb[:, h2 * 128:(h2 + 1) * 128],
                                   start=False, stop=True), f"u{h2}")
                need(V, f"u{h2}")
                V.drain()
                prod(cv, V.tensor_copy(R(h2, 768, 896), A_[0:64, 128:256]),
                     f"ups{h2}")
                prod(ct, TE.matmul(B_[0:64, 192:256], k_c, q_c,
                                   start=True, stop=True), f"at{h2}")
                need(V, f"at{h2}")
                V.drain()
                prod(cv, V.tensor_tensor(R(h2, 576, 640), B_[0:64, 192:256], triui,
                                         OP.mult), f"ats{h2}")
                need(TE, f"oev{h2}")
                need(TE, f"ups{h2}")
                need(TE, f"ats{h2}")
                TE.matmul(B_[:, 256:320], Sb[:, h2 * 128:(h2 + 1) * 128], q_c,
                          start=True, stop=False)
                prod(ct, TE.matmul(B_[:, 256:320], R(h2, 768, 896), R(h2, 576, 640),
                                   start=False, stop=True), f"o{h2}")
                need(V, f"o{h2}")
                V.drain()
                prod(cv, V.tensor_copy(oraw[:, h2 * 64:(h2 + 1) * 64],
                                       B_[:, 256:320]), f"orawv{h2}")
                need(S, f"orawv{h2}")
                prod(cs, S.activation(osq[:, h2 * 64:(h2 + 1) * 64],
                                      oraw[:, h2 * 64:(h2 + 1) * 64], AF.Square),
                     f"osq{h2}")
                need(TE, f"osq{h2}")
                need(TE, f"onp_r{h2}")
                prod(ct, TE.matmul(prns[h2][0:1, 0:64], k.ones32[:, 0:1],
                                   osq[:, h2 * 64:(h2 + 1) * 64],
                                   start=True, stop=True), f"on{h2}")
                need(V, f"on{h2}")
                V.drain()
                prod(cv, V.tensor_scalar(onr[0:1, h2 * 64:(h2 + 1) * 64],
                                         prns[h2][0:1, 0:64], 1.0 / DK, RMS0_EPS,
                                         OP.mult, OP.add), f"onp{h2}")
                ev[f"onp_r{h2}"] = ev[f"onp{h2}"]
                need(S, f"onp{h2}")
                prod(cs, S.activation(onr[0:1, h2 * 64:(h2 + 1) * 64],
                                      onr[0:1, h2 * 64:(h2 + 1) * 64], AF.Sqrt),
                     f"onsq{h2}")
                need(V, f"onsq{h2}")
                V.drain()
                prod(cv, V.reciprocal(onr[0:1, h2 * 64:(h2 + 1) * 64],
                                      onr[0:1, h2 * 64:(h2 + 1) * 64]), f"orr{h2}")
                need(TE, f"orr{h2}")
                prod(ct, TE.matmul(A_[:, 320:384], k.ones_row[0:1, :],
                                   onr[0:1, h2 * 64:(h2 + 1) * 64],
                                   start=True, stop=True), f"ob{h2}")
                need(V, f"ob{h2}")
                V.drain()
                prod(cv, V.scalar_tensor_tensor(
                    o_all[:, h_abs * T + c * CH:h_abs * T + (c + 1) * CH],
                    oraw[:, h2 * 64:(h2 + 1) * 64], onwc[:, 0:1], A_[:, 320:384],
                    OP.mult, OP.mult), f"oev{h2}")
                ev[f"evA{h2}"] = ev[f"oev{h2}"]
                need(TE, f"ktms{h2}")
                need(TE, f"orawv{h2}")
                prod(ct, TE.matmul(B_[:, 320:448], R(h2, 896, 1024), R(h2, 768, 896),
                                   start=True, stop=True), f"dS{h2}")
                need(V, f"dS{h2}")
                V.drain()
                V.tensor_tensor(Sf[:, h2 * 128:(h2 + 1) * 128],
                                Sf[:, h2 * 128:(h2 + 1) * 128], B_[:, 320:448],
                                OP.add)
                V.drain()
                prod(cv, V.tensor_copy(Sb[:, h2 * 128:(h2 + 1) * 128],
                                       Sf[:, h2 * 128:(h2 + 1) * 128]), f"Sbf{h2}")
                ev[f"evB{h2}"] = ev[f"Sbf{h2}"]
    nc.all_engine_barrier()


def build_nc(nsub=N_SUB):
    nc = bass.Bass("TRN2", target_bir_lowering=False)
    inp = _decl_inputs(nc)
    y = nc.dram_tensor("y", [TOK, D], F32, kind="ExternalOutput")
    k = K(nc, inp, nsub)
    k.x_hbm = nc.dram_tensor("x_hbm", [TOK, D], F32, kind="Internal")
    k.y = y
    emit_const_load(k)
    if DEV_DUMP_F0:
        k.dbg_xnT = nc.dram_tensor("dbg_xnT", [128, 4 * D], BF16, kind="ExternalOutput")
        k.dbg_xnb = nc.dram_tensor("dbg_xnb", [128, 4 * D], BF16, kind="ExternalOutput")
        k.dbg_h = nc.dram_tensor("dbg_h", [128, 4 * INTER], BF16, kind="ExternalOutput")
    if nsub >= 1 or nsub == -1:
        emit_ghost_decode(k)
    if 4 <= nsub <= 6:
        emit_bitnet_prep(k)
    if nsub == -1:
        nsub = 0
        if DEV_DUMP_DECODE:
            dbg_g = nc.dram_tensor("dbg_g", [D, INTER], BF16, kind="ExternalOutput")
            dbg_d = nc.dram_tensor("dbg_d", [INTER, D], BF16, kind="ExternalOutput")
            cdbg = k.ctr("c_dbg")
            with nc.Block() as blk:
                @blk.sync
                def _(sync):
                    cdbg.inc(sync.dma_start(out=dbg_g[:, :], in_=k.wg_hbm[:, :]), 16)
                    cdbg.inc(sync.dma_start(out=dbg_d[:, :], in_=k.wd_hbm[:, :]), 16)
                    cdbg.wait(sync)
            nc.all_engine_barrier()

    if DEV_DUMP_F1:
        k.dbg_xqT = nc.dram_tensor("dbg_xqT", [128, 4 * D], BF16, kind="ExternalOutput")
        k.dbg_h = nc.dram_tensor("dbg_h", [128, 2 * H1], BF16, kind="ExternalOutput")
        k.dbg_hq = nc.dram_tensor("dbg_hq", [128, 2 * H1], BF16, kind="ExternalOutput")
    if nsub == 9905:
        emitters = [emit_attn0]
        nsub = 1
    elif nsub == 9904:
        emitters = [emit_attn1]
        nsub = 1
    elif nsub == 9903:
        emit_bitnet_prep(k)
        emitters = [emit_ffn1]
        nsub = 1
    elif nsub == 9902:
        k.dbg_qs = nc.dram_tensor("dbg_qs", [128, 16], F32, kind="ExternalOutput")
        emit_bitnet_prep(k)
        cdbg = k.ctr("c_dbgq")
        with nc.Block() as blk:
            @blk.sync
            def _(sync):
                cdbg.inc(sync.dma_start(out=k.dbg_qs[:, :], in_=k.qs_dbg[:, :]), 16)
                cdbg.wait(sync)
        nc.all_engine_barrier()
        emitters = []
        nsub = 0
    elif nsub == 9901:
        emit_bitnet_prep(k)
        emitters = [emit_ffn0, emit_ffn1]
        nsub = 2
    else:
        emitters = [
            emit_ffn0,
            emit_attn0,
            emit_ffn0,
            emit_ffn1,
            emit_attn1,
            emit_ffn1,
        ]
    nrun = min(max(nsub, 0), len(emitters))
    for i in range(nrun):
        src = inp["x"] if i == 0 else k.x_hbm
        dst = y if i == nrun - 1 else k.x_hbm
        emitters[i](k, src, dst)
    if nrun == 0:
        c = k.ctr("c_copyout")
        with nc.Block() as blk:
            @blk.sync
            def _(sync):
                c.inc(sync.dma_start(out=y[:, :], in_=inp["x"][:, :]), 16)
                c.wait(sync)
        nc.all_engine_barrier()
    nc.clear_and_free_semaphores([c.h for c in k.pool])
    nc.all_engine_barrier()
    return nc


def kernel(**inputs):
    in_maps = _host_layout(inputs)
    nc = build_nc(N_SUB)
    res = run_bass_kernel_spmd(nc, in_maps, core_ids=list(range(8)))
    out = np.zeros((B, T, D), np.float32)
    for c in range(8):
        b, q = c // 4, c % 4
        out[b, 1024 * q:1024 * (q + 1)] = res.results[c]["y"]
    return out


# version 07
# speedup vs baseline: 1.1113x; 1.1113x over previous
from contextlib import ExitStack

import numpy as np
import ml_dtypes
import concourse.bass as bass
import concourse.mybir as mybir
from concourse.bass_utils import run_bass_kernel_spmd
from concourse.replica_groups import maybe_share_collective_output_space

F32 = mybir.dt.float32
BF16 = mybir.dt.bfloat16
I32 = mybir.dt.int32
U8 = mybir.dt.uint8
OP = mybir.AluOpType
AF = mybir.ActivationFunctionType
AX = mybir.AxisListType

B, T, D = 2, 4096, 2048
TOK = 1024
NTI = 8
INTER = 5504
NKD = 43
H1 = 8192
DK = 128
CH = 64
G4 = [[0, 1, 2, 3], [4, 5, 6, 7]]
G8 = [list(range(8))]
LN_EPS = 1e-5
RMS0_EPS = 1e-5
L2_EPS = 1e-6
BIT_EPS = 1e-8
ATANH_HALF = 0.5493061443340549
RND = 3 * 2.0 ** 22

N_SUB = 6
DEV_DUMP_DECODE = False
DEV_DUMP_F0 = False
DEV_DUMP_F1 = False


class Ctr:
    def __init__(self, nc, name):
        self.h = nc.alloc_semaphore(name)
        self.v = 0

    def inc(self, inst, k=1):
        inst.then_inc(self.h, k)
        self.v += k
        return inst

    def wait(self, eng, v=None):
        eng.wait_ge(self.h, self.v if v is None else v)


def _bf(a):
    return np.ascontiguousarray(np.asarray(a).astype(ml_dtypes.bfloat16))


def _c(a):
    return np.ascontiguousarray(a)


def _host_layout(inp):
    x = np.asarray(inp["L0_attn_x"])
    g_idxT = _c(np.asarray(inp["L0_ffn_gate_idx"]).T)
    u_idxT = _c(np.asarray(inp["L0_ffn_up_idx"]).T)
    d_idxT = _c(np.asarray(inp["L0_ffn_down_idx"]).T)
    lutb = _c(np.broadcast_to(np.asarray(inp["L0_ffn_lut"])[None, :], (128, 16)))

    w0q = np.asarray(inp["L0_attn_Wq"]); w0k = np.asarray(inp["L0_attn_Wk"])
    w0v = np.asarray(inp["L0_attn_Wv"]); w0b = np.asarray(inp["L0_attn_Wb"])
    w0o = np.asarray(inp["L0_attn_Wo"])
    cq = _c(np.broadcast_to(np.asarray(inp["L0_attn_conv_q"])[None, :], (128, 4)))
    ck = _c(np.broadcast_to(np.asarray(inp["L0_attn_conv_k"])[None, :], (128, 4)))
    cv = _c(np.broadcast_to(np.asarray(inp["L0_attn_conv_v"])[None, :], (128, 4)))
    onw = _c(np.broadcast_to(np.asarray(inp["L0_attn_onorm_w"])[None, :], (128, 128)))
    onwcol = _c(np.asarray(inp["L0_attn_onorm_w"], np.float32)[:, None])

    w1q = np.asarray(inp["L1_attn_Wq"]); w1k = np.asarray(inp["L1_attn_Wk"])
    w1v = np.asarray(inp["L1_attn_Wv"]); w1o = np.asarray(inp["L1_attn_Wo"])
    perm = np.concatenate([np.arange(0, 128, 2), np.arange(1, 128, 2)])

    f1T = _c(np.asarray(inp["L1_ffn_w1"]).T)
    f2T = _c(np.asarray(inp["L1_ffn_w2"]).T)
    f3T = _c(np.asarray(inp["L1_ffn_w3"]).T)
    g1b = _c(np.broadcast_to(np.asarray(inp["L1_ffn_g1"])[None, :], (128, D)))
    g2b = _c(np.broadcast_to(np.asarray(inp["L1_ffn_g2"])[None, :], (128, D)))
    g3b = _c(np.broadcast_to(np.asarray(inp["L1_ffn_g3"])[None, :], (128, H1)))

    ident = np.eye(128, dtype=ml_dtypes.bfloat16)
    eye64 = np.eye(64, dtype=ml_dtypes.bfloat16)
    eye64x2 = (2.0 * np.eye(64, dtype=np.float32)).astype(ml_dtypes.bfloat16)
    stril_u8 = _c(np.tril(np.ones((64, 64), np.uint8), -1))
    striu_u8 = _c(np.triu(np.ones((64, 64), np.uint8), 1))
    triu_i = _bf(np.triu(np.ones((64, 64), np.float32)))
    triu_i128 = _bf(np.triu(np.ones((128, 128), np.float32)))
    stril128_u8 = _c(np.tril(np.ones((128, 128), np.uint8), -1))
    striu128_u8 = _c(np.triu(np.ones((128, 128), np.uint8), 1))
    ones_col = np.ones((128, 1), dtype=ml_dtypes.bfloat16)
    inv_freq = 1.0 / (10000.0 ** (np.arange(0, DK, 2, dtype=np.float64) / DK))
    freqs = np.outer(inv_freq, np.arange(T, dtype=np.float64))
    cosb = _bf(np.concatenate([np.cos(freqs), np.cos(freqs)], axis=0).astype(np.float32))
    sinb = _bf(np.concatenate([np.sin(freqs), np.sin(freqs)], axis=0).astype(np.float32))
    mr = np.zeros((128, 128), np.float32)
    for d in range(64):
        mr[d, d + 64] = -1.0
        mr[d + 64, d] = 1.0
    rope_perm = _bf(mr.T)
    kk = np.arange(128)[:, None]
    qq = np.arange(512)[None, :]
    cmask = np.concatenate(
        [(qq >= off * 128 + kk).astype(np.float32) for off in range(4)],
        axis=1)
    cmask = _bf(cmask)

    in_maps = []
    for c in range(8):
        b, q = c // 4, c % 4
        hc0 = 512 * q
        rows = slice(256 * c, 256 * (c + 1))
        d_rows = slice(688 * c, 688 * (c + 1))
        prow = (np.arange(512 * q, 512 * (q + 1)).reshape(4, 128)[:, perm]).reshape(-1)
        m = {
            "x": _c(x[b, 1024 * q:1024 * (q + 1)]),
            "gidx": _c(g_idxT[rows]), "uidx": _c(u_idxT[rows]),
            "didx": _c(d_idxT[d_rows]), "lutb": lutb,
            "w0q": _c(w0q[:, hc0:hc0 + 512]), "w0k": _c(w0k[:, hc0:hc0 + 512]),
            "w0v": _c(w0v[:, hc0:hc0 + 512]), "w0b": _c(w0b[:, 4 * q:4 * q + 4]),
            "w0o": _c(w0o[hc0:hc0 + 512, :]),
            "cq": cq, "ck": ck, "cv": cv, "onw": onw, "onwc": onwcol,
            "w1q": _c(w1q[prow, :].T), "w1k": _c(w1k[prow, :].T),
            "w1v": _c(w1v[hc0:hc0 + 512, :].T),
            "w1o": _c(w1o.T[hc0:hc0 + 512, :]),
            "f1s": _c(f1T[rows]), "f2s": _c(f2T[rows]),
            "f3s": _c(f3T[1024 * c:1024 * (c + 1)]),
            "g1b": g1b, "g2b": g2b, "g3b": g3b,
            "ident": ident, "eye64": eye64, "eye64x2": eye64x2,
            "stril_u8": stril_u8, "striu_u8": striu_u8,
            "stril128_u8": stril128_u8, "striu128_u8": striu128_u8,
            "triu_i": triu_i, "triu_i128": triu_i128, "ones_col": ones_col,
            "ones_row": np.ones((1, 128), np.float32),
            "ones_rowb": np.ones((1, 128), ml_dtypes.bfloat16),
            "ones32": np.ones((128, 1), np.float32),
            "cosb": cosb, "sinb": sinb, "cmask": cmask, "rope_perm": rope_perm,
        }
        in_maps.append(m)
    return in_maps


def _decl_inputs(nc):
    t = {}
    def d(name, shape, dt):
        t[name] = nc.dram_tensor(name, list(shape), dt, kind="ExternalInput")
    d("x", (TOK, D), F32)
    d("gidx", (256, INTER), I32); d("uidx", (256, INTER), I32)
    d("didx", (688, D), I32); d("lutb", (128, 16), F32)
    d("w0q", (D, 512), F32); d("w0k", (D, 512), F32); d("w0v", (D, 512), F32)
    d("w0b", (D, 4), F32); d("w0o", (512, D), F32)
    d("cq", (128, 4), F32); d("ck", (128, 4), F32); d("cv", (128, 4), F32)
    d("onw", (128, 128), F32); d("onwc", (128, 1), F32)
    d("w1q", (D, 512), F32); d("w1k", (D, 512), F32); d("w1v", (D, 512), F32)
    d("w1o", (512, D), F32)
    d("f1s", (256, H1), F32); d("f2s", (256, H1), F32); d("f3s", (1024, D), F32)
    d("g1b", (128, D), F32); d("g2b", (128, D), F32); d("g3b", (128, H1), F32)
    d("ident", (128, 128), BF16); d("eye64", (64, 64), BF16)
    d("eye64x2", (64, 64), BF16)
    d("stril_u8", (64, 64), U8); d("striu_u8", (64, 64), U8)
    d("stril128_u8", (128, 128), U8); d("striu128_u8", (128, 128), U8)
    d("triu_i", (64, 64), BF16); d("triu_i128", (128, 128), BF16)
    d("ones_col", (128, 1), BF16)
    d("ones_row", (1, 128), F32); d("ones32", (128, 1), F32)
    d("cosb", (128, T), BF16); d("sinb", (128, T), BF16)
    d("rope_perm", (128, 128), BF16)
    d("cmask", (128, 4 * 512), BF16); d("ones_rowb", (1, 128), BF16)
    return t


CTR_ALIAS = {
    "dec_ld0": "ln_x0", "dec_ld1": "ln_x1", "dec_done": "ln_rc",
    "dec_st0": "ag_so0", "dec_st1": "ag_so1", "dec_col": "ccol",
    "bp_l0": "ln_x0", "bp_l1": "ln_x1", "bp_rd": "ln_rc", "bp_qz": "ln_sr",
    "bp_s0": "ag_so0", "bp_s1": "ag_so1", "bp_pr": "f0_ln", "bp_cl": "ccol",
    "bp_bc": "f0_tr",
    "f1_ln": "f0_ln", "f1_q": "f0_sl", "f1_qs": "f0_h",
    "f1_h": "f0_sl", "f1_hq": "f0_htr", "f1_hqs": "f0_hte",
    "a1_cl": "ccol", "a0_cl": "ccol", "a0_c0": "a1_c0",
    "a0_cvv": "a1_sc", "a0_cvs": "a1_e1",
    "a0_l2a": "a1_ns", "a0_l2b": "a1_nm", "a0_l2c": "a1_nv",
    "a0_l2d": "a1_s2", "a0_l2e": "a1_mk",
    "sc_t": "a1_av", "sc_v": "a1_dn", "sc_s": "a1_br",
    "wo_w": "ag_so0", "wo_so0": "f0_st0", "wo_so1": "f0_st1",
    "wo_so2": "f0_st2", "wo_so3": "f0_st3",
    "wo_mm": "f0_gu", "wo_ev": "f0_dk", "wo_cl": "ccol",
    "rs_x0": "ln_x0", "rs_x1": "ln_x1",
    "rs_y0": "f0_xr0", "rs_y1": "f0_xr1", "rs_ad": "f0_yev",
    "rs_w0": "ag_so0", "rs_w1": "ag_so1",
    "a1_wl": "wgl", "a1_rp": "f0_te", "a1_ra": "f0_tr",
    "c_dbgf0": "cdbg", "c_dbgf1": "cdbg", "c_dbgq": "cdbg",
    "c_copyout": "cdbg",
}


class K:
    def __init__(self, nc, inp, nsub):
        self.nc = nc
        self.inp = inp
        self.nsub = nsub
        self.sems = {}
        self._uid = 0
        self.pool = [Ctr(nc, f"ctr{i}") for i in range(72)]
        self._pool_i = 0

    def nm(self, base):
        self._uid += 1
        return f"{base}_{self._uid}"

    def ctr(self, name):
        name = CTR_ALIAS.get(name, name)
        if name not in self.sems:
            assert self._pool_i < len(self.pool), "sem pool exhausted"
            self.sems[name] = self.pool[self._pool_i]
            self._pool_i += 1
        return self.sems[name]


def emit_sem_clear(k):
    nc = k.nc
    with nc.Block() as blk:
        @blk.gpsimd
        def _(g):
            for c in k.pool:
                g.sem_clear(c.h)
    nc.all_engine_barrier()


def emit_const_load(k):
    nc, inp = k.nc, k.inp
    k.ident = nc.alloc_sbuf_tensor("sb_ident", [128, 128], BF16)
    k.lut_bf = nc.alloc_sbuf_tensor("sb_lutbf", [128, 16], F32)
    k.ones_col = nc.alloc_sbuf_tensor("sb_ones", [128, 1], BF16)
    k.ones_row = nc.alloc_sbuf_tensor("sb_onesr", [1, 128], F32)
    k.ones32 = nc.alloc_sbuf_tensor("sb_ones32", [128, 1], F32)
    k.g1b32 = nc.alloc_sbuf_tensor("sb_g1b", [128, D], F32)
    k.g2b32 = nc.alloc_sbuf_tensor("sb_g2b", [128, D], F32)
    k.g3bb = nc.alloc_sbuf_tensor("sb_g3b", [128, H1], BF16)
    c = k.ctr("c0")
    with nc.Block() as blk:
        @blk.sync
        def _(sync):
            c.inc(sync.dma_start(out=k.ident[:, :], in_=inp["ident"][:, :]), 16)
            c.inc(sync.dma_start(out=k.ones_col[:, :], in_=inp["ones_col"][:, :]), 16)
            c.inc(sync.dma_start(out=k.ones_row[:, :], in_=inp["ones_row"][:, :]), 16)
            c.inc(sync.dma_start(out=k.ones32[:, :], in_=inp["ones32"][:, :]), 16)
            c.inc(sync.dma_start(out=k.g1b32[:, :], in_=inp["g1b"][:, :]), 16)
            c.inc(sync.dma_start(out=k.g2b32[:, :], in_=inp["g2b"][:, :]), 16)
            c.wait(sync)
        @blk.gpsimd
        def _(g):
            cg = k.ctr("c0g")
            cg.inc(g.dma_start(out=k.lut_bf[:, :], in_=inp["lutb"][:, :]), 16)
            cg.inc(g.dma_start(out=k.g3bb[:, :], in_=inp["g3b"][:, :]), 16)
            cg.wait(g)
            c.wait(g)
    nc.all_engine_barrier()


def emit_ghost_decode(k):
    nc, inp = k.nc, k.inp
    sp8 = maybe_share_collective_output_space("AllGather", G8, prefer_local=True)
    k.wg_hbm = nc.dram_tensor("wg_hbm", [D, INTER], BF16, kind="Internal", addr_space=sp8)
    k.wu_hbm = nc.dram_tensor("wu_hbm", [D, INTER], BF16, kind="Internal", addr_space=sp8)
    k.wd_hbm = nc.dram_tensor("wd_hbm", [INTER, D], BF16, kind="Internal", addr_space=sp8)
    wg_sh = nc.dram_tensor("wg_sh", [256, INTER], BF16, kind="Internal")
    wu_sh = nc.dram_tensor("wu_sh", [256, INTER], BF16, kind="Internal")
    wd_sh = nc.dram_tensor("wd_sh", [688, D], BF16, kind="Internal")

    cl = [k.ctr("dec_ld0"), k.ctr("dec_ld1")]
    cd = k.ctr("dec_done")
    cs = [k.ctr("dec_st0"), k.ctr("dec_st1")]
    tiles = []
    for src, dst, rt, w in (
        (inp["gidx"], wg_sh, [128, 128], INTER),
        (inp["uidx"], wu_sh, [128, 128], INTER),
        (inp["didx"], wd_sh, [128] * 5 + [48], D),
    ):
        r0 = 0
        for nrow in rt:
            tiles.append((src, dst, r0, nrow, w))
            r0 += nrow

    with (
        nc.sbuf_tensor("dec_idx0", [128, INTER], I32) as ib0,
        nc.sbuf_tensor("dec_idx1", [128, INTER], I32) as ib1,
        nc.sbuf_tensor("dec_f", [128, INTER], BF16) as fb,
        nc.sbuf_tensor("dec_tmp", [128, INTER], BF16) as tmp,
        nc.sbuf_tensor("dec_acc0", [128, INTER], BF16) as ac0,
        nc.sbuf_tensor("dec_acc1", [128, INTER], BF16) as ac1,
    ):
        ibufs = [ib0, ib1]; accs = [ac0, ac1]
        b_l = [c.v for c in cl]; b_d = cd.v; b_s = [c.v for c in cs]
        with nc.Block() as blk:
            @blk.sync
            def _(sync):
                for i, (src, dst, r0, nrow, w) in enumerate(tiles):
                    if i >= 2:
                        cd.wait(sync, b_d + i - 1)
                    cl[i % 2].inc(sync.dma_start(out=ibufs[i % 2][0:nrow, 0:w],
                                                 in_=src[r0:r0 + nrow, :]), 16)

            @blk.vector
            def _(v):
                for i, (src, dst, r0, nrow, w) in enumerate(tiles):
                    cl[i % 2].wait(v, b_l[i % 2] + 16 * (i // 2 + 1))
                    if i >= 2:
                        cs[i % 2].wait(v, b_s[i % 2] + 16 * ((i - 2) // 2 + 1))
                    ib = ibufs[i % 2][0:nrow, 0:w]
                    ac = accs[i % 2][0:nrow, 0:w]
                    fbs = fb[0:nrow, 0:w]
                    v.tensor_copy(fbs, ib)
                    v.drain()
                    v.tensor_scalar(ac, fbs, 0.0, k.lut_bf[0:nrow, 0:1],
                                    OP.is_equal, OP.mult)
                    for j in range(1, 15):
                        v.drain()
                        v.tensor_scalar(tmp[0:nrow, 0:w], fbs, float(j),
                                        k.lut_bf[0:nrow, j:j + 1], OP.is_equal, OP.mult)
                        v.drain()
                        v.tensor_tensor(ac, ac, tmp[0:nrow, 0:w], OP.add)
                    v.drain()
                    v.tensor_scalar(tmp[0:nrow, 0:w], fbs, 15.0,
                                    k.lut_bf[0:nrow, 15:16], OP.is_equal, OP.mult)
                    v.drain()
                    cd.inc(v.tensor_tensor(ac, ac, tmp[0:nrow, 0:w], OP.add))

            @blk.gpsimd
            def _(g):
                for i, (src, dst, r0, nrow, w) in enumerate(tiles):
                    cd.wait(g, b_d + i + 1)
                    cs[i % 2].inc(g.dma_start(out=dst[r0:r0 + nrow, :],
                                              in_=accs[i % 2][0:nrow, 0:w]), 16)
                cs[0].wait(g)
                cs[1].wait(g)
                ccol = k.ctr("dec_col")
                ccol.inc(g.collective_compute("AllGather", OP.bypass, replica_groups=G8,
                                              ins=[wg_sh[:, :]], outs=[k.wg_hbm[:, :]]), 1)
                ccol.inc(g.collective_compute("AllGather", OP.bypass, replica_groups=G8,
                                              ins=[wu_sh[:, :]], outs=[k.wu_hbm[:, :]]), 1)
                ccol.inc(g.collective_compute("AllGather", OP.bypass, replica_groups=G8,
                                              ins=[wd_sh[:, :]], outs=[k.wd_hbm[:, :]]), 1)
                ccol.wait(g)
    nc.all_engine_barrier()


def emit_ln_block(k, x_src, tiles, xnb, xnb_stride, cdone):
    nc = k.nc
    cx = [k.ctr("ln_x0"), k.ctr("ln_x1")]
    crc = k.ctr("ln_rc"); csr = k.ctr("ln_sr")
    b_x = [c.v for c in cx]
    b_rc, b_sr, b_done = crc.v, csr.v, cdone.v
    xb = k.ln_xb; sq = k.ln_sq; st = k.ln_st
    n = len(tiles)
    with nc.Block() as blk:
        @blk.sync
        def _(sync):
            for i, gt in enumerate(tiles):
                if i >= 2:
                    cdone.wait(sync, b_done + i - 1)
                cx[i % 2].inc(sync.dma_start(out=xb[:, (i % 2) * D:(i % 2 + 1) * D],
                                             in_=x_src[gt * 128:(gt + 1) * 128, :]), 16)

        @blk.vector
        def _(v):
            for i, gt in enumerate(tiles):
                xt = xb[:, (i % 2) * D:(i % 2 + 1) * D]
                cx[i % 2].wait(v, b_x[i % 2] + 16 * (i // 2 + 1))
                if i > 0:
                    v.drain()
                v.reduce_sum(st[:, 0:1], xt, AX.X)
                v.tensor_tensor(sq[:, :], xt, xt, OP.mult)
                v.drain()
                v.reduce_sum(st[:, 1:2], sq[:, :], AX.X)
                v.drain()
                v.tensor_scalar(st[:, 0:1], st[:, 0:1], 1.0 / D, None, OP.mult)
                v.tensor_scalar(st[:, 1:2], st[:, 1:2], 1.0 / D, None, OP.mult)
                v.drain()
                v.tensor_tensor(st[:, 2:3], st[:, 0:1], st[:, 0:1], OP.mult)
                v.drain()
                v.tensor_tensor(st[:, 2:3], st[:, 1:2], st[:, 2:3], OP.subtract)
                v.drain()
                v.tensor_scalar(st[:, 2:3], st[:, 2:3], LN_EPS, None, OP.add)
                v.drain()
                crc.inc(v.reciprocal(st[:, 3:4], st[:, 2:3]))
                v.drain()
                csr.wait(v, b_sr + i + 1)
                cdone.inc(v.tensor_scalar(xnb[:, i * xnb_stride:i * xnb_stride + D],
                                          xt, st[:, 0:1], st[:, 4:5],
                                          OP.subtract, OP.mult))

        @blk.scalar
        def _(s):
            for i, gt in enumerate(tiles):
                crc.wait(s, b_rc + i + 1)
                csr.inc(s.activation(st[:, 4:5], st[:, 3:4], AF.Sqrt))


def emit_ffn0(k, x_src, x_dst):
    nc = k.nc
    ST = 4
    NSUP = NTI // ST
    OCW = 512
    NOC = (INTER + OCW - 1) // OCW

    cln = k.ctr("f0_ln")
    ctr_ = k.ctr("f0_tr"); cte = k.ctr("f0_te")
    cwg = [k.ctr("f0_wg0"), k.ctr("f0_wg1")]
    cgu = k.ctr("f0_gu")
    csl = k.ctr("f0_sl"); ch_ = k.ctr(k.nm("f0_h"))
    chtr = k.ctr("f0_htr"); chte = k.ctr("f0_hte")
    cwd = [k.ctr(f"f0_wd{i}") for i in range(8)]
    cdk = k.ctr("f0_dk")
    cxr = [k.ctr(f"f0_xr{i}") for i in range(4)]
    cyev = k.ctr("f0_yev")
    cst = [k.ctr(f"f0_st{i}") for i in range(4)]

    with (
        nc.sbuf_tensor(k.nm("f0_xnb"), [128, ST * D], BF16) as xnb,
        nc.sbuf_tensor(k.nm("f0_xnT"), [128, ST * D], BF16) as xnT,
        nc.sbuf_tensor(k.nm("f0_h"), [128, ST * INTER], BF16) as hbuf,
        nc.sbuf_tensor(k.nm("f0_silu"), [128, 2 * OCW], F32) as silu_t,
        nc.sbuf_tensor(k.nm("f0_xr"), [128, 4 * 512], F32) as xrbuf,
        nc.sbuf_tensor(k.nm("f0_y"), [128, 4 * 512], F32) as ybuf,
    ):
        for sup in range(NSUP):
            tiles = list(range(sup * ST, sup * ST + ST))
            with (
                nc.sbuf_tensor(k.nm("f0_lnx"), [128, 2 * D], F32) as ln_xb,
                nc.sbuf_tensor(k.nm("f0_lnsq"), [128, D], F32) as ln_sq,
                nc.sbuf_tensor(k.nm("f0_lnst"), [128, 16], F32) as ln_st,
                nc.psum_tensor(k.nm("f0_pta"), [128, 128], BF16) as pta,
                nc.psum_tensor(k.nm("f0_ptb"), [128, 128], BF16) as ptb,
            ):
                pts = [pta, ptb]
                k.ln_xb, k.ln_sq, k.ln_st = ln_xb, ln_sq, ln_st
                b_ln = cln.v
                emit_ln_block(k, x_src, tiles, xnb, D, cln)
                b_tr, b_te = ctr_.v, cte.v
                with nc.Block() as blk:
                    @blk.tensor
                    def _(te):
                        for i in range(ST):
                            cln.wait(te, b_ln + i + 1)
                            for kc in range(16):
                                n = i * 16 + kc
                                if n >= 2:
                                    cte.wait(te, b_te + n - 1)
                                ctr_.inc(te.transpose(
                                    pts[n % 2][:, :],
                                    xnb[:, i * D + kc * 128:i * D + (kc + 1) * 128],
                                    k.ident[:, :]))

                    @blk.scalar
                    def _(s):
                        for i in range(ST):
                            for kc in range(16):
                                n = i * 16 + kc
                                ctr_.wait(s, b_tr + n + 1)
                                cte.inc(s.copy(
                                    xnT[:, i * D + kc * 128:i * D + (kc + 1) * 128],
                                    pts[n % 2][:, :]))
                nc.all_engine_barrier()

            if DEV_DUMP_F0 and sup == 0:
                cdbg = k.ctr("c_dbgf0")
                with nc.Block() as blk:
                    @blk.sync
                    def _(sync):
                        cdbg.inc(sync.dma_start(out=k.dbg_xnT[:, :], in_=xnT[:, :]), 16)
                        cdbg.inc(sync.dma_start(out=k.dbg_xnb[:, :], in_=xnb[:, :]), 16)
                        cdbg.wait(sync)
                nc.all_engine_barrier()

            with (
                nc.sbuf_tensor(k.nm("f0_wgs"), [128, 2 * 16 * OCW], BF16) as wgs,
                nc.sbuf_tensor(k.nm("f0_wus"), [128, 2 * 16 * OCW], BF16) as wus,
                nc.psum_tensor(k.nm("f0_pg0"), [128, 512], F32) as pg0,
                nc.psum_tensor(k.nm("f0_pg1"), [128, 512], F32) as pg1,
                nc.psum_tensor(k.nm("f0_pu0"), [128, 512], F32) as pu0,
                nc.psum_tensor(k.nm("f0_pu1"), [128, 512], F32) as pu1,
            ):
                pgs = [pg0, pg1]; pus = [pu0, pu1]
                b_gu, b_sl, b_h = cgu.v, csl.v, ch_.v
                b_wg = [c.v for c in cwg]
                with nc.Block() as blk:
                    @blk.sync
                    def _(sync):
                        for oc in range(NOC):
                            w = min(OCW, INTER - oc * OCW)
                            if oc >= 2:
                                cgu.wait(sync, b_gu + (oc - 1) * ST)
                            sl = (oc % 2) * 16 * OCW
                            for hbm, sb in ((k.wg_hbm, wgs), (k.wu_hbm, wus)):
                                for qtr in range(4):
                                    cwg[oc % 2].inc(sync.dma_start(
                                        out=sb[:, sl + qtr * 4 * w: sl + (qtr + 1) * 4 * w]
                                            .rearrange("p (c w) -> p c w", c=4),
                                        in_=hbm[qtr * 512:(qtr + 1) * 512,
                                                oc * OCW:oc * OCW + w]
                                            .rearrange("(c p) w -> p c w", p=128)), 16)

                    @blk.tensor
                    def _(te):
                        for oc in range(NOC):
                            w = min(OCW, INTER - oc * OCW)
                            cwg[oc % 2].wait(te, b_wg[oc % 2] + 16 * 8 * (oc // 2 + 1))
                            sl = (oc % 2) * 16 * OCW
                            for t in range(ST):
                                g = oc * ST + t
                                pgx, pux = pgs[g % 2], pus[g % 2]
                                if g >= 2:
                                    csl.wait(te, b_sl + g - 1)
                                    ch_.wait(te, b_h + g - 1)
                                for kc in range(16):
                                    te.matmul(pgx[:, 0:w],
                                              xnT[:, t * D + kc * 128:t * D + (kc + 1) * 128],
                                              wgs[:, sl + kc * w: sl + (kc + 1) * w],
                                              start=(kc == 0), stop=(kc == 15))
                                for kc in range(16):
                                    inst = te.matmul(pux[:, 0:w],
                                                     xnT[:, t * D + kc * 128:t * D + (kc + 1) * 128],
                                                     wus[:, sl + kc * w: sl + (kc + 1) * w],
                                                     start=(kc == 0), stop=(kc == 15))
                                cgu.inc(inst)

                    @blk.scalar
                    def _(s):
                        for oc in range(NOC):
                            w = min(OCW, INTER - oc * OCW)
                            for t in range(ST):
                                g = oc * ST + t
                                cgu.wait(s, b_gu + g + 1)
                                if g >= 2:
                                    ch_.wait(s, b_h + g - 1)
                                csl.inc(s.activation(
                                    silu_t[:, (g % 2) * OCW:(g % 2) * OCW + w],
                                    pgs[g % 2][:, 0:w], AF.Silu))

                    @blk.vector
                    def _(v):
                        for oc in range(NOC):
                            w = min(OCW, INTER - oc * OCW)
                            for t in range(ST):
                                g = oc * ST + t
                                csl.wait(v, b_sl + g + 1)
                                ch_.inc(v.scalar_tensor_tensor(
                                    hbuf[:, t * INTER + oc * OCW:t * INTER + oc * OCW + w],
                                    pus[g % 2][:, 0:w], 1.0,
                                    silu_t[:, (g % 2) * OCW:(g % 2) * OCW + w],
                                    OP.mult, OP.mult))
                nc.all_engine_barrier()

            if DEV_DUMP_F0 and sup == 0:
                cdbg = k.ctr("c_dbgf0")
                with nc.Block() as blk:
                    @blk.sync
                    def _(sync):
                        cdbg.inc(sync.dma_start(out=k.dbg_h[:, :], in_=hbuf[:, :]), 16)
                        cdbg.wait(sync)
                nc.all_engine_barrier()

            with (
                nc.psum_tensor(k.nm("f0_pt2a"), [128, 128], BF16) as pt2a,
                nc.psum_tensor(k.nm("f0_pt2b"), [128, 128], BF16) as pt2b,
            ):
                pt2s = [pt2a, pt2b]
                b_htr, b_hte = chtr.v, chte.v
                with nc.Block() as blk:
                    @blk.tensor
                    def _(te):
                        for t in range(ST):
                            for kc in range(NKD):
                                n = t * NKD + kc
                                if n >= 2:
                                    chte.wait(te, b_hte + n - 1)
                                chtr.inc(te.transpose(
                                    pt2s[n % 2][:, :],
                                    hbuf[:, t * INTER + kc * 128:t * INTER + (kc + 1) * 128],
                                    k.ident[:, :]))

                    @blk.scalar
                    def _(s):
                        for t in range(ST):
                            for kc in range(NKD):
                                n = t * NKD + kc
                                chtr.wait(s, b_htr + n + 1)
                                chte.inc(s.copy(
                                    hbuf[:, t * INTER + kc * 128:t * INTER + (kc + 1) * 128],
                                    pt2s[n % 2][:, :]))
                nc.all_engine_barrier()

            with (
                nc.sbuf_tensor(k.nm("f0_wds"), [128, 8 * 512], BF16) as wds,
                nc.psum_tensor(k.nm("f0_pd"), [128, 4 * 512], F32) as pd,
            ):
                b_wd = [c.v for c in cwd]; b_dk = cdk.v
                b_xr = [c.v for c in cxr]; b_yev = cyev.v
                b_st = [c.v for c in cst]
                with nc.Block() as blk:
                    @blk.sync
                    def _(sync):
                        for ms in range(4):
                            for kc in range(NKD):
                                n = ms * NKD + kc
                                if n >= 8:
                                    cdk.wait(sync, b_dk + n - 7)
                                cwd[n % 8].inc(sync.dma_start(
                                    out=wds[:, (n % 8) * 512:(n % 8 + 1) * 512],
                                    in_=k.wd_hbm[kc * 128:(kc + 1) * 128,
                                                 ms * 512:(ms + 1) * 512]), 16)

                    @blk.scalar
                    def _(s):
                        for ms in range(4):
                            for t in range(ST):
                                n = ms * ST + t
                                gt = sup * ST + t
                                if n >= 4:
                                    cyev.wait(s, b_yev + n - 3)
                                cxr[n % 4].inc(s.dma_start(
                                    out=xrbuf[:, (n % 4) * 512:(n % 4 + 1) * 512],
                                    in_=x_src[gt * 128:(gt + 1) * 128,
                                              ms * 512:(ms + 1) * 512]), 16)

                    @blk.gpsimd
                    def _(g):
                        for ms in range(4):
                            for t in range(ST):
                                n = ms * ST + t
                                gt = sup * ST + t
                                cyev.wait(g, b_yev + n + 1)
                                cst[n % 4].inc(g.dma_start(
                                    out=x_dst[gt * 128:(gt + 1) * 128,
                                              ms * 512:(ms + 1) * 512],
                                    in_=ybuf[:, (n % 4) * 512:(n % 4 + 1) * 512]), 16)
                        for c4 in cst:
                            c4.wait(g)

                    @blk.tensor
                    def _(te):
                        for ms in range(4):
                            for kc in range(NKD):
                                n = ms * NKD + kc
                                cwd[n % 8].wait(te, b_wd[n % 8] + 16 * (n // 8 + 1))
                                if kc == 0 and ms >= 1:
                                    cyev.wait(te, b_yev + (ms - 1) * ST + ST)
                                for t in range(ST):
                                    inst = te.matmul(
                                        pd[:, t * 512:(t + 1) * 512],
                                        hbuf[:, t * INTER + kc * 128:t * INTER + (kc + 1) * 128],
                                        wds[:, (n % 8) * 512:(n % 8 + 1) * 512],
                                        start=(kc == 0), stop=(kc == NKD - 1))
                                cdk.inc(inst)

                    @blk.vector
                    def _(v):
                        for ms in range(4):
                            for t in range(ST):
                                n = ms * ST + t
                                cdk.wait(v, b_dk + (ms + 1) * NKD)
                                cxr[n % 4].wait(v, b_xr[n % 4] + 16 * (n // 4 + 1))
                                if n >= 4:
                                    cst[n % 4].wait(v, b_st[n % 4] + 16 * ((n - 4) // 4 + 1))
                                cyev.inc(v.scalar_tensor_tensor(
                                    ybuf[:, (n % 4) * 512:(n % 4 + 1) * 512],
                                    pd[:, t * 512:(t + 1) * 512], 0.5,
                                    xrbuf[:, (n % 4) * 512:(n % 4 + 1) * 512],
                                    OP.mult, OP.add))
                nc.all_engine_barrier()
    nc.all_engine_barrier()


def emit_bitnet_prep(k):
    nc, inp = k.nc, k.inp
    sp8 = maybe_share_collective_output_space("AllGather", G8, prefer_local=True)
    spr = maybe_share_collective_output_space("AllReduce", G8, prefer_local=True)
    k.w1q_hbm = nc.dram_tensor("w1q_hbm", [D, H1], BF16, kind="Internal", addr_space=sp8)
    k.w2q_hbm = nc.dram_tensor("w2q_hbm", [D, H1], BF16, kind="Internal", addr_space=sp8)
    k.w3q_hbm = nc.dram_tensor("w3q_hbm", [H1, D], BF16, kind="Internal", addr_space=sp8)
    w1_sh = nc.dram_tensor("w1_sh", [256, H1], BF16, kind="Internal")
    w2_sh = nc.dram_tensor("w2_sh", [256, H1], BF16, kind="Internal")
    w3_sh = nc.dram_tensor("w3_sh", [1024, D], BF16, kind="Internal")
    ar_in = nc.dram_tensor("ar_in", [1, 4], F32, kind="Internal")
    ar_out = nc.dram_tensor("ar_out", [1, 4], F32, kind="Internal", addr_space=spr)

    tiles = []
    for src, dst, nt, w, mi in ((inp["f1s"], w1_sh, 2, H1, 0),
                                (inp["f2s"], w2_sh, 2, H1, 1),
                                (inp["f3s"], w3_sh, 8, D, 2)):
        for r in range(nt):
            tiles.append((src, dst, r, w, mi))

    cwl = [k.ctr("bp_l0"), k.ctr("bp_l1")]
    crd = k.ctr("bp_rd")
    cqz = k.ctr("bp_qz")
    cqs = [k.ctr("bp_s0"), k.ctr("bp_s1")]
    cpr = k.ctr("bp_pr")
    ccl = k.ctr("bp_cl")
    cbc = k.ctr("bp_bc")
    k.qs_dbg = nc.alloc_sbuf_tensor("bp_qs_keep", [128, 16], F32)

    with (
        nc.sbuf_tensor("bp_w0", [128, H1], F32) as wb0,
        nc.sbuf_tensor("bp_w1", [128, H1], F32) as wb1,
        nc.sbuf_tensor("bp_m", [128, H1], BF16) as mbuf,
        nc.sbuf_tensor("bp_q0", [128, H1], BF16) as qb0,
        nc.sbuf_tensor("bp_q1", [128, H1], BF16) as qb1,
        nc.sbuf_tensor("bp_st", [128, 16], F32) as st,
        nc.sbuf_tensor("bp_prow", [1, 16], F32) as prow,
        nc.sbuf_tensor("bp_qs", [128, 16], F32) as qs,
    ):
        wbufs = [wb0, wb1]; qbufs = [qb0, qb1]
        b_l = [c.v for c in cwl]; b_rd = crd.v
        with nc.Block() as blk:
            @blk.sync
            def _(sync):
                for i, (src, dst, r, w, mi) in enumerate(tiles):
                    if i >= 2:
                        crd.wait(sync, b_rd + i - 1)
                    cwl[i % 2].inc(sync.dma_start(
                        out=wbufs[i % 2][:, 0:w],
                        in_=src[r * 128:(r + 1) * 128, :]), 16)

            @blk.vector
            def _(v):
                v.memset(st[:, 0:3], 0.0)
                v.drain()
                for i, (src, dst, r, w, mi) in enumerate(tiles):
                    cwl[i % 2].wait(v, b_l[i % 2] + 16 * (i // 2 + 1))
                    v.reduce_sum(st[:, 4:5], wbufs[i % 2][:, 0:w], AX.X,
                                 apply_absolute_value=True)
                    v.drain()
                    crd.inc(v.tensor_tensor(st[:, mi:mi + 1], st[:, mi:mi + 1],
                                            st[:, 4:5], OP.add))
                    v.drain()
        nc.all_engine_barrier()
        with (nc.psum_tensor("bp_pp", [1, 8], F32) as pp,):
            with nc.Block() as blk:
                @blk.tensor
                def _(te):
                    te.matmul(pp[0:1, 0:3], k.ones32[:, 0:1], st[:, 0:3],
                              start=True, stop=True)
            nc.all_engine_barrier()
            with nc.Block() as blk:
                @blk.vector
                def _(v):
                    v.memset(prow[0:1, 0:4], 0.0)
                    v.drain()
                    v.tensor_copy(prow[0:1, 0:3], pp[0:1, 0:3])
                    v.drain()
            nc.all_engine_barrier()
        with nc.Block() as blk:
            @blk.sync
            def _(sync):
                cpr.inc(sync.dma_start(out=ar_in[0:1, 0:4], in_=prow[0:1, 0:4]), 16)
                cpr.wait(sync)
            @blk.gpsimd
            def _(g):
                cpr.wait(g)
                ccl.inc(g.collective_compute("AllReduce", OP.add, replica_groups=G8,
                                             ins=[ar_in[:, :]], outs=[ar_out[:, :]]), 1)
                ccl.wait(g)
        nc.all_engine_barrier()
        with (nc.psum_tensor("bp_bc", [128, 8], F32) as bc,):
            with nc.Block() as blk:
                @blk.sync
                def _(sync):
                    cbc.inc(sync.dma_start(out=prow[0:1, 4:7], in_=ar_out[0:1, 0:3]), 16)
                    cbc.wait(sync)
            nc.all_engine_barrier()
            with nc.Block() as blk:
                @blk.tensor
                def _(te):
                    te.matmul(bc[:, 0:3], k.ones_row[0:1, :], prow[0:1, 4:7],
                              start=True, stop=True)
            nc.all_engine_barrier()
            with nc.Block() as blk:
                @blk.vector
                def _(v):
                    v.memset(qs[:, :], 0.0)
                    v.drain()
                    v.tensor_scalar(qs[:, 0:3], bc[:, 0:3], 1.0 / (H1 * D), None, OP.mult)
                    v.drain()
                    v.tensor_scalar(qs[:, 3:6], qs[:, 0:3], BIT_EPS, ATANH_HALF,
                                    OP.add, OP.mult)
                    v.drain()
                    v.tensor_scalar(qs[:, 6:9], qs[:, 3:6], -1.0, None, OP.mult)
                    v.drain()
                    v.tensor_scalar(qs[:, 12:15], qs[:, 0:3], -1.0, 1.0, OP.mult, OP.add)
                    v.drain()
                    v.reciprocal(qs[:, 12:15], qs[:, 12:15])
                    v.drain()
                    v.tensor_scalar(st[:, 8:11], qs[:, 0:3], 1.0, None, OP.add)
                    v.drain()
                    crd.inc(v.tensor_tensor(qs[:, 12:15], qs[:, 12:15], st[:, 8:11],
                                            OP.mult))
                @blk.scalar
                def _(s):
                    crd.wait(s)
                    s.activation(qs[:, 9:12], qs[:, 12:15], AF.Ln, scale=1.0)
                    s.drain()
                    cqz.inc(s.mul(qs[:, 9:12], qs[:, 9:12], 0.5))
            nc.all_engine_barrier()
        with nc.Block() as blk:
            @blk.vector
            def _(v):
                v.tensor_copy(k.qs_dbg[:, :], qs[:, :])
                v.drain()
                v.tensor_copy(k.qs_dbg[0:1, 15:16], prow[0:1, 4:5])
        nc.all_engine_barrier()
        b_l2 = [c.v for c in cwl]; b_qz = cqz.v; b_s2 = [c.v for c in cqs]
        with nc.Block() as blk:
            @blk.sync
            def _(sync):
                for i, (src, dst, r, w, mi) in enumerate(tiles):
                    if i >= 2:
                        cqz.wait(sync, b_qz + i - 1)
                    cwl[i % 2].inc(sync.dma_start(
                        out=wbufs[i % 2][:, 0:w],
                        in_=src[r * 128:(r + 1) * 128, :]), 16)

            @blk.vector
            def _(v):
                for i, (src, dst, r, w, mi) in enumerate(tiles):
                    cwl[i % 2].wait(v, b_l2[i % 2] + 16 * (i // 2 + 1))
                    if i >= 2:
                        cqs[i % 2].wait(v, b_s2[i % 2] + 16 * ((i - 2) // 2 + 1))
                    wt = wbufs[i % 2][:, 0:w]
                    qt = qbufs[i % 2][:, 0:w]
                    v.drain()
                    v.tensor_scalar(qt, wt, qs[:, 3 + mi:4 + mi], qs[:, 9 + mi:10 + mi],
                                    OP.is_gt, OP.mult)
                    v.tensor_scalar(mbuf[:, 0:w], wt, qs[:, 6 + mi:7 + mi],
                                    qs[:, 9 + mi:10 + mi], OP.is_lt, OP.mult)
                    v.drain()
                    cqz.inc(v.tensor_tensor(qt, qt, mbuf[:, 0:w], OP.subtract))

            @blk.gpsimd
            def _(g):
                for i, (src, dst, r, w, mi) in enumerate(tiles):
                    cqz.wait(g, b_qz + i + 1)
                    cqs[i % 2].inc(g.dma_start(out=dst[r * 128:(r + 1) * 128, :],
                                               in_=qbufs[i % 2][:, 0:w]), 16)
                for c2 in cqs:
                    c2.wait(g)
                ccl.inc(g.collective_compute("AllGather", OP.bypass, replica_groups=G8,
                                             ins=[w1_sh[:, :]], outs=[k.w1q_hbm[:, :]]), 1)
                ccl.inc(g.collective_compute("AllGather", OP.bypass, replica_groups=G8,
                                             ins=[w2_sh[:, :]], outs=[k.w2q_hbm[:, :]]), 1)
                ccl.inc(g.collective_compute("AllGather", OP.bypass, replica_groups=G8,
                                             ins=[w3_sh[:, :]], outs=[k.w3q_hbm[:, :]]), 1)
                ccl.wait(g)
    nc.all_engine_barrier()


def emit_ffn1(k, x_src, x_dst):
    nc = k.nc
    ST = 2
    NSUP = NTI // ST
    OCW = 512
    NOC1 = H1 // OCW
    NKD1 = H1 // 128

    cln = k.ctr("f1_ln")
    cq = k.ctr("f1_q")
    cqs = k.ctr("f1_qs")
    ctr_ = k.ctr("f0_tr"); cte = k.ctr("f0_te")
    cwg = [k.ctr("f0_wg0"), k.ctr("f0_wg1")]
    cgu = k.ctr("f0_gu")
    ch_ = k.ctr("f1_h")
    chq = k.ctr("f1_hq")
    chqs = k.ctr("f1_hqs")
    chtr = k.ctr("f0_htr"); chte = k.ctr("f0_hte")
    cwd = [k.ctr(f"f0_wd{i}") for i in range(8)]
    cdk = k.ctr("f0_dk")
    cxr = [k.ctr(f"f0_xr{i}") for i in range(4)]
    cyev = k.ctr("f0_yev")
    cst = [k.ctr(f"f0_st{i}") for i in range(4)]

    with (
        nc.sbuf_tensor(k.nm("f1_xnb"), [128, ST * D], BF16) as xnb,
        nc.sbuf_tensor(k.nm("f1_xqT"), [128, 2 * ST * D], BF16) as xqT,
        nc.sbuf_tensor(k.nm("f1_xq"), [128, 2 * D], BF16) as xqb,
        nc.sbuf_tensor(k.nm("f1_h"), [128, ST * H1], BF16) as hbuf,
        nc.sbuf_tensor(k.nm("f1_u"), [128, D], F32) as ubuf,
        nc.sbuf_tensor(k.nm("f1_hsq"), [128, H1], BF16) as hsq,
        nc.sbuf_tensor(k.nm("f1_st"), [128, 16], F32) as qst,
        nc.sbuf_tensor(k.nm("f1_xr"), [128, 4 * 512], F32) as xrbuf,
        nc.sbuf_tensor(k.nm("f1_y"), [128, 4 * 512], F32) as ybuf,
    ):
        for sup in range(NSUP):
            tiles = list(range(sup * ST, sup * ST + ST))
            with (
                nc.sbuf_tensor(k.nm("f1_lnx"), [128, 2 * D], F32) as ln_xb,
                nc.sbuf_tensor(k.nm("f1_lnsq"), [128, D], F32) as ln_sq,
                nc.sbuf_tensor(k.nm("f1_lnst"), [128, 16], F32) as ln_st,
                nc.psum_tensor(k.nm("f1_pta"), [128, 128], BF16) as pta,
                nc.psum_tensor(k.nm("f1_ptb"), [128, 128], BF16) as ptb,
            ):
                pts = [pta, ptb]
                k.ln_xb, k.ln_sq, k.ln_st = ln_xb, ln_sq, ln_st
                b_ln = cln.v
                emit_ln_block(k, x_src, tiles, xnb, D, cln)
                b_q, b_qs, b_tr, b_te = cq.v, cqs.v, ctr_.v, cte.v
                with nc.Block() as blk:
                    @blk.vector
                    def _(v):
                        for i in range(ST):
                            cln.wait(v, b_ln + i + 1)
                            if i >= 1:
                                cte.wait(v, b_te + 32 * i)
                            xnt = xnb[:, i * D:(i + 1) * D]
                            v.drain()
                            v.tensor_tensor(ubuf[:, :], xnt, xnt, OP.mult)
                            v.drain()
                            v.reduce_sum(qst[:, 0:1], ubuf[:, :], AX.X)
                            v.drain()
                            v.tensor_scalar(qst[:, 0:1], qst[:, 0:1], 1.0 / D, BIT_EPS,
                                            OP.mult, OP.add)
                            v.drain()
                            cq.inc(v.reciprocal(qst[:, 1:2], qst[:, 0:1]))
                            cqs.wait(v, b_qs + i + 1)
                            for bidx, gb in ((0, k.g1b32), (1, k.g2b32)):
                                v.drain()
                                v.scalar_tensor_tensor(ubuf[:, :], xnt, qst[:, 2:3],
                                                       gb[:, :], OP.mult, OP.mult)
                                v.drain()
                                v.reduce_max(qst[:, 3:4], ubuf[:, :], AX.X,
                                             apply_absolute_value=True)
                                v.drain()
                                v.tensor_scalar(qst[:, 3:4], qst[:, 3:4], 1e-4, None,
                                                OP.max)
                                v.drain()
                                v.reciprocal(qst[:, 4:5], qst[:, 3:4])
                                v.drain()
                                v.tensor_scalar(qst[:, 4:5], qst[:, 4:5], 127.0, None,
                                                OP.mult)
                                v.tensor_scalar(qst[:, 5:6], qst[:, 3:4], 1.0 / 127.0,
                                                None, OP.mult)
                                v.drain()
                                v.tensor_scalar(ubuf[:, :], ubuf[:, :], qst[:, 4:5],
                                                RND, OP.mult, OP.add)
                                v.drain()
                                cq.inc(v.tensor_scalar(
                                    xqb[:, bidx * D:(bidx + 1) * D], ubuf[:, :],
                                    RND, qst[:, 5:6], OP.subtract, OP.mult))

                    @blk.scalar
                    def _(s):
                        for i in range(ST):
                            cq.wait(s, b_q + 3 * i + 1)
                            s.activation(qst[:, 2:3], qst[:, 1:2], AF.Sqrt)
                            s.drain()
                            cqs.inc(s.copy(qst[:, 6:7], qst[:, 2:3]))
                            for n2 in range(32):
                                n = i * 32 + n2
                                ctr_.wait(s, b_tr + n + 1)
                                b2, kc = n2 // 16, n2 % 16
                                cte.inc(s.copy(
                                    xqT[:, b2 * ST * D + i * D + kc * 128:
                                        b2 * ST * D + i * D + (kc + 1) * 128],
                                    pts[n % 2][:, :]))

                    @blk.tensor
                    def _(te):
                        for i in range(ST):
                            cq.wait(te, b_q + 3 * (i + 1))
                            for n2 in range(32):
                                n = i * 32 + n2
                                b2, kc = n2 // 16, n2 % 16
                                if n >= 2:
                                    cte.wait(te, b_te + n - 1)
                                ctr_.inc(te.transpose(
                                    pts[n % 2][:, :],
                                    xqb[:, b2 * D + kc * 128:b2 * D + (kc + 1) * 128],
                                    k.ident[:, :]))
                nc.all_engine_barrier()

            if DEV_DUMP_F1 and sup == 0:
                cdbg = k.ctr("c_dbgf1")
                with nc.Block() as blk:
                    @blk.sync
                    def _(sync):
                        cdbg.inc(sync.dma_start(out=k.dbg_xqT[:, :], in_=xqT[:, :]), 16)
                        cdbg.wait(sync)
                nc.all_engine_barrier()

            for wstream in range(2):
                whbm = k.w1q_hbm if wstream == 0 else k.w2q_hbm
                with (
                    nc.sbuf_tensor(k.nm("f1_ws"), [128, 2 * 16 * OCW], BF16) as wss,
                    nc.psum_tensor(k.nm("f1_pg0"), [128, 512], F32) as pg0,
                    nc.psum_tensor(k.nm("f1_pg1"), [128, 512], F32) as pg1,
                ):
                    pgs = [pg0, pg1]
                    b_gu = cgu.v; b_h = ch_.v
                    b_wg = [c.v for c in cwg]
                    with nc.Block() as blk:
                        @blk.sync
                        def _(sync):
                            for oc in range(NOC1):
                                if oc >= 2:
                                    cgu.wait(sync, b_gu + (oc - 1) * ST)
                                sl = (oc % 2) * 16 * OCW
                                for qtr in range(4):
                                    cwg[oc % 2].inc(sync.dma_start(
                                        out=wss[:, sl + qtr * 4 * OCW: sl + (qtr + 1) * 4 * OCW]
                                            .rearrange("p (c w) -> p c w", c=4),
                                        in_=whbm[qtr * 512:(qtr + 1) * 512,
                                                 oc * OCW:(oc + 1) * OCW]
                                            .rearrange("(c p) w -> p c w", p=128)), 16)

                        @blk.tensor
                        def _(te):
                            for oc in range(NOC1):
                                cwg[oc % 2].wait(te, b_wg[oc % 2] + 64 * (oc // 2 + 1))
                                sl = (oc % 2) * 16 * OCW
                                for t in range(ST):
                                    g = oc * ST + t
                                    if g >= 2:
                                        ch_.wait(te, b_h + g - 1)
                                    for kc in range(16):
                                        inst = te.matmul(
                                            pgs[g % 2][:, :],
                                            xqT[:, wstream * ST * D + t * D + kc * 128:
                                                wstream * ST * D + t * D + (kc + 1) * 128],
                                            wss[:, sl + kc * OCW: sl + (kc + 1) * OCW],
                                            start=(kc == 0), stop=(kc == 15))
                                    cgu.inc(inst)

                        @blk.scalar
                        def _(s):
                            if wstream != 0:
                                return
                            for oc in range(NOC1):
                                for t in range(ST):
                                    g = oc * ST + t
                                    cgu.wait(s, b_gu + g + 1)
                                    ch_.inc(s.activation(
                                        hbuf[:, t * H1 + oc * OCW:t * H1 + (oc + 1) * OCW],
                                        pgs[g % 2][:, :], AF.Silu))

                        @blk.vector
                        def _(v):
                            if wstream != 1:
                                return
                            for oc in range(NOC1):
                                for t in range(ST):
                                    g = oc * ST + t
                                    cgu.wait(v, b_gu + g + 1)
                                    hs = hbuf[:, t * H1 + oc * OCW:t * H1 + (oc + 1) * OCW]
                                    v.drain()
                                    ch_.inc(v.tensor_tensor(hs, hs, pgs[g % 2][:, :],
                                                            OP.mult))
                    nc.all_engine_barrier()

            if DEV_DUMP_F1 and sup == 0:
                cdbg = k.ctr("c_dbgf1")
                with nc.Block() as blk:
                    @blk.sync
                    def _(sync):
                        cdbg.inc(sync.dma_start(out=k.dbg_h[:, :], in_=hbuf[:, :]), 16)
                        cdbg.wait(sync)
                nc.all_engine_barrier()

            b_hq, b_hqs = chq.v, chqs.v
            with nc.Block() as blk:
                @blk.vector
                def _(v):
                    for t in range(ST):
                        ht = hbuf[:, t * H1:(t + 1) * H1]
                        v.drain()
                        v.tensor_tensor(hsq[:, :], ht, ht, OP.mult)
                        v.drain()
                        v.reduce_sum(qst[:, 0:1], hsq[:, :], AX.X)
                        v.drain()
                        v.tensor_scalar(qst[:, 0:1], qst[:, 0:1], 1.0 / H1, BIT_EPS,
                                        OP.mult, OP.add)
                        v.drain()
                        chq.inc(v.reciprocal(qst[:, 1:2], qst[:, 0:1]))
                        chqs.wait(v, b_hqs + t + 1)
                        v.drain()
                        v.scalar_tensor_tensor(ht, ht, qst[:, 2:3], k.g3bb[:, :],
                                               OP.mult, OP.mult)
                        v.drain()
                        v.reduce_max(qst[:, 3:4], ht, AX.X, apply_absolute_value=True)
                        v.drain()
                        v.tensor_scalar(qst[:, 3:4], qst[:, 3:4], 1e-4, None, OP.max)
                        v.drain()
                        v.reciprocal(qst[:, 4:5], qst[:, 3:4])
                        v.drain()
                        v.tensor_scalar(qst[:, 4:5], qst[:, 4:5], 127.0, None, OP.mult)
                        v.tensor_scalar(qst[:, 5:6], qst[:, 3:4], 1.0 / 127.0, None,
                                        OP.mult)
                        for ch2 in range(4):
                            hs2 = hbuf[:, t * H1 + ch2 * D:t * H1 + (ch2 + 1) * D]
                            v.drain()
                            v.tensor_scalar(ubuf[:, :], hs2, qst[:, 4:5], RND,
                                            OP.mult, OP.add)
                            v.drain()
                            inst = v.tensor_scalar(hs2, ubuf[:, :], RND, qst[:, 5:6],
                                                   OP.subtract, OP.mult)
                        chq.inc(inst)

                @blk.scalar
                def _(s):
                    for t in range(ST):
                        chq.wait(s, b_hq + 2 * t + 1)
                        s.activation(qst[:, 2:3], qst[:, 1:2], AF.Sqrt)
                        s.drain()
                        chqs.inc(s.copy(qst[:, 7:8], qst[:, 2:3]))
            nc.all_engine_barrier()

            if DEV_DUMP_F1 and sup == 0:
                cdbg = k.ctr("c_dbgf1")
                with nc.Block() as blk:
                    @blk.sync
                    def _(sync):
                        cdbg.inc(sync.dma_start(out=k.dbg_hq[:, :], in_=hbuf[:, :]), 16)
                        cdbg.wait(sync)
                nc.all_engine_barrier()

            with (
                nc.psum_tensor(k.nm("f1_pt2a"), [128, 128], BF16) as pt2a,
                nc.psum_tensor(k.nm("f1_pt2b"), [128, 128], BF16) as pt2b,
            ):
                pt2s = [pt2a, pt2b]
                b_htr, b_hte = chtr.v, chte.v
                with nc.Block() as blk:
                    @blk.tensor
                    def _(te):
                        for t in range(ST):
                            for kc in range(NKD1):
                                n = t * NKD1 + kc
                                if n >= 2:
                                    chte.wait(te, b_hte + n - 1)
                                chtr.inc(te.transpose(
                                    pt2s[n % 2][:, :],
                                    hbuf[:, t * H1 + kc * 128:t * H1 + (kc + 1) * 128],
                                    k.ident[:, :]))

                    @blk.scalar
                    def _(s):
                        for t in range(ST):
                            for kc in range(NKD1):
                                n = t * NKD1 + kc
                                chtr.wait(s, b_htr + n + 1)
                                chte.inc(s.copy(
                                    hbuf[:, t * H1 + kc * 128:t * H1 + (kc + 1) * 128],
                                    pt2s[n % 2][:, :]))
                nc.all_engine_barrier()

            with (
                nc.sbuf_tensor(k.nm("f1_wds"), [128, 8 * 512], BF16) as wds,
                nc.psum_tensor(k.nm("f1_pd"), [128, ST * 512], F32) as pd,
            ):
                b_wd = [c.v for c in cwd]; b_dk = cdk.v
                b_xr = [c.v for c in cxr]; b_yev = cyev.v
                b_st = [c.v for c in cst]
                with nc.Block() as blk:
                    @blk.sync
                    def _(sync):
                        for ms in range(4):
                            for kc in range(NKD1):
                                n = ms * NKD1 + kc
                                if n >= 8:
                                    cdk.wait(sync, b_dk + n - 7)
                                cwd[n % 8].inc(sync.dma_start(
                                    out=wds[:, (n % 8) * 512:(n % 8 + 1) * 512],
                                    in_=k.w3q_hbm[kc * 128:(kc + 1) * 128,
                                                  ms * 512:(ms + 1) * 512]), 16)

                    @blk.scalar
                    def _(s):
                        for ms in range(4):
                            for t in range(ST):
                                n = ms * ST + t
                                gt = sup * ST + t
                                if n >= 4:
                                    cyev.wait(s, b_yev + n - 3)
                                cxr[n % 4].inc(s.dma_start(
                                    out=xrbuf[:, (n % 4) * 512:(n % 4 + 1) * 512],
                                    in_=x_src[gt * 128:(gt + 1) * 128,
                                              ms * 512:(ms + 1) * 512]), 16)

                    @blk.gpsimd
                    def _(g):
                        for ms in range(4):
                            for t in range(ST):
                                n = ms * ST + t
                                gt = sup * ST + t
                                cyev.wait(g, b_yev + n + 1)
                                cst[n % 4].inc(g.dma_start(
                                    out=x_dst[gt * 128:(gt + 1) * 128,
                                              ms * 512:(ms + 1) * 512],
                                    in_=ybuf[:, (n % 4) * 512:(n % 4 + 1) * 512]), 16)
                        for c4 in cst:
                            c4.wait(g)

                    @blk.tensor
                    def _(te):
                        for ms in range(4):
                            for kc in range(NKD1):
                                n = ms * NKD1 + kc
                                cwd[n % 8].wait(te, b_wd[n % 8] + 16 * (n // 8 + 1))
                                if kc == 0 and ms >= 1:
                                    cyev.wait(te, b_yev + (ms - 1) * ST + ST)
                                for t in range(ST):
                                    inst = te.matmul(
                                        pd[:, t * 512:(t + 1) * 512],
                                        hbuf[:, t * H1 + kc * 128:t * H1 + (kc + 1) * 128],
                                        wds[:, (n % 8) * 512:(n % 8 + 1) * 512],
                                        start=(kc == 0), stop=(kc == NKD1 - 1))
                                cdk.inc(inst)

                    @blk.vector
                    def _(v):
                        for ms in range(4):
                            for t in range(ST):
                                n = ms * ST + t
                                cdk.wait(v, b_dk + (ms + 1) * NKD1)
                                cxr[n % 4].wait(v, b_xr[n % 4] + 16 * (n // 4 + 1))
                                if n >= 4:
                                    cst[n % 4].wait(v, b_st[n % 4] + 16 * ((n - 4) // 4 + 1))
                                v.drain()
                                cyev.inc(v.scalar_tensor_tensor(
                                    ybuf[:, (n % 4) * 512:(n % 4 + 1) * 512],
                                    pd[:, t * 512:(t + 1) * 512], 0.5,
                                    xrbuf[:, (n % 4) * 512:(n % 4 + 1) * 512],
                                    OP.mult, OP.add))
                nc.all_engine_barrier()
    nc.all_engine_barrier()


def emit_ln_to_hbm(k, x_src, dst_hbm):
    nc = k.nc
    cln = k.ctr("ag_ln")
    cso = [k.ctr("ag_so0"), k.ctr("ag_so1")]
    cx = [k.ctr("ln_x0"), k.ctr("ln_x1")]
    crc = k.ctr("ln_rc"); csr = k.ctr("ln_sr")
    with (
        nc.sbuf_tensor(k.nm("ag_lnx"), [128, 2 * D], F32) as xb,
        nc.sbuf_tensor(k.nm("ag_lnsq"), [128, D], F32) as sq,
        nc.sbuf_tensor(k.nm("ag_lnst"), [128, 16], F32) as st,
        nc.sbuf_tensor(k.nm("ag_xnb"), [128, 2 * D], BF16) as xnb,
    ):
        b_x = [c.v for c in cx]
        b_rc, b_sr, b_ln = crc.v, csr.v, cln.v
        b_so = [c.v for c in cso]
        tiles = list(range(NTI))
        with nc.Block() as blk:
            @blk.sync
            def _(sync):
                for i, gt in enumerate(tiles):
                    if i >= 2:
                        cln.wait(sync, b_ln + i - 1)
                    cx[i % 2].inc(sync.dma_start(
                        out=xb[:, (i % 2) * D:(i % 2 + 1) * D],
                        in_=x_src[gt * 128:(gt + 1) * 128, :]), 16)

            @blk.vector
            def _(v):
                for i, gt in enumerate(tiles):
                    xt = xb[:, (i % 2) * D:(i % 2 + 1) * D]
                    cx[i % 2].wait(v, b_x[i % 2] + 16 * (i // 2 + 1))
                    if i > 0:
                        v.drain()
                    if i >= 2:
                        cso[i % 2].wait(v, b_so[i % 2] + 16 * ((i - 2) // 2 + 1))
                    v.reduce_sum(st[:, 0:1], xt, AX.X)
                    v.tensor_tensor(sq[:, :], xt, xt, OP.mult)
                    v.drain()
                    v.reduce_sum(st[:, 1:2], sq[:, :], AX.X)
                    v.drain()
                    v.tensor_scalar(st[:, 0:1], st[:, 0:1], 1.0 / D, None, OP.mult)
                    v.tensor_scalar(st[:, 1:2], st[:, 1:2], 1.0 / D, None, OP.mult)
                    v.drain()
                    v.tensor_tensor(st[:, 2:3], st[:, 0:1], st[:, 0:1], OP.mult)
                    v.drain()
                    v.tensor_tensor(st[:, 2:3], st[:, 1:2], st[:, 2:3], OP.subtract)
                    v.drain()
                    v.tensor_scalar(st[:, 2:3], st[:, 2:3], LN_EPS, None, OP.add)
                    v.drain()
                    crc.inc(v.reciprocal(st[:, 3:4], st[:, 2:3]))
                    v.drain()
                    csr.wait(v, b_sr + i + 1)
                    cln.inc(v.tensor_scalar(xnb[:, (i % 2) * D:(i % 2 + 1) * D],
                                            xt, st[:, 0:1], st[:, 4:5],
                                            OP.subtract, OP.mult))

            @blk.scalar
            def _(s):
                for i, gt in enumerate(tiles):
                    crc.wait(s, b_rc + i + 1)
                    csr.inc(s.activation(st[:, 4:5], st[:, 3:4], AF.Sqrt))

            @blk.gpsimd
            def _(g):
                for i, gt in enumerate(tiles):
                    cln.wait(g, b_ln + i + 1)
                    cso[i % 2].inc(g.dma_start(
                        out=dst_hbm[gt * 128:(gt + 1) * 128, :],
                        in_=xnb[:, (i % 2) * D:(i % 2 + 1) * D]), 16)
                for c2 in cso:
                    c2.wait(g)
    nc.all_engine_barrier()


def emit_wo_rs_residual(k, o_all, wo_inp, x_src, x_dst, tag):
    nc = k.nc
    yp_hbm = nc.dram_tensor(k.nm(f"{tag}_yp"), [T, D], BF16, kind="Internal")
    sprs = maybe_share_collective_output_space("ReduceScatter", G4, prefer_local=True)
    yrs_hbm = nc.dram_tensor(k.nm(f"{tag}_yrs"), [TOK, D], BF16, kind="Internal",
                             addr_space=sprs)
    cwo = k.ctr("wo_w")
    cmm = k.ctr("wo_mm")
    cev = k.ctr("wo_ev")
    cso = [k.ctr(f"wo_so{i}") for i in range(4)]
    ccl = k.ctr("wo_cl")
    with (
        nc.sbuf_tensor(k.nm("wo_w"), [128, 4 * D], BF16) as wos,
        nc.sbuf_tensor(k.nm("wo_y"), [128, 4 * 512], BF16) as ybuf,
        nc.psum_tensor(k.nm("wo_p0"), [128, 512], F32) as pw0,
        nc.psum_tensor(k.nm("wo_p1"), [128, 512], F32) as pw1,
    ):
        pws = [pw0, pw1]
        b_wo, b_mm, b_ev = cwo.v, cmm.v, cev.v
        b_so = [c.v for c in cso]
        with nc.Block() as blk:
            @blk.gpsimd
            def _(g):
                cwo.inc(g.dma_start(
                    out=wos[:, :].rearrange("p (c m) -> p c m", c=4),
                    in_=wo_inp[:, :].rearrange("(c p) m -> p c m", p=128)), 16)
                cwo.wait(g)
                for tt in range(32):
                    for ms in range(4):
                        n = tt * 4 + ms
                        cev.wait(g, b_ev + n + 1)
                        cso[n % 4].inc(g.dma_start(
                            out=yp_hbm[tt * 128:(tt + 1) * 128,
                                       ms * 512:(ms + 1) * 512],
                            in_=ybuf[:, (n % 4) * 512:(n % 4 + 1) * 512]), 16)
                for c2 in cso:
                    c2.wait(g)
                ccl.inc(g.collective_compute(
                    "ReduceScatter", OP.add, replica_groups=G4,
                    ins=[yp_hbm[:, :]], outs=[yrs_hbm[:, :]]), 1)
                ccl.wait(g)

            @blk.tensor
            def _(te):
                cwo.wait(te, b_wo + 16)
                for tt in range(32):
                    for ms in range(4):
                        n = tt * 4 + ms
                        if n >= 2:
                            cev.wait(te, b_ev + n - 1)
                        for h in range(4):
                            inst = te.matmul(
                                pws[n % 2][:, :],
                                o_all[:, h * T + tt * 128:h * T + (tt + 1) * 128],
                                wos[:, h * D + ms * 512:h * D + (ms + 1) * 512],
                                start=(h == 0), stop=(h == 3))
                        cmm.inc(inst)

            @blk.vector
            def _(v):
                for tt in range(32):
                    for ms in range(4):
                        n = tt * 4 + ms
                        cmm.wait(v, b_mm + n + 1)
                        if n >= 4:
                            cso[n % 4].wait(v, b_so[n % 4] + 16 * ((n - 4) // 4 + 1))
                        v.drain()
                        cev.inc(v.tensor_copy(
                            ybuf[:, (n % 4) * 512:(n % 4 + 1) * 512],
                            pws[n % 2][:, :]))
        nc.all_engine_barrier()

    cxl = [k.ctr("rs_x0"), k.ctr("rs_x1")]
    cyl = [k.ctr("rs_y0"), k.ctr("rs_y1")]
    cad = k.ctr("rs_ad")
    cwr = [k.ctr("rs_w0"), k.ctr("rs_w1")]
    with (
        nc.sbuf_tensor(k.nm("rs_x"), [128, 2 * D], F32) as xrb,
        nc.sbuf_tensor(k.nm("rs_yb"), [128, 2 * D], BF16) as yrb,
        nc.sbuf_tensor(k.nm("rs_o"), [128, 2 * D], F32) as orb,
    ):
        b_xl = [c.v for c in cxl]; b_yl = [c.v for c in cyl]
        b_ad = cad.v; b_wr = [c.v for c in cwr]
        with nc.Block() as blk:
            @blk.sync
            def _(sync):
                for i in range(NTI):
                    if i >= 2:
                        cad.wait(sync, b_ad + i - 1)
                    cxl[i % 2].inc(sync.dma_start(
                        out=xrb[:, (i % 2) * D:(i % 2 + 1) * D],
                        in_=x_src[i * 128:(i + 1) * 128, :]), 16)

            @blk.scalar
            def _(s):
                for i in range(NTI):
                    if i >= 2:
                        cad.wait(s, b_ad + i - 1)
                    cyl[i % 2].inc(s.dma_start(
                        out=yrb[:, (i % 2) * D:(i % 2 + 1) * D],
                        in_=yrs_hbm[i * 128:(i + 1) * 128, :]), 16)

            @blk.vector
            def _(v):
                for i in range(NTI):
                    cxl[i % 2].wait(v, b_xl[i % 2] + 16 * (i // 2 + 1))
                    cyl[i % 2].wait(v, b_yl[i % 2] + 16 * (i // 2 + 1))
                    if i >= 2:
                        cwr[i % 2].wait(v, b_wr[i % 2] + 16 * ((i - 2) // 2 + 1))
                    v.drain()
                    cad.inc(v.tensor_tensor(orb[:, (i % 2) * D:(i % 2 + 1) * D],
                                            xrb[:, (i % 2) * D:(i % 2 + 1) * D],
                                            yrb[:, (i % 2) * D:(i % 2 + 1) * D],
                                            OP.add))

            @blk.gpsimd
            def _(g):
                for i in range(NTI):
                    cad.wait(g, b_ad + i + 1)
                    cwr[i % 2].inc(g.dma_start(
                        out=x_dst[i * 128:(i + 1) * 128, :],
                        in_=orb[:, (i % 2) * D:(i % 2 + 1) * D]), 16)
                for c2 in cwr:
                    c2.wait(g)
        nc.all_engine_barrier()


def emit_attn1(k, x_src, x_dst):
    nc, inp = k.nc, k.inp
    sp4 = maybe_share_collective_output_space("AllGather", G4, prefer_local=True)
    xn_sh = nc.dram_tensor(k.nm("a1_xnsh"), [TOK, D], BF16, kind="Internal")
    xn_full = nc.dram_tensor(k.nm("a1_xnf"), [T, D], BF16, kind="Internal",
                             addr_space=sp4)
    emit_ln_to_hbm(k, x_src, xn_sh)
    ccl = k.ctr("a1_cl")
    with nc.Block() as blk:
        @blk.gpsimd
        def _(g):
            ccl.inc(g.collective_compute("AllGather", OP.bypass, replica_groups=G4,
                                         ins=[xn_sh[:, :]], outs=[xn_full[:, :]]), 1)
            ccl.wait(g)
    nc.all_engine_barrier()

    NTC = 8
    with (
        nc.sbuf_tensor(k.nm("a1_o"), [128, 4 * T], BF16) as o_all,
        nc.sbuf_tensor(k.nm("a1_cos"), [128, T], BF16) as cosb,
        nc.sbuf_tensor(k.nm("a1_sin"), [128, T], BF16) as sinb,
        nc.sbuf_tensor(k.nm("a1_rpm"), [128, 128], BF16) as rpermb,
        nc.sbuf_tensor(k.nm("a1_msk"), [128, 4 * 512], BF16) as cmask,
        nc.sbuf_tensor(k.nm("a1_st"), [128, 16], F32) as ast,
        nc.sbuf_tensor(k.nm("a1_sq"), [128, 2 * 512], F32) as sqb,
        nc.sbuf_tensor(k.nm("a1_nr"), [1, 2 * 512], F32) as nrow,
    ):
        cc0 = k.ctr("a1_c0")
        b_c0 = cc0.v
        with nc.Block() as blk:
            @blk.sync
            def _(sync):
                cc0.inc(sync.dma_start(out=cosb[:, :], in_=inp["cosb"][:, :]), 16)
                cc0.inc(sync.dma_start(out=sinb[:, :], in_=inp["sinb"][:, :]), 16)
                cc0.inc(sync.dma_start(out=cmask[:, :], in_=inp["cmask"][:, :]), 16)
                cc0.inc(sync.dma_start(out=rpermb[:, :], in_=inp["rope_perm"][:, :]), 16)
                cc0.wait(sync)
        nc.all_engine_barrier()

        for hp in range(2):
            with (
                nc.sbuf_tensor(k.nm("a1_q"), [128, 2 * T], BF16) as qb,
                nc.sbuf_tensor(k.nm("a1_k"), [128, 2 * T], BF16) as kb,
                nc.sbuf_tensor(k.nm("a1_v"), [128, 2 * T], BF16) as vb,
                nc.sbuf_tensor(k.nm("a1_rp"), [128, T], BF16) as rpb,
            ):
                cxt = [k.ctr("a1_xt0"), k.ctr("a1_xt1")]
                cwl = k.ctr("a1_wl")
                cpj = k.ctr("a1_pj")
                cpe = k.ctr("a1_pe")
                with (
                    nc.sbuf_tensor(k.nm("a1_w"), [128, 3 * 4096], BF16) as wqkv,
                    nc.sbuf_tensor(k.nm("a1_xnT"), [128, 16 * 512], BF16) as xnT,
                    nc.psum_tensor(k.nm("a1_pp0"), [128, 512], F32) as pp0,
                    nc.psum_tensor(k.nm("a1_pp1"), [128, 512], F32) as pp1,
                ):
                    pps = [pp0, pp1]
                    b_wl, b_pj, b_pe = cwl.v, cpj.v, cpe.v
                    b_xt = [c.v for c in cxt]
                    with nc.Block() as blk:
                        @blk.gpsimd
                        def _(g):
                            for wi, wname in enumerate(("w1q", "w1k", "w1v")):
                                cwl.inc(g.dma_start(
                                    out=wqkv[:, wi * 4096:(wi + 1) * 4096]
                                        .rearrange("p (c w) -> p c w", c=16),
                                    in_=inp[wname][:, hp * 256:(hp + 1) * 256]
                                        .rearrange("(c p) w -> p c w", p=128)), 16)
                            cwl.wait(g)

                        @blk.sync
                        def _(sync):
                            for tc in range(NTC):
                                if tc >= 1:
                                    cpj.wait(sync, b_pj + 6 * tc)
                                for kc in range(16):
                                    cxt[tc % 2].inc(sync.dma_start_transpose(
                                        xnT[:, kc * 512:(kc + 1) * 512],
                                        xn_full[tc * 512:(tc + 1) * 512,
                                                kc * 128:(kc + 1) * 128]), 16)

                        @blk.tensor
                        def _(te):
                            cwl.wait(te, b_wl + 48)
                            for tc in range(NTC):
                                cxt[tc % 2].wait(te, b_xt[tc % 2] + 256 * (tc // 2 + 1))
                                for ti in range(6):
                                    wi, h2 = ti % 3, ti // 3
                                    g = tc * 6 + ti
                                    if g >= 2:
                                        cpe.wait(te, b_pe + g - 1)
                                    for kc in range(16):
                                        inst = te.matmul(
                                            pps[g % 2][:, :],
                                            wqkv[:, wi * 4096 + kc * 256 + h2 * 128:
                                                 wi * 4096 + kc * 256 + h2 * 128 + 128],
                                            xnT[:, kc * 512:(kc + 1) * 512],
                                            start=(kc == 0), stop=(kc == 15))
                                    cpj.inc(inst)

                        @blk.scalar
                        def _(s):
                            for tc in range(NTC):
                                for ti in range(6):
                                    wi, h2 = ti % 3, ti // 3
                                    g = tc * 6 + ti
                                    cpj.wait(s, b_pj + g + 1)
                                    dst = (qb, kb, vb)[wi]
                                    cpe.inc(s.copy(
                                        dst[:, h2 * T + tc * 512:h2 * T + (tc + 1) * 512],
                                        pps[g % 2][:, :]))
                    nc.all_engine_barrier()

                crp = k.ctr("a1_rp")
                crm = k.ctr("a1_nm")
                cra = k.ctr("a1_ra")
                b_rp, b_rm, b_ra = crp.v, crm.v, cra.v
                with (
                    nc.sbuf_tensor(k.nm("a1_rp2"), [128, T], BF16) as rp2,
                    nc.psum_tensor(k.nm("a1_prp"), [128, 512], F32) as prp,
                    nc.psum_tensor(k.nm("a1_prq"), [128, 512], F32) as prq,
                ):
                    prps = [prp, prq]
                    ir = [0]
                    for h2 in range(2):
                        for ti, tens in enumerate((qb, kb)):
                            t_idx = h2 * 2 + ti
                            tt_ = tens[:, h2 * T:(h2 + 1) * T]
                            nc.vector.drain()
                            nc.vector.tensor_tensor(rpb[:, :], tt_, cosb[:, :],
                                                    OP.mult)
                            nc.vector.drain()
                            crp.inc(nc.vector.tensor_tensor(rp2[:, :], tt_,
                                                            sinb[:, :], OP.mult))
                            for ch in range(8):
                                n = ir[0]
                                crp.wait(nc.tensor, b_rp + t_idx + 1)
                                if n >= 2:
                                    cra.wait(nc.tensor, b_ra + n - 1)
                                crm.inc(nc.tensor.matmul(
                                    prps[n % 2][:, :], rpermb[:, :],
                                    rp2[:, ch * 512:(ch + 1) * 512],
                                    start=True, stop=True))
                                crm.wait(nc.vector, b_rm + n + 1)
                                nc.vector.drain()
                                cra.inc(nc.vector.tensor_tensor(
                                    tt_[:, ch * 512:(ch + 1) * 512],
                                    rpb[:, ch * 512:(ch + 1) * 512],
                                    prps[n % 2][:, :], OP.add))
                                ir[0] += 1
                nc.all_engine_barrier()

                chtr = k.ctr("f0_htr"); chte = k.ctr("f0_hte")
                with (
                    nc.psum_tensor(k.nm("a1_pva"), [128, 128], BF16) as pva,
                    nc.psum_tensor(k.nm("a1_pvb"), [128, 128], BF16) as pvb,
                ):
                    pvs = [pva, pvb]
                    b_htr, b_hte = chtr.v, chte.v
                    with nc.Block() as blk:
                        @blk.tensor
                        def _(te):
                            for n in range(64):
                                if n >= 2:
                                    chte.wait(te, b_hte + n - 1)
                                chtr.inc(te.transpose(pvs[n % 2][:, :],
                                                      vb[:, n * 128:(n + 1) * 128],
                                                      k.ident[:, :]))
                        @blk.scalar
                        def _(s):
                            for n in range(64):
                                chtr.wait(s, b_htr + n + 1)
                                chte.inc(s.copy(vb[:, n * 128:(n + 1) * 128],
                                                pvs[n % 2][:, :]))
                    nc.all_engine_barrier()

                for h2 in range(2):
                    h_abs = hp * 2 + h2
                    q_h = qb[:, h2 * T:(h2 + 1) * T]
                    k_h = kb[:, h2 * T:(h2 + 1) * T]
                    v_h = vb[:, h2 * T:(h2 + 1) * T]
                    cns = k.ctr("a1_ns"); cnm = k.ctr("a1_nm"); cnv = k.ctr("a1_nv")
                    b_ns, b_nm, b_nv = cns.v, cnm.v, cnv.v
                    with (nc.psum_tensor(k.nm("a1_pn"), [1, 512], F32) as pn,
                          nc.psum_tensor(k.nm("a1_pb"), [128, 8], F32) as pbb):
                        with nc.Block() as blk:
                            @blk.scalar
                            def _(s):
                                for g in range(16):
                                    ti, tc = g // 8, g % 8
                                    tens = q_h if ti == 0 else k_h
                                    if g >= 2:
                                        cnm.wait(s, b_nm + g - 1)
                                    cns.inc(s.activation(
                                        sqb[:, (g % 2) * 512:(g % 2 + 1) * 512],
                                        tens[:, tc * 512:(tc + 1) * 512], AF.Square))

                            @blk.tensor
                            def _(te):
                                for g in range(16):
                                    cns.wait(te, b_ns + g + 1)
                                    if g >= 1:
                                        cnv.wait(te, b_nv + g)
                                    cnm.inc(te.matmul(pn[0:1, :], k.ones32[:, 0:1],
                                                      sqb[:, (g % 2) * 512:(g % 2 + 1) * 512],
                                                      start=True, stop=True))

                            @blk.vector
                            def _(v):
                                v.memset(ast[:, 0:3], 0.0)
                                v.drain()
                                for g in range(16):
                                    ti = g // 8
                                    cnm.wait(v, b_nm + g + 1)
                                    v.drain()
                                    v.reduce_max(ast[0:1, 2:3], pn[0:1, :], AX.X)
                                    v.drain()
                                    cnv.inc(v.tensor_tensor(ast[0:1, ti:ti + 1],
                                                            ast[0:1, ti:ti + 1],
                                                            ast[0:1, 2:3], OP.max))
                                    v.drain()
                        nc.all_engine_barrier()
                        csc2 = k.ctr("a1_s2")
                        b_s2 = csc2.v
                        with nc.Block() as blk:
                            @blk.scalar
                            def _(s):
                                s.activation(ast[0:1, 3:4], ast[0:1, 0:1], AF.Sqrt)
                                s.drain()
                                csc2.inc(s.activation(ast[0:1, 4:5], ast[0:1, 1:2],
                                                      AF.Sqrt))
                            @blk.vector
                            def _(v):
                                csc2.wait(v, b_s2 + 1)
                                v.drain()
                                v.tensor_tensor(ast[0:1, 6:7], ast[0:1, 3:4],
                                                ast[0:1, 4:5], OP.mult)
                                v.drain()
                                csc2.inc(v.tensor_scalar(ast[0:1, 6:7], ast[0:1, 6:7],
                                                         -1.0 / (DK ** 0.5), None,
                                                         OP.mult))
                            @blk.tensor
                            def _(te):
                                csc2.wait(te, b_s2 + 2)
                                csc2.inc(te.matmul(pbb[:, 0:1], k.ones_row[0:1, :],
                                                   ast[0:1, 6:7], start=True, stop=True))
                            @blk.scalar
                            def _(s):
                                csc2.wait(s, b_s2 + 3)
                                csc2.inc(s.copy(ast[:, 8:9], pbb[:, 0:1]))
                        nc.all_engine_barrier()

                    csc = k.ctr("a1_sc"); ce1 = k.ctr("a1_e1"); cmk = k.ctr("a1_mk")
                    cav = k.ctr("a1_av"); cdn = k.ctr("a1_dn"); cbr = k.ctr("a1_br")
                    coe = k.ctr("a1_oe")
                    b_sc, b_e1, b_mk = csc.v, ce1.v, cmk.v
                    b_av, b_dn, b_br, b_oe = cav.v, cdn.v, cbr.v, coe.v
                    with ExitStack() as _sa:
                        pSa = _sa.enter_context(
                            nc.psum_tensor(k.nm("a1_psa"), [128, 512], F32))
                        pSb = _sa.enter_context(
                            nc.psum_tensor(k.nm("a1_psb"), [128, 512], F32))
                        pAV = _sa.enter_context(
                            nc.psum_tensor(k.nm("a1_pav"), [128, 512], F32))
                        pD = _sa.enter_context(
                            nc.psum_tensor(k.nm("a1_pdn"), [1, 512], F32))
                        pR = _sa.enter_context(
                            nc.psum_tensor(k.nm("a1_pr"), [128, 512], F32))
                        Pb = _sa.enter_context(
                            nc.sbuf_tensor(k.nm("a1_P"), [128, 2 * 512], BF16))
                        rb = _sa.enter_context(
                            nc.sbuf_tensor(k.nm("a1_rb"), [128, 512], F32))
                        pSs = [pSa, pSb]

                        def m_of(qs, j):
                            return sum(4 * (x + 1) for x in range(qs)) + j

                        def mk_of(qs, j):
                            return 4 * qs + (j - 4 * qs)

                        with nc.Block() as blk:
                            @blk.tensor
                            def _(te):
                                for qs in range(NTC):
                                    nj = 4 * (qs + 1)
                                    for j in range(nj):
                                        m = m_of(qs, j)
                                        if m >= 2:
                                            ce1.wait(te, b_e1 + m - 1)
                                        csc.inc(te.matmul(
                                            pSs[m % 2][:, :],
                                            k_h[:, j * 128:(j + 1) * 128],
                                            q_h[:, qs * 512:(qs + 1) * 512],
                                            start=True, stop=True))
                                        if j >= 4 * qs:
                                            cmk.wait(te, b_mk + 4 * qs + (j - 4 * qs) + 1)
                                        else:
                                            ce1.wait(te, b_e1 + m + 1)
                                        if j == 0:
                                            if qs >= 1:
                                                coe.wait(te, b_oe + qs)
                                                cdn.wait(te, b_dn + qs)
                                        te.matmul(pAV[:, :], v_h[:, j * 128:(j + 1) * 128],
                                                  Pb[:, (m % 2) * 512:(m % 2 + 1) * 512],
                                                  start=(j == 0), stop=(j == nj - 1))
                                        inst = te.matmul(pD[0:1, :], k.ones_col[:, 0:1],
                                                         Pb[:, (m % 2) * 512:(m % 2 + 1) * 512],
                                                         start=(j == 0), stop=(j == nj - 1))
                                        cav.inc(inst)
                                    cdn.wait(te, b_dn + qs + 1)
                                    if qs >= 1:
                                        cbr.wait(te, b_br + qs)
                                    csc.inc(te.matmul(pR[:, :], k.ones_row[0:1, :],
                                                      nrow[0:1, 512:1024],
                                                      start=True, stop=True))

                            @blk.scalar
                            def _(s):
                                nsc = 0
                                for qs in range(NTC):
                                    nj = 4 * (qs + 1)
                                    for j in range(nj):
                                        m = m_of(qs, j)
                                        nsc += 1
                                        csc.wait(s, b_sc + nsc)
                                        if m >= 2:
                                            cav.wait(s, b_av + m - 1)
                                        ce1.inc(s.activation(
                                            Pb[:, (m % 2) * 512:(m % 2 + 1) * 512],
                                            pSs[m % 2][:, :], AF.Exp,
                                            bias=ast[:, 8:9],
                                            scale=1.0 / (DK ** 0.5)))
                                    nsc += 1

                            @blk.vector
                            def _(v):
                                for qs in range(NTC):
                                    nj = 4 * (qs + 1)
                                    for j in range(4 * qs, nj):
                                        m = m_of(qs, j)
                                        off_i = j - 4 * qs
                                        ce1.wait(v, b_e1 + m + 1)
                                        v.drain()
                                        cmk.inc(v.tensor_tensor(
                                            Pb[:, (m % 2) * 512:(m % 2 + 1) * 512],
                                            Pb[:, (m % 2) * 512:(m % 2 + 1) * 512],
                                            cmask[:, off_i * 512:(off_i + 1) * 512],
                                            OP.mult))
                                    cav.wait(v, b_av + m_of(qs, nj - 1) + 1)
                                    v.drain()
                                    v.tensor_copy(nrow[0:1, 0:512], pD[0:1, :])
                                    v.drain()
                                    cdn.inc(v.reciprocal(nrow[0:1, 512:1024],
                                                         nrow[0:1, 0:512]))
                                    csc.wait(v, b_sc + m_of(qs, nj - 1) + 1 + (qs + 1))
                                    v.drain()
                                    v.tensor_copy(rb[:, :], pR[:, :])
                                    v.drain()
                                    cbr.inc(v.tensor_copy(rb[0:1, 0:1], rb[0:1, 0:1]))
                                    v.drain()
                                    coe.inc(v.tensor_tensor(
                                        o_all[:, h_abs * T + qs * 512:
                                              h_abs * T + (qs + 1) * 512],
                                        pAV[:, :], rb[:, :], OP.mult))
                        nc.all_engine_barrier()

        emit_wo_rs_residual(k, o_all, k.inp["w1o"], x_src, x_dst, "a1")
    nc.all_engine_barrier()


def emit_attn0(k, x_src, x_dst):
    nc, inp = k.nc, k.inp
    sp4 = maybe_share_collective_output_space("AllGather", G4, prefer_local=True)
    xn_sh = nc.dram_tensor(k.nm("a0_xnsh"), [TOK, D], BF16, kind="Internal")
    xn_full = nc.dram_tensor(k.nm("a0_xnf"), [T, D], BF16, kind="Internal",
                             addr_space=sp4)
    emit_ln_to_hbm(k, x_src, xn_sh)
    ccl = k.ctr("a0_cl")
    with nc.Block() as blk:
        @blk.gpsimd
        def _(g):
            ccl.inc(g.collective_compute("AllGather", OP.bypass, replica_groups=G4,
                                         ins=[xn_sh[:, :]], outs=[xn_full[:, :]]), 1)
            ccl.wait(g)
    nc.all_engine_barrier()

    NTC = 8
    DKS = 1.0 / (DK ** 0.5)

    with ExitStack() as _st:
        o_all = _st.enter_context(nc.sbuf_tensor(k.nm("a0_o"), [128, 4 * T], BF16))
        betab = _st.enter_context(nc.sbuf_tensor(k.nm("a0_beta"), [1, T], BF16))
        betab1 = _st.enter_context(nc.sbuf_tensor(k.nm("a0_beta1"), [1, T], BF16))
        taps = _st.enter_context(nc.sbuf_tensor(k.nm("a0_taps"), [128, 16], F32))
        onwc = _st.enter_context(nc.sbuf_tensor(k.nm("a0_onw"), [128, 1], F32))
        mk128 = _st.enter_context(nc.sbuf_tensor(k.nm("a0_msk"), [128, 128], BF16))
        mku128 = _st.enter_context(nc.sbuf_tensor(k.nm("a0_msku"), [128, 2 * 128], U8))
        onesrb = _st.enter_context(nc.sbuf_tensor(k.nm("a0_orow"), [1, 128], BF16))
        nrow = _st.enter_context(nc.sbuf_tensor(k.nm("a0_nr"), [1, 2 * 512], F32))
        cc0 = k.ctr("a0_c0")
        with nc.Block() as blk:
            @blk.sync
            def _(sync):
                cc0.inc(sync.dma_start(out=taps[:, 0:4], in_=inp["cq"][:, :]), 16)
                cc0.inc(sync.dma_start(out=taps[:, 4:8], in_=inp["ck"][:, :]), 16)
                cc0.inc(sync.dma_start(out=taps[:, 8:12], in_=inp["cv"][:, :]), 16)
                cc0.inc(sync.dma_start(out=onwc[:, :], in_=inp["onwc"][:, :]), 16)
                cc0.inc(sync.dma_start(out=mk128[:, :], in_=inp["triu_i128"][:, :]), 16)
                cc0.inc(sync.dma_start(out=mku128[:, 0:128],
                                       in_=inp["stril128_u8"][:, :]), 16)
                cc0.inc(sync.dma_start(out=mku128[:, 128:256],
                                       in_=inp["striu128_u8"][:, :]), 16)
                cc0.inc(sync.dma_start(out=onesrb[:, :], in_=inp["ones_rowb"][:, :]), 16)
                cc0.wait(sync)
        nc.all_engine_barrier()

        for hp in range(2):
            with ExitStack() as _st2:
                qb = _st2.enter_context(nc.sbuf_tensor(k.nm("a0_q"), [128, 2 * T], BF16))
                kb = _st2.enter_context(nc.sbuf_tensor(k.nm("a0_k"), [128, 2 * T], BF16))
                vb = _st2.enter_context(nc.sbuf_tensor(k.nm("a0_v"), [128, 2 * T], BF16))
                cvb = _st2.enter_context(nc.sbuf_tensor(k.nm("a0_cvb"), [128, T], BF16))
                sqb = _st2.enter_context(nc.sbuf_tensor(k.nm("a0_sq"), [128, 2 * 512], F32))
                cxt = [k.ctr("a1_xt0"), k.ctr("a1_xt1")]
                cwl = k.ctr("a1_wl")
                cpj = k.ctr("a1_pj")
                cpe = k.ctr("a1_pe")
                with ExitStack() as _st3:
                    wqkv = _st3.enter_context(nc.sbuf_tensor(k.nm("a0_w"), [128, 3 * 4096 + 32], BF16))
                    xnT = _st3.enter_context(nc.sbuf_tensor(k.nm("a0_xnT"), [128, 16 * 512], BF16))
                    pp0 = _st3.enter_context(nc.psum_tensor(k.nm("a0_pp0"), [128, 512], F32))
                    pp1 = _st3.enter_context(nc.psum_tensor(k.nm("a0_pp1"), [128, 512], F32))
                    ppb0 = _st3.enter_context(nc.psum_tensor(k.nm("a0_ppb"), [1, 512], F32))
                    ppb1 = _st3.enter_context(nc.psum_tensor(k.nm("a0_ppb2"), [1, 512], F32))
                    pps = [pp0, pp1]
                    b_wl, b_pj, b_pe = cwl.v, cpj.v, cpe.v
                    b_xt = [c.v for c in cxt]
                    with nc.Block() as blk:
                        @blk.gpsimd
                        def _(g):
                            for wi, wname in enumerate(("w0q", "w0k", "w0v")):
                                cwl.inc(g.dma_start(
                                    out=wqkv[:, wi * 4096:(wi + 1) * 4096]
                                        .rearrange("p (c w) -> p c w", c=16),
                                    in_=inp[wname][:, hp * 256:(hp + 1) * 256]
                                        .rearrange("(c p) w -> p c w", p=128)), 16)
                            cwl.inc(g.dma_start(
                                out=wqkv[:, 3 * 4096:3 * 4096 + 32]
                                    .rearrange("p (c w) -> p c w", c=16),
                                in_=inp["w0b"][:, hp * 2:hp * 2 + 2]
                                    .rearrange("(c p) w -> p c w", p=128)), 16)
                            cwl.wait(g)

                        @blk.sync
                        def _(sync):
                            for tc in range(NTC):
                                if tc >= 1:
                                    cpj.wait(sync, b_pj + 7 * tc)
                                for kc in range(16):
                                    cxt[tc % 2].inc(sync.dma_start_transpose(
                                        xnT[:, kc * 512:(kc + 1) * 512],
                                        xn_full[tc * 512:(tc + 1) * 512,
                                                kc * 128:(kc + 1) * 128]), 16)

                        @blk.tensor
                        def _(te):
                            cwl.wait(te, b_wl + 64)
                            for tc in range(NTC):
                                cxt[tc % 2].wait(te, b_xt[tc % 2] + 256 * (tc // 2 + 1))
                                for ti in range(6):
                                    wi, h2 = ti % 3, ti // 3
                                    g = tc * 7 + ti
                                    if g >= 2:
                                        cpe.wait(te, b_pe + g - 1)
                                    for kc in range(16):
                                        inst = te.matmul(
                                            pps[g % 2][:, :],
                                            wqkv[:, wi * 4096 + kc * 256 + h2 * 128:
                                                 wi * 4096 + kc * 256 + h2 * 128 + 128],
                                            xnT[:, kc * 512:(kc + 1) * 512],
                                            start=(kc == 0), stop=(kc == 15))
                                    cpj.inc(inst)
                                if tc >= 1:
                                    cpe.wait(te, b_pe + 7 * tc - 1)
                                for n3 in range(32):
                                    h3, kc = divmod(n3, 16)
                                    inst = te.matmul(
                                        (ppb0, ppb1)[h3][0:1, :],
                                        wqkv[:, 3 * 4096 + kc * 2 + h3:
                                             3 * 4096 + kc * 2 + h3 + 1],
                                        xnT[:, kc * 512:(kc + 1) * 512],
                                        start=(kc == 0), stop=(kc == 15))
                                cpj.inc(inst)

                        @blk.scalar
                        def _(s):
                            for tc in range(NTC):
                                for ti in range(6):
                                    wi, h2 = ti % 3, ti // 3
                                    g = tc * 7 + ti
                                    cpj.wait(s, b_pj + g + 1)
                                    dst = (qb, kb, vb)[wi]
                                    cpe.inc(s.copy(
                                        dst[:, h2 * T + tc * 512:h2 * T + (tc + 1) * 512],
                                        pps[g % 2][:, :]))
                                cpj.wait(s, b_pj + tc * 7 + 7)
                                s.activation(
                                    betab[0:1, tc * 512:(tc + 1) * 512],
                                    ppb0[0:1, :], AF.Sigmoid)
                                cpe.inc(s.activation(
                                    betab1[0:1, tc * 512:(tc + 1) * 512],
                                    ppb1[0:1, :], AF.Sigmoid))
                    nc.all_engine_barrier()

                ccv = k.ctr("a0_cvv"); ccs = k.ctr("a0_cvs")
                b_cv, b_cs = ccv.v, ccs.v
                with nc.Block() as blk:
                    @blk.vector
                    def _(v):
                        for ti in range(6):
                            wi, h2 = ti % 3, ti // 3
                            src_t = (qb, kb, vb)[wi][:, h2 * T:(h2 + 1) * T]
                            tp = taps[:, wi * 4:wi * 4 + 4]
                            if ti >= 1:
                                ccs.wait(v, b_cs + ti)
                            v.drain()
                            v.tensor_scalar(cvb[:, :], src_t, tp[:, 3:4], None, OP.mult)
                            v.drain()
                            v.scalar_tensor_tensor(cvb[:, 1:T], src_t[:, 0:T - 1],
                                                   tp[:, 2:3], cvb[:, 1:T],
                                                   OP.mult, OP.add)
                            v.drain()
                            v.scalar_tensor_tensor(cvb[:, 2:T], src_t[:, 0:T - 2],
                                                   tp[:, 1:2], cvb[:, 2:T],
                                                   OP.mult, OP.add)
                            v.drain()
                            ccv.inc(v.scalar_tensor_tensor(cvb[:, 3:T], src_t[:, 0:T - 3],
                                                           tp[:, 0:1], cvb[:, 3:T],
                                                           OP.mult, OP.add))

                    @blk.scalar
                    def _(s):
                        for ti in range(6):
                            wi, h2 = ti % 3, ti // 3
                            dst_t = (qb, kb, vb)[wi][:, h2 * T:(h2 + 1) * T]
                            ccv.wait(s, b_cv + ti + 1)
                            ccs.inc(s.activation(dst_t, cvb[:, :], AF.Silu))
                nc.all_engine_barrier()

                cl2 = [k.ctr("a0_l2a"), k.ctr("a0_l2b"), k.ctr("a0_l2c"),
                       k.ctr("a0_l2d"), k.ctr("a0_l2e")]
                b2 = [c.v for c in cl2]
                with (
                    nc.psum_tensor(k.nm("a0_pn"), [1, 512], F32) as pn,
                    nc.psum_tensor(k.nm("a0_pb2"), [128, 512], F32) as pb2,
                ):
                    for g in range(32):
                        ti, tc = g // 8, g % 8
                        wi, h2 = ti % 2, ti // 2
                        tens = (qb, kb)[wi][:, h2 * T + tc * 512:h2 * T + (tc + 1) * 512]
                        if g >= 2:
                            cl2[1].wait(nc.scalar, b2[1] + g - 1)
                        cl2[0].inc(nc.scalar.activation(
                            sqb[:, (g % 2) * 512:(g % 2 + 1) * 512], tens, AF.Square))
                        cl2[0].wait(nc.tensor, b2[0] + g + 1)
                        if g >= 1:
                            cl2[2].wait(nc.tensor, b2[2] + g)
                        cl2[1].inc(nc.tensor.matmul(pn[0:1, :], k.ones32[:, 0:1],
                                                    sqb[:, (g % 2) * 512:(g % 2 + 1) * 512],
                                                    start=True, stop=True))
                        cl2[1].wait(nc.vector, b2[1] + g + 1)
                        if g >= 1:
                            cl2[3].wait(nc.vector, b2[3] + 2 * g - 1)
                        nc.vector.drain()
                        nc.vector.tensor_scalar(nrow[0:1, 0:512], pn[0:1, :], 1.0,
                                                L2_EPS, OP.mult, OP.add)
                        nc.vector.drain()
                        cl2[2].inc(nc.vector.reciprocal(nrow[0:1, 0:512],
                                                        nrow[0:1, 0:512]))
                        cl2[2].wait(nc.scalar, b2[2] + g + 1)
                        if g >= 1:
                            cl2[3].wait(nc.scalar, b2[3] + 2 * g)
                        cl2[3].inc(nc.scalar.activation(nrow[0:1, 512:1024],
                                                        nrow[0:1, 0:512], AF.Sqrt))
                        cl2[3].wait(nc.tensor, b2[3] + 2 * g + 1)
                        if g >= 1:
                            cl2[4].wait(nc.tensor, b2[4] + g)
                        cl2[3].inc(nc.tensor.matmul(pb2[:, :], k.ones_row[0:1, :],
                                                    nrow[0:1, 512:1024],
                                                    start=True, stop=True))
                        cl2[3].wait(nc.vector, b2[3] + 2 * (g + 1))
                        nc.vector.drain()
                        sc = DKS if wi == 0 else 1.0
                        cl2[4].inc(nc.vector.scalar_tensor_tensor(
                            tens, tens, sc, pb2[:, :], OP.mult, OP.mult))
                    nc.all_engine_barrier()

                emit_delta_scan(k, hp, qb, kb, vb, betab, betab1, o_all, onwc,
                                mk128, mku128, onesrb)
        emit_wo_rs_residual(k, o_all, k.inp["w0o"], x_src, x_dst, "a0")
    nc.all_engine_barrier()


def emit_delta_scan(k, hp, qb, kb, vb, betab, betab1, o_all, onwc, mk128,
                    mku128, onesrb):
    nc = k.nc
    C2 = 128
    NCH = T // C2
    SM = 1792
    ev = {}

    ct = k.ctr("sc_t"); cv = k.ctr("sc_v"); cs = k.ctr("sc_s")

    def prod(ctr, inst, name):
        ctr.inc(inst)
        ev[name] = (ctr, ctr.v)

    def need(eng, name):
        c2, vv = ev[name]
        eng.wait_ge(c2.h, vv)

    stril = mku128[:, 0:128]; striu = mku128[:, 128:256]
    triui = mk128[:, :]
    eye = k.ident[:, :]
    TE, V, S = nc.tensor, nc.vector, nc.scalar
    KBF, VBF, M_, N_, Y_, YT, G_, GT = 0, 128, 256, 384, 512, 640, 768, 896
    TTN, ATT, KBT, VBT, KTM, WN = 1024, 1152, 1280, 1408, 1536, 1664

    with (
        nc.sbuf_tensor(k.nm("sc_S"), [128, 2 * 128], F32) as Sf,
        nc.sbuf_tensor(k.nm("sc_Sb"), [128, 2 * 128], BF16) as Sb,
        nc.sbuf_tensor(k.nm("sc_sm"), [128, 2 * SM], BF16) as sm,
        nc.sbuf_tensor(k.nm("sc_or"), [128, 2 * 128], BF16) as oraw,
        nc.sbuf_tensor(k.nm("sc_sq"), [128, 2 * 128], F32) as osq,
        nc.sbuf_tensor(k.nm("sc_onr"), [1, 2 * 128], F32) as onr,
        nc.psum_tensor(k.nm("sc_pa0"), [128, 512], F32) as pA0,
        nc.psum_tensor(k.nm("sc_pb0"), [128, 512], F32) as pB0,
        nc.psum_tensor(k.nm("sc_pc0"), [128, 512], F32) as pC0,
        nc.psum_tensor(k.nm("sc_pa1"), [128, 512], F32) as pA1,
        nc.psum_tensor(k.nm("sc_pb1"), [128, 512], F32) as pB1,
        nc.psum_tensor(k.nm("sc_pc1"), [128, 512], F32) as pC1,
        nc.psum_tensor(k.nm("sc_pta"), [128, 128], BF16) as pta,
        nc.psum_tensor(k.nm("sc_ptb"), [128, 128], BF16) as ptb,
    ):
        pAs = [pA0, pA1]; pBs = [pB0, pB1]; pCs = [pC0, pC1]
        ptt = [pta, ptb]
        tcnt = [0]

        def R(h2, off):
            b = h2 * SM
            return sm[:, b + off:b + off + 128]

        V.memset(Sf[:, :], 0.0)
        V.drain()
        prod(cv, V.memset(Sb[:, :], 0.0), "init")
        for h2 in range(2):
            ev[f"Sbf{h2}"] = ev["init"]
            ev[f"evA{h2}"] = ev["init"]
            ev[f"evB{h2}"] = ev["init"]
            ev[f"evC{h2}"] = ev["init"]
            ev[f"oev{h2}"] = ev["init"]
            ev[f"onp_r{h2}"] = ev["init"]
        ev["evT0"] = ev["init"]
        ev["evT1"] = ev["init"]
        V.drain()

        for c in range(NCH):
            for h2 in range(2):
                h_abs = 2 * hp + h2
                q_c = qb[:, h2 * T + c * C2:h2 * T + (c + 1) * C2]
                k_c = kb[:, h2 * T + c * C2:h2 * T + (c + 1) * C2]
                v_c = vb[:, h2 * T + c * C2:h2 * T + (c + 1) * C2]
                A_, B_, Cb = pAs[h2], pBs[h2], pCs[h2]
                bsrc = betab if h2 == 0 else betab1
                brow = bsrc[0:1, c * C2:(c + 1) * C2]

                need(TE, f"evA{h2}")
                prod(ct, TE.matmul(A_[:, 384:512], onesrb[0:1, :], brow,
                                   start=True, stop=True), f"bB{h2}")
                need(V, f"bB{h2}")
                V.drain()
                prod(cv, V.tensor_tensor(R(h2, KBF), k_c, A_[:, 384:512],
                                         OP.mult), f"kbf{h2}")
                V.drain()
                prod(cv, V.tensor_tensor(R(h2, VBF), v_c, A_[:, 384:512],
                                         OP.mult), f"vbf{h2}")
                need(TE, f"kbf{h2}")
                need(TE, f"evB{h2}")
                TE.matmul(B_[:, 0:128], R(h2, KBF), k_c, start=True, stop=True)
                prod(ct, TE.matmul(B_[:, 128:256], k_c, R(h2, KBF),
                                   start=True, stop=True), f"AAt{h2}")
                need(TE, f"vbf{h2}")
                for nm3, srcb, off in (("kbt", R(h2, KBF), KBT),
                                       ("vbt", R(h2, VBF), VBT),
                                       ("ktm", k_c, KTM)):
                    slot = tcnt[0] % 2
                    need(TE, f"evT{slot}")
                    prod(ct, TE.transpose(ptt[slot][:, :], srcb, k.ident[:, :]),
                         f"tr_{nm3}{h2}")
                    need(V, f"tr_{nm3}{h2}")
                    V.drain()
                    prod(cv, V.tensor_copy(R(h2, off), ptt[slot][:, :]),
                         f"{nm3}s{h2}")
                    ev[f"evT{slot}"] = ev[f"{nm3}s{h2}"]
                    tcnt[0] += 1
                need(V, f"AAt{h2}")
                V.drain()
                V.select(R(h2, M_), stril, B_[:, 0:128], eye, add_drain=True)
                V.drain()
                prod(cv, V.tensor_copy(R(h2, M_)[:, 0:1], R(h2, M_)[:, 0:1]),
                     f"M{h2}")
                V.select(R(h2, N_), striu, B_[:, 128:256], eye, add_drain=True)
                V.drain()
                prod(cv, V.tensor_copy(R(h2, N_)[:, 0:1], R(h2, N_)[:, 0:1]),
                     f"N{h2}")
                V.tensor_scalar(R(h2, G_), B_[:, 128:256], -1.0, None, OP.mult)
                V.drain()
                V.select(R(h2, Y_), striu, R(h2, G_), eye, add_drain=True)
                V.drain()
                prod(cv, V.tensor_copy(R(h2, Y_)[:, 0:1], R(h2, Y_)[:, 0:1]),
                     f"Y{h2}")
                V.tensor_scalar(R(h2, G_), B_[:, 0:128], -1.0, None, OP.mult)
                V.drain()
                V.select(R(h2, YT), stril, R(h2, G_), eye, add_drain=True)
                V.drain()
                prod(cv, V.tensor_copy(R(h2, YT)[:, 0:1], R(h2, YT)[:, 0:1]),
                     f"Yt{h2}")

                for it in range(6):
                    last = (it == 5)
                    need(TE, f"Y{h2}")
                    need(TE, f"M{h2}")
                    prod(ct, TE.matmul(A_[:, 0:128], R(h2, M_), R(h2, Y_),
                                       start=True, stop=True), f"Z{h2}")
                    if not last:
                        need(TE, f"Yt{h2}")
                        need(TE, f"N{h2}")
                        prod(ct, TE.matmul(B_[:, 0:128], R(h2, N_), R(h2, YT),
                                           start=True, stop=True), f"Zt{h2}")
                    need(V, f"Z{h2}")
                    V.drain()
                    prod(cv, V.scalar_tensor_tensor(R(h2, G_), eye, 2.0,
                                                    A_[:, 0:128],
                                                    OP.mult, OP.subtract),
                         f"G{h2}")
                    if not last:
                        need(V, f"Zt{h2}")
                        V.drain()
                        prod(cv, V.scalar_tensor_tensor(R(h2, GT), eye, 2.0,
                                                        B_[:, 0:128],
                                                        OP.mult, OP.subtract),
                             f"Gt{h2}")
                    need(TE, f"G{h2}")
                    prod(ct, TE.matmul(A_[:, 128:256], R(h2, YT), R(h2, G_),
                                       start=True, stop=True), f"Yp{h2}")
                    if not last:
                        need(TE, f"Gt{h2}")
                        prod(ct, TE.matmul(B_[:, 128:256], R(h2, G_), R(h2, YT),
                                           start=True, stop=True), f"Ytp{h2}")
                    need(V, f"Yp{h2}")
                    V.drain()
                    prod(cv, V.tensor_copy(R(h2, Y_), A_[:, 128:256]), f"Y{h2}")
                    if not last:
                        need(V, f"Ytp{h2}")
                        V.drain()
                        prod(cv, V.tensor_copy(R(h2, YT), B_[:, 128:256]),
                             f"Yt{h2}")
                need(V, f"Y{h2}")
                V.drain()
                prod(cv, V.tensor_scalar(R(h2, TTN), R(h2, Y_), -1.0,
                                         None, OP.mult), f"Ttn{h2}")
                need(TE, f"Y{h2}")
                need(TE, f"vbts{h2}")
                TE.matmul(A_[:, 256:384], R(h2, Y_), R(h2, VBT),
                          start=True, stop=False)
                need(TE, f"Ttn{h2}")
                need(TE, f"kbts{h2}")
                prod(ct, TE.matmul(B_[:, 256:384], R(h2, KBT), R(h2, TTN),
                                   start=True, stop=True), f"wn{h2}")
                need(V, f"wn{h2}")
                V.drain()
                prod(cv, V.tensor_copy(R(h2, WN), B_[:, 256:384]), f"wns{h2}")
                need(TE, f"wns{h2}")
                need(TE, f"Sbf{h2}")
                prod(ct, TE.matmul(A_[:, 256:384], R(h2, WN),
                                   Sb[:, h2 * 128:(h2 + 1) * 128],
                                   start=False, stop=True), f"u{h2}")
                need(V, f"u{h2}")
                V.drain()
                prod(cv, V.tensor_copy(R(h2, VBT), A_[:, 256:384]), f"ups{h2}")
                prod(ct, TE.matmul(B_[:, 384:512], k_c, q_c,
                                   start=True, stop=True), f"at{h2}")
                need(V, f"at{h2}")
                V.drain()
                prod(cv, V.tensor_tensor(R(h2, ATT), B_[:, 384:512], triui,
                                         OP.mult), f"ats{h2}")
                ev[f"evB{h2}"] = ev[f"ats{h2}"]
                need(TE, f"oev{h2}")
                need(TE, f"evC{h2}")
                need(TE, f"ups{h2}")
                need(TE, f"ats{h2}")
                TE.matmul(Cb[:, 0:128], Sb[:, h2 * 128:(h2 + 1) * 128], q_c,
                          start=True, stop=False)
                prod(ct, TE.matmul(Cb[:, 0:128], R(h2, VBT), R(h2, ATT),
                                   start=False, stop=True), f"o{h2}")
                need(V, f"o{h2}")
                V.drain()
                prod(cv, V.tensor_copy(oraw[:, h2 * 128:(h2 + 1) * 128],
                                       Cb[:, 0:128]), f"orawv{h2}")
                need(S, f"orawv{h2}")
                prod(cs, S.activation(osq[:, h2 * 128:(h2 + 1) * 128],
                                      oraw[:, h2 * 128:(h2 + 1) * 128],
                                      AF.Square), f"osq{h2}")
                need(TE, f"osq{h2}")
                need(TE, f"onp_r{h2}")
                prod(ct, TE.matmul(A_[0:1, 384:512], k.ones32[:, 0:1],
                                   osq[:, h2 * 128:(h2 + 1) * 128],
                                   start=True, stop=True), f"on{h2}")
                need(V, f"on{h2}")
                V.drain()
                prod(cv, V.tensor_scalar(onr[0:1, h2 * 128:(h2 + 1) * 128],
                                         A_[0:1, 384:512], 1.0 / DK, RMS0_EPS,
                                         OP.mult, OP.add), f"onp{h2}")
                ev[f"onp_r{h2}"] = ev[f"onp{h2}"]
                need(S, f"onp{h2}")
                prod(cs, S.activation(onr[0:1, h2 * 128:(h2 + 1) * 128],
                                      onr[0:1, h2 * 128:(h2 + 1) * 128],
                                      AF.Sqrt), f"onsq{h2}")
                need(V, f"onsq{h2}")
                V.drain()
                prod(cv, V.reciprocal(onr[0:1, h2 * 128:(h2 + 1) * 128],
                                      onr[0:1, h2 * 128:(h2 + 1) * 128]),
                     f"orr{h2}")
                need(TE, f"orr{h2}")
                prod(ct, TE.matmul(A_[:, 384:512], k.ones_row[0:1, :],
                                   onr[0:1, h2 * 128:(h2 + 1) * 128],
                                   start=True, stop=True), f"ob{h2}")
                need(V, f"ob{h2}")
                V.drain()
                prod(cv, V.scalar_tensor_tensor(
                    o_all[:, h_abs * T + c * C2:h_abs * T + (c + 1) * C2],
                    oraw[:, h2 * 128:(h2 + 1) * 128], onwc[:, 0:1],
                    A_[:, 384:512], OP.mult, OP.mult), f"oev{h2}")
                ev[f"evA{h2}"] = ev[f"oev{h2}"]
                need(TE, f"ktms{h2}")
                need(TE, f"orawv{h2}")
                prod(ct, TE.matmul(Cb[:, 128:256], R(h2, KTM), R(h2, VBT),
                                   start=True, stop=True), f"dS{h2}")
                need(V, f"dS{h2}")
                V.drain()
                V.tensor_tensor(Sf[:, h2 * 128:(h2 + 1) * 128],
                                Sf[:, h2 * 128:(h2 + 1) * 128],
                                Cb[:, 128:256], OP.add)
                V.drain()
                prod(cv, V.tensor_copy(Sb[:, h2 * 128:(h2 + 1) * 128],
                                       Sf[:, h2 * 128:(h2 + 1) * 128]),
                     f"Sbf{h2}")
                ev[f"evC{h2}"] = ev[f"Sbf{h2}"]
    nc.all_engine_barrier()


def build_nc(nsub=N_SUB):
    nc = bass.Bass("TRN2", target_bir_lowering=False)
    inp = _decl_inputs(nc)
    y = nc.dram_tensor("y", [TOK, D], F32, kind="ExternalOutput")
    k = K(nc, inp, nsub)
    k.x_hbm = nc.dram_tensor("x_hbm", [TOK, D], F32, kind="Internal")
    k.y = y
    emit_const_load(k)
    if DEV_DUMP_F0:
        k.dbg_xnT = nc.dram_tensor("dbg_xnT", [128, 4 * D], BF16, kind="ExternalOutput")
        k.dbg_xnb = nc.dram_tensor("dbg_xnb", [128, 4 * D], BF16, kind="ExternalOutput")
        k.dbg_h = nc.dram_tensor("dbg_h", [128, 4 * INTER], BF16, kind="ExternalOutput")
    if nsub >= 1 or nsub == -1:
        emit_ghost_decode(k)
    if 4 <= nsub <= 6:
        emit_bitnet_prep(k)
    if nsub == -1:
        nsub = 0
        if DEV_DUMP_DECODE:
            dbg_g = nc.dram_tensor("dbg_g", [D, INTER], BF16, kind="ExternalOutput")
            dbg_d = nc.dram_tensor("dbg_d", [INTER, D], BF16, kind="ExternalOutput")
            cdbg = k.ctr("c_dbg")
            with nc.Block() as blk:
                @blk.sync
                def _(sync):
                    cdbg.inc(sync.dma_start(out=dbg_g[:, :], in_=k.wg_hbm[:, :]), 16)
                    cdbg.inc(sync.dma_start(out=dbg_d[:, :], in_=k.wd_hbm[:, :]), 16)
                    cdbg.wait(sync)
            nc.all_engine_barrier()

    if DEV_DUMP_F1:
        k.dbg_xqT = nc.dram_tensor("dbg_xqT", [128, 4 * D], BF16, kind="ExternalOutput")
        k.dbg_h = nc.dram_tensor("dbg_h", [128, 2 * H1], BF16, kind="ExternalOutput")
        k.dbg_hq = nc.dram_tensor("dbg_hq", [128, 2 * H1], BF16, kind="ExternalOutput")
    if nsub == 9905:
        emitters = [emit_attn0]
        nsub = 1
    elif nsub == 9904:
        emitters = [emit_attn1]
        nsub = 1
    elif nsub == 9903:
        emit_bitnet_prep(k)
        emitters = [emit_ffn1]
        nsub = 1
    elif nsub == 9902:
        k.dbg_qs = nc.dram_tensor("dbg_qs", [128, 16], F32, kind="ExternalOutput")
        emit_bitnet_prep(k)
        cdbg = k.ctr("c_dbgq")
        with nc.Block() as blk:
            @blk.sync
            def _(sync):
                cdbg.inc(sync.dma_start(out=k.dbg_qs[:, :], in_=k.qs_dbg[:, :]), 16)
                cdbg.wait(sync)
        nc.all_engine_barrier()
        emitters = []
        nsub = 0
    elif nsub == 9901:
        emit_bitnet_prep(k)
        emitters = [emit_ffn0, emit_ffn1]
        nsub = 2
    else:
        emitters = [
            emit_ffn0,
            emit_attn0,
            emit_ffn0,
            emit_ffn1,
            emit_attn1,
            emit_ffn1,
        ]
    nrun = min(max(nsub, 0), len(emitters))
    for i in range(nrun):
        src = inp["x"] if i == 0 else k.x_hbm
        dst = y if i == nrun - 1 else k.x_hbm
        emitters[i](k, src, dst)
    if nrun == 0:
        c = k.ctr("c_copyout")
        with nc.Block() as blk:
            @blk.sync
            def _(sync):
                c.inc(sync.dma_start(out=y[:, :], in_=inp["x"][:, :]), 16)
                c.wait(sync)
        nc.all_engine_barrier()
    nc.clear_and_free_semaphores([c.h for c in k.pool])
    nc.all_engine_barrier()
    return nc


def kernel(**inputs):
    in_maps = _host_layout(inputs)
    nc = build_nc(N_SUB)
    res = run_bass_kernel_spmd(nc, in_maps, core_ids=list(range(8)))
    out = np.zeros((B, T, D), np.float32)
    for c in range(8):
        b, q = c // 4, c % 4
        out[b, 1024 * q:1024 * (q + 1)] = res.results[c]["y"]
    return out
